# version 14
# baseline (speedup 1.0000x reference)
import contextlib

import numpy as np
import ml_dtypes

import concourse.bass as bass
import concourse.mybir as mybir
from concourse.bass_utils import run_bass_kernel_spmd

F32 = mybir.dt.float32
BF16 = mybir.dt.bfloat16
I32 = mybir.dt.int32
AF = mybir.ActivationFunctionType
ALU = mybir.AluOpType
AX = mybir.AxisListType


def full_cfg():
    return dict(
        S=4096, D=4096, HL=16, DN=128, DV=128, E=8, R=16, O=4096,
        W0=611, W1=4095,
        SCALE=192.0 ** -0.5, SHIFT=12.0, EPS=1e-5, LORA_SCALING=2.0,
        P1GW=128,
        P2GW=256,
        OQN=8,
        DEBUG_TAPS=(),
    )


def small_cfg():
    return dict(
        S=512, D=512, HL=2, DN=128, DV=128, E=8, R=16, O=512,
        W0=77, W1=511,
        SCALE=192.0 ** -0.5, SHIFT=12.0, EPS=1e-5, LORA_SCALING=2.0,
        P1GW=128, P2GW=256, OQN=2,
        DEBUG_TAPS=(),
    )


class C:

    def __init__(self, cfg):
        self.__dict__.update(cfg)
        self.cfg = cfg
        self.NDC = self.D // 128
        self.NTC = self.S // 128
        self.QS = self.S // 4
        self.HDV = 2 * self.HL * self.DV
        self.ER = self.E * self.R
        assert self.ER == 128
        self.WIN_N = self.W1 - self.W0
        self.MYTOK = 2 * self.QS
        self.NMT = self.MYTOK // 128
        self.NFT = self.QS // 128
        self.OQ = self.O // self.OQN
        self.NP1 = self.S // self.P1GW
        self.NP2 = self.S // self.P2GW
        self.HLDN = self.HL * self.DN
        self.NCC = self.HLDN // 128


def _finish(block, _):
    return


def build_program(cfg):
    c = C(cfg)
    S, D, HL, DN, DV = c.S, c.D, c.HL, c.DN, c.DV
    E, R, O, ER = c.E, c.R, c.O, c.ER
    NDC, NTC, QS = c.NDC, c.NTC, c.QS
    taps = set(cfg.get("DEBUG_TAPS", ()))
    RL = bool(cfg.get("RACE_LOCAL", False))

    nc = bass.Bass("TRN2", target_bir_lowering=False,
                   detect_race_conditions=cfg.get("RACE_DETECT", False))

    xT = nc.dram_tensor("xT", [D, S], F32, kind="ExternalInput").ap()
    x_strips = nc.dram_tensor("x_strips", [c.MYTOK, D], F32, kind="ExternalInput").ap()
    wq_l = nc.dram_tensor("wq_l", [D, c.HLDN], F32, kind="ExternalInput").ap()
    wk_l = nc.dram_tensor("wk_l", [D, c.HLDN], F32, kind="ExternalInput").ap()
    wv_l = nc.dram_tensor("wv_l", [D, HL * DV], F32, kind="ExternalInput").ap()
    wo_full = nc.dram_tensor("wo_full", [c.HDV, D], F32, kind="ExternalInput").ap()
    WTt = nc.dram_tensor("WTt", [D, O], F32, kind="ExternalInput").ap()
    bTt = nc.dram_tensor("bTt", [1, O], F32, kind="ExternalInput").ap()
    ATt = nc.dram_tensor("ATt", [D, ER], F32, kind="ExternalInput").ap()
    BTt = nc.dram_tensor("BTt", [ER, O], F32, kind="ExternalInput").ap()
    rwHt = nc.dram_tensor("rwHt", [128, E * NDC], F32, kind="ExternalInput").ap()
    rbTt = nc.dram_tensor("rbTt", [1, E], F32, kind="ExternalInput").ap()
    wmaskt = nc.dram_tensor("wmaskt", [128, c.MYTOK], F32, kind="ExternalInput").ap()
    prm = nc.dram_tensor("prm", [1, 12], I32, kind="ExternalInput").ap()

    out_t = nc.dram_tensor("out", [QS, O], F32, kind="ExternalOutput").ap()

    lnT_h = nc.dram_tensor("lnT_h", [D, S], BF16, kind="Internal").ap()
    qT_h = nc.dram_tensor("qT_h", [c.HLDN, S], BF16, kind="Internal").ap()
    kT_h = nc.dram_tensor("kT_h", [c.HLDN, S], BF16, kind="Internal").ap()
    v_h = nc.dram_tensor("v_h", [S, HL * DV], BF16, kind="Internal").ap()
    xaft_h = nc.dram_tensor("xaft_h", [c.MYTOK, D], F32, kind="Internal").ap()
    attnT_loc = nc.dram_tensor("attnT_loc", [c.HL * 128, S], BF16, kind="Internal").ap()
    attnT_l2 = nc.dram_tensor("attnT_l2", [c.HDV, c.MYTOK], BF16, kind="Internal").ap()
    ln2T_h = nc.dram_tensor("ln2T_h", [D, c.MYTOK], BF16, kind="Internal").ap()

    sh_attnT = nc.dram_tensor("sh_attnT", [c.HDV, S], BF16, kind="Internal",
                              addr_space="Shared").ap()
    sh_qpart = nc.dram_tensor("sh_qpart", [2 * 128, NDC], F32, kind="Internal",
                              addr_space="Shared").ap()
    sh_flags = nc.dram_tensor("sh_flags", [2, 16], I32, kind="Internal",
                              addr_space="Shared").ap()

    ident_c = nc.inline_tensor(np.eye(128, dtype=ml_dtypes.bfloat16), "ident_c").ap()
    ones_colb = nc.inline_tensor(np.ones((128, 1), dtype=ml_dtypes.bfloat16), "ones_colb").ap()
    ones_rowb = nc.inline_tensor(np.ones((1, 128), dtype=ml_dtypes.bfloat16), "ones_rowb").ap()
    ones_colf = nc.inline_tensor(np.ones((128, 1), dtype=np.float32), "ones_colf").ap()
    ones_rowf = nc.inline_tensor(np.ones((1, 128), dtype=np.float32), "ones_rowf").ap()
    cm = np.zeros((128, 4 * 512), dtype=ml_dtypes.bfloat16)
    for j in range(4):
        cm[:, j * 512:(j + 1) * 512] = (
            np.arange(128)[:, None] + j * 128 <= np.arange(512)[None, :])
    cmask_c = nc.inline_tensor(cm, "cmask_c").ap()
    rep = np.zeros((E, ER), dtype=ml_dtypes.bfloat16)
    for e in range(E):
        rep[e, e * R:(e + 1) * R] = np.float32(cfg["LORA_SCALING"])
    rep_c = nc.inline_tensor(rep, "rep_c").ap()

    CVALS = [0.0, 1.0, -1.0, float(cfg["SCALE"]), -float(cfg["SHIFT"]),
             float(cfg["EPS"]), -0.5, 1.0 / D, 1.0 / c.WIN_N, -1.0]
    cvals_c = nc.inline_tensor(
        np.tile(np.array([CVALS], dtype=np.float32), (128, 1)), "cvals_c").ap()

    st_ = contextlib.ExitStack()

    def sb(name, shape, dtype):
        return st_.enter_context(nc.sbuf_tensor(name, shape, dtype))

    def mkpsum(name, shape, dtype=F32):
        return st_.enter_context(nc.psum_tensor(name, shape, dtype))

    def mksem(name):
        return st_.enter_context(nc.semaphore(name))

    with st_:
        P1GW, P2GW = c.P1GW, c.P2GW
        big = sb("big", [128, 32768], BF16)
        ar2 = sb("ar2", [128, 16384], BF16)
        arf = sb("arf", [128, NDC * P1GW], F32)
        attsb = sb("attsb", [128, 2 * (c.HDV // 128) * 128], BF16)
        wkb = sb("wkb", [128, S], BF16)
        wkb2 = sb("wkb2", [128, S], BF16)
        vbuf = sb("vbuf", [128, S], BF16)
        pt2 = sb("pt2", [128, 2048], BF16)
        attc = sb("attc", [128, 1024], BF16)
        evb = sb("evb", [128, 2048], BF16)
        EVW = max(c.O // 4, c.OQ)
        evf = sb("evf", [128, 2 * EVW], F32)
        xres = sb("xres", [128, 2 * EVW], F32)
        wk1 = sb("wk1", [128, 512], F32)
        bcv = sb("bcv", [128, 1024], F32)
        st_m = sb("st_m", [1, 512], F32)
        st_sq = sb("st_sq", [1, 512], F32)
        st_var = sb("st_var", [1, 512], F32)
        st_rstd = sb("st_rstd", [1, 512], F32)
        cstv = sb("cstv", [128, len(CVALS)], F32)
        idn = sb("idn", [128, 128], BF16)
        onebc = sb("onebc", [128, 1], BF16)
        onebr = sb("onebr", [1, 128], BF16)
        onefr = sb("onefr", [1, 128], F32)
        onefc = sb("onefc", [128, 1], F32)
        cmv = sb("cmv", [128, 4 * 512], BF16)
        repv = sb("repv", [E, ER], BF16)
        rbv = sb("rbv", [1, E], F32)
        rwv = sb("rwv", [128, E * NDC], F32)
        prmv = sb("prmv", [1, 12], I32)
        flgrd = sb("flgrd", [1, 4], I32)
        wmv = wkb2
        atv = vbuf
        tts = sb("tts", [128, c.NFT * 128], BF16)
        pfb = sb("pfb", [128, 512], BF16)
        facc = sb("facc", [128, 512], F32)
        faccB = sb("faccB", [128, 1024], BF16)
        ln2s = sb("ln2s", [128, 16], F32)
        l2p = sb("l2p", [128, 256], F32)
        qpart = sb("qpart", [128, NDC], F32)
        qpeer = sb("qpeer", [128, NDC], F32)
        r8 = sb("r8", [1, 64], F32)
        r8b = sb("r8b", [8, 8], BF16)
        rexp = sb("rexp", [128, 1], F32)

        for i, v in enumerate(CVALS):
            nc.const_aps.register(F32, float(v), cstv[:, i:i + 1])

        ps_sc0 = mkpsum("ps_sc0", [128, 1024])
        ps_sc1 = mkpsum("ps_sc1", [128, 1024])
        ps4t = mkpsum("ps4t", [128, 512])
        ps = [ps_sc0.ap()[:, 0:512], ps_sc0.ap()[:, 512:1024],
              ps_sc1.ap()[:, 0:512], ps_sc1.ap()[:, 512:1024],
              ps4t.ap()[:, 0:512]]
        ps_sumA = mkpsum("ps_sumA", [1, 512])
        ps_tr = mkpsum("ps_tr", [128, 1024], BF16)
        ps_bc = ps_tr.ap().bitcast(F32)
        ps_tr2 = mkpsum("ps_tr2", [128, 1024], BF16)
        ps_av1 = ps_tr2.ap().bitcast(F32)

        s_hw = mksem("s_hw")
        s_hwB = mksem("s_hwB")
        s_st = mksem("s_st")
        s_st2 = mksem("s_st2")
        s_st2B = mksem("s_st2B")
        s_sw = mksem("s_sw")
        s_ac2 = mksem("s_ac2")
        s_ac2B = mksem("s_ac2B")
        s_pe = mksem("s_pe")
        s_ve = mksem("s_ve")
        s_ac = mksem("s_ac")
        s_gp = mksem("s_gp")
        s_wv0 = mksem("s_wv0")
        s_wv1 = mksem("s_wv1")
        s_g1 = mksem("s_g1")
        s_g2 = mksem("s_g2")
        s_g3 = mksem("s_g3")
        s_po1 = mksem("s_po1")
        s_po2 = mksem("s_po2")

        cnt = {}
        for s in (s_hw, s_hwB, s_st, s_st2, s_st2B, s_sw, s_ac2, s_ac2B,
                  s_pe, s_ve, s_ac, s_gp, s_wv0, s_wv1, s_g1, s_g2, s_g3):
            cnt[id(s)] = 0

        def inc(instr, s, n=1):
            cnt[id(s)] += n
            return instr.then_inc(s, n)

        M = {}

        def mark(name, s):
            M[name] = (s, cnt[id(s)])

        def wait(eng, name, extra=0):
            s, v = M[name]
            if v + extra > 0:
                eng.wait_ge(s, v + extra)

        block = st_.enter_context(nc.Block())
        gpregs = {}

        def p0_sync(sync):
            for dst, src in ((idn, ident_c), (onebc, ones_colb), (onebr, ones_rowb),
                             (onefr, ones_rowf), (onefc, ones_colf),
                             (cmv, cmask_c), (repv, rep_c),
                             (rbv, rbTt), (rwv, rwHt), (cstv, cvals_c),
                             (prmv, prm)):
                inc(sync.dma_start(out=dst[:, :], in_=src[:, :]), s_hw, 16)
        block.sync(p0_sync)
        mark("p0_const", s_hw)


        GW = c.P1GW
        NP1 = c.NP1
        assert 4 * GW <= 512
        arf2 = attsb.bitcast(F32)
        BGR = 2 * NDC * GW

        def p1_arf(par):
            return arf if par == 0 else arf2

        ve_ord = [("cast", 0), ("cast", 1)]
        ac_ord = [("sq", 0), ("sq", 1)]
        pe_ord = [("stat", 0)]
        hw_ord = [("ld", 0), ("ld", 1)]
        st_ord = []
        for g in range(NP1):
            ve_ord += [("vstat", g), ("vstat2", g), ("apply", g)]
            if g + 2 < NP1:
                ve_ord.append(("cast", g + 2))
            ac_ord += [("astat", g), ("bce", g)]
            if g + 2 < NP1:
                ac_ord.append(("sq", g + 2))
            if g + 1 < NP1:
                pe_ord.append(("stat", g + 1))
            pe_ord.append(("bc", g))
            st_ord.append(("st", g))
            if g + 2 < NP1:
                hw_ord.append(("ld", g + 2))

        p1b = dict(hw=cnt[id(s_hw)], st=cnt[id(s_st)], ve=cnt[id(s_ve)],
                   ac=cnt[id(s_ac)], pe=cnt[id(s_pe)])
        p1b["hwB"] = cnt[id(s_hwB)]
        VE_I = {k: p1b["ve"] + i + 1 for i, k in enumerate(ve_ord)}
        AC_I = {k: p1b["ac"] + i + 1 for i, k in enumerate(ac_ord)}
        PE_I = {k: p1b["pe"] + i + 1 for i, k in enumerate(pe_ord)}
        HW_I = {}
        _evn = _odd = 0
        for kind, g in hw_ord:
            if g % 2 == 0:
                _evn += 16
                HW_I[(kind, g)] = (s_hw, p1b["hw"] + _evn)
            else:
                _odd += 16
                HW_I[(kind, g)] = (s_hwB, p1b["hwB"] + _odd)
        ST_I = {k: p1b["st"] + (i + 1) * 16 for i, k in enumerate(st_ord)}

        def p1_sync(sync):
            wait(sync, "p0_const")
            for kind, g in hw_ord + []:
                pass
            def ld(g):
                par = g % 2
                if g >= 2:
                    sync.wait_ge(s_ve, VE_I[("apply", g - 2)])
                inc(sync.dma_start(
                    out=p1_arf(par)[:, 0:NDC * GW]
                        .rearrange("p (dc t) -> p dc t", t=GW),
                    in_=xT[0:D, g * GW:(g + 1) * GW]
                        .rearrange("(dc p) t -> p dc t", p=128)),
                    s_hw if par == 0 else s_hwB, 16)

            def stg(g):
                par = g % 2
                sync.wait_ge(s_ve, VE_I[("apply", g)])
                inc(sync.dma_start(
                    out=lnT_h[0:D, g * GW:(g + 1) * GW]
                        .rearrange("(dc p) t -> p dc t", p=128),
                    in_=ar2[:, par * NDC * GW:(par + 1) * NDC * GW]
                        .rearrange("p (dc t) -> p dc t", t=GW)), s_st, 16)

            ld(0)
            ld(1)
            for g in range(NP1):
                if g + 2 < NP1:
                    ld(g + 2)
                stg(g)
        block.sync(p1_sync)

        def p1_ve(v):
            def cast(g):
                par = g % 2
                v.wait_ge(*HW_I[("ld", g)])
                if g >= 2:
                    v.wait_ge(s_pe, PE_I[("stat", g - 2)])
                v.tensor_copy(big[:, par * BGR:par * BGR + NDC * GW],
                              p1_arf(par)[:, 0:NDC * GW])
                inc(v.drain(), s_ve)

            def vstat(g):
                par = g % 2
                v.wait_ge(s_pe, PE_I[("stat", g)])
                pss = ps_sumA[0:1, 0:512] if par == 0 else ps[4][0:1, 0:512]
                o = par * GW
                v.tensor_scalar_mul(st_m[0:1, o:o + GW], pss[0:1, 0:GW], 1.0 / D)
                v.tensor_scalar_mul(st_sq[0:1, o:o + GW], pss[0:1, GW:2 * GW], 1.0 / D)
                v.drain()
                v.tensor_tensor(st_var[0:1, o:o + GW], st_m[0:1, o:o + GW],
                                st_m[0:1, o:o + GW], ALU.mult)
                v.drain()
                v.tensor_tensor(st_var[0:1, o:o + GW], st_sq[0:1, o:o + GW],
                                st_var[0:1, o:o + GW], ALU.subtract)
                inc(v.drain(), s_ve)

            def vstat2(g):
                par = g % 2
                o = par * GW
                v.wait_ge(s_ac, AC_I[("astat", g)])
                v.tensor_tensor(st_m[0:1, o:o + GW], st_m[0:1, o:o + GW],
                                st_rstd[0:1, o:o + GW], ALU.mult)
                v.drain()
                v.tensor_scalar_mul(st_m[0:1, o:o + GW], st_m[0:1, o:o + GW], -1.0)
                inc(v.drain(), s_ve)

            def apply(g):
                par = g % 2
                o = par * 2 * GW
                v.wait_ge(s_ac, AC_I[("bce", g)])
                if g >= 1:
                    v.wait_ge(s_st, ST_I[("st", g - 1)])
                a3 = p1_arf(par)[:, 0:NDC * GW].rearrange("p (o t) -> p o t", t=GW)
                o3 = ar2[:, par * NDC * GW:(par + 1) * NDC * GW] \
                    .rearrange("p (o t) -> p o t", t=GW)
                rb = wk1[:, o:o + GW].rearrange("p (q t) -> p q t", q=1) \
                    .broadcast_to([128, NDC, GW])
                mb = wk1[:, o + GW:o + 2 * GW].rearrange("p (q t) -> p q t", q=1) \
                    .broadcast_to([128, NDC, GW])
                v.tensor_tensor(a3, a3, rb, ALU.mult)
                v.drain()
                v.tensor_tensor(o3, a3, mb, ALU.add)
                inc(v.drain(), s_ve)

            fns = dict(cast=cast, vstat=vstat, vstat2=vstat2, apply=apply)
            for kind, g in ve_ord:
                fns[kind](g)
        block.vector(p1_ve)

        def p1_ac(sc):
            def sq(g):
                par = g % 2
                sc.wait_ge(s_ve, VE_I[("cast", g)])
                if g >= 2:
                    sc.wait_ge(s_pe, PE_I[("stat", g - 2)])
                sc.activation(big[:, par * BGR + NDC * GW:par * BGR + 2 * NDC * GW],
                              big[:, par * BGR:par * BGR + NDC * GW],
                              AF.Square, bias=0.0, scale=1.0)
                inc(sc.drain(), s_ac)

            def astat(g):
                par = g % 2
                o = par * GW
                sc.wait_ge(s_ve, VE_I[("vstat", g)])
                sc.activation(st_rstd[0:1, o:o + GW], st_var[0:1, o:o + GW],
                              AF.Ln, bias=float(cfg["EPS"]), scale=1.0)
                sc.drain()
                sc.activation(st_rstd[0:1, o:o + GW], st_rstd[0:1, o:o + GW],
                              AF.Exp, bias=0.0, scale=-0.5)
                inc(sc.drain(), s_ac)

            def bce(g):
                par = g % 2
                o = par * 2 * GW
                sc.wait_ge(s_pe, PE_I[("bc", g)])
                if g >= 2:
                    sc.wait_ge(s_ve, VE_I[("apply", g - 2)])
                pa, pb = (ps[0], ps[1]) if par == 0 else (ps[2], ps[3])
                sc.copy(wk1[:, o:o + GW], pa[:, 0:GW])
                sc.copy(wk1[:, o + GW:o + 2 * GW], pb[:, 0:GW])
                inc(sc.drain(), s_ac)

            fns = dict(sq=sq, astat=astat, bce=bce)
            for kind, g in ac_ord:
                fns[kind](g)
        block.scalar(p1_ac)

        def p1_pe(t):
            def stat(g):
                par = g % 2
                t.wait_ge(s_ac, AC_I[("sq", g)])
                if g >= 2:
                    t.wait_ge(s_ve, VE_I[("vstat", g - 2)])
                pss = ps_sumA[0:1, 0:512] if par == 0 else ps[4][0:1, 0:512]
                for dc in range(NDC):
                    t.matmul(pss[0:1, 0:GW], onebc[:, :],
                             big[:, par * BGR + dc * GW:par * BGR + (dc + 1) * GW],
                             start=(dc == 0), stop=(dc == NDC - 1))
                for dc in range(NDC):
                    mm = t.matmul(pss[0:1, GW:2 * GW], onebc[:, :],
                                  big[:, par * BGR + (NDC + dc) * GW:
                                      par * BGR + (NDC + dc + 1) * GW],
                                  start=(dc == 0), stop=(dc == NDC - 1))
                inc(mm, s_pe)

            def bc(g):
                par = g % 2
                o = par * GW
                t.wait_ge(s_ve, VE_I[("vstat2", g)])
                if g >= 2:
                    t.wait_ge(s_ac, AC_I[("bce", g - 2)])
                pa, pb = (ps[0], ps[1]) if par == 0 else (ps[2], ps[3])
                t.matmul(pa[:, 0:GW], onefr[:, :], st_rstd[0:1, o:o + GW],
                         start=True, stop=True)
                mm = t.matmul(pb[:, 0:GW], onefr[:, :], st_m[0:1, o:o + GW],
                              start=True, stop=True)
                inc(mm, s_pe)

            fns = dict(stat=stat, bc=bc)
            for kind, g in pe_ord:
                fns[kind](g)
        block.tensor(p1_pe)

        for g in range(NP1):
            M[f"p1_apply_{g}"] = (s_ve, VE_I[("apply", g)])
            M[f"p1_st_{g}"] = (s_st, ST_I[("st", g)])
            M[f"p1_stat_{g}"] = (s_pe, PE_I[("stat", g)])

        mark("p1_done_hw", s_hw)
        mark("p1_done_ve", s_ve)

        PH = cfg.get("PHASES", 9)
        if PH < 2:
            meta = dict(cfg=cfg, c=c)
            _finish(block, None)
            return nc, meta
        GP = c.P2GW
        NP2 = c.NP2
        NCC = c.NCC
        NCC2 = max(1, NCC // 2)
        NHALF = NCC // NCC2
        VB = HL * DV
        NHV = 2 if NDC * VB > 32768 else 1
        VB2 = VB // NHV
        NVB = (VB2 + 511) // 512
        NTG = GP // 128

        p1_pe_end = cnt[id(s_pe)]
        p1_st_end = cnt[id(s_st)]
        p2 = dict(sw0=cnt[id(s_sw)], hw0=cnt[id(s_hw)], hwB0=cnt[id(s_hwB)],
                  pe0=cnt[id(s_pe)], ac0=cnt[id(s_ac)], st0=cnt[id(s_st)])

        def p2_gsem(gg):
            if gg % 2 == 0:
                return s_hw, p2["hw0"] + (gg // 2 + 1) * 16
            return s_hwB, p2["hwB0"] + (gg // 2 + 1) * 16
        NU = 2 * NHALF

        def p2_gp(g):
            for u in range(NU):
                pi, hf = divmod(u, NHALF)
                wsrc = wq_l if pi == 0 else wk_l
                if u == 0:
                    g.wait_ge(s_pe, p1_pe_end)
                else:
                    g.wait_ge(s_pe, p2["pe0"] + u * NP2 * NCC2)
                for cc in range(NCC2):
                    ccg = hf * NCC2 + cc
                    inc(g.dma_start(
                        out=big[:, cc * NDC * 128:(cc + 1) * NDC * 128]
                            .rearrange("p (dc i) -> p dc i", i=128),
                        in_=wsrc[0:D, ccg * 128:(ccg + 1) * 128]
                            .rearrange("(dc p) i -> p dc i", p=128)), s_sw, 16)
            nqk = NU * NP2 * NCC2
            for hv in range(NHV):
                g.wait_ge(s_pe, p2["pe0"] + nqk + hv * NP2 * NTG)
                inc(g.dma_start(
                    out=big[:, 0:NDC * VB2].rearrange("p (dc i) -> p dc i", i=VB2),
                    in_=wv_l[0:D, hv * VB2:(hv + 1) * VB2]
                        .rearrange("(dc p) i -> p dc i", p=128)), s_sw, 16)
        block.gpsimd(p2_gp)

        def p2_sync(sync):
            st_i = p2["st0"]
            ac_i = p2["ac0"]
            for u in range(NU):
                pi, hf = divmod(u, NHALF)
                dst = qT_h if pi == 0 else kT_h
                for gI in range(NP2):
                    gg = u * NP2 + gI
                    sl = gg % 2
                    if gg < 2:
                        sync.wait_ge(s_st, p1_st_end)
                    else:
                        pu, pg = divmod(gg - 1, NP2)
                        sync.wait_ge(s_pe, p2["pe0"] + (gg - 1) * NCC2)
                    inc(sync.dma_start(
                        out=ar2[:, sl * NDC * GP:(sl + 1) * NDC * GP]
                            .rearrange("p (dc t) -> p dc t", t=GP),
                        in_=lnT_h[0:D, gI * GP:(gI + 1) * GP]
                            .rearrange("(dc p) t -> p dc t", p=128)),
                        p2_gsem(gg)[0], 16)
                    for cc in range(NCC2):
                        ccg = hf * NCC2 + cc
                        ac_i += 1
                        sync.wait_ge(s_ac, ac_i)
                        slot = (u * NP2 * NCC2 + gI * NCC2 + cc) % 2
                        inc(sync.dma_start(
                            out=dst[ccg * 128:(ccg + 1) * 128, gI * GP:(gI + 1) * GP],
                            in_=evb[:, slot * GP:(slot + 1) * GP]), s_st, 16)
                        st_i += 16
            nqk = NU * NP2 * NCC2
            for hv in range(NHV):
                for gI in range(NP2):
                    vu = hv * NP2 + gI
                    gg = (NU + hv) * NP2 + gI
                    sl = gg % 2
                    if vu == 0:
                        sync.wait_ge(s_pe, p2["pe0"] + nqk - NCC2)
                    else:
                        sync.wait_ge(s_pe, p2["pe0"] + nqk + (vu - 1) * NTG)
                    inc(sync.dma_start(
                        out=ar2[:, sl * NDC * GP:(sl + 1) * NDC * GP]
                            .rearrange("p (dc t) -> p dc t", t=GP),
                        in_=lnT_h[0:D, gI * GP:(gI + 1) * GP]
                            .rearrange("(dc p) t -> p dc t", p=128)),
                        p2_gsem(gg)[0], 16)
                    for tc in range(NTG):
                        ac_i += 1
                        sync.wait_ge(s_ac, ac_i)
                        inc(sync.dma_start(
                            out=v_h[(gI * NTG + tc) * 128:(gI * NTG + tc + 1) * 128,
                                    hv * VB2:(hv + 1) * VB2],
                            in_=evb[:, 0:VB2]), s_st, 16)
        block.sync(p2_sync)

        def p2_pe(t):
            pe_i = p2["pe0"]
            for u in range(NU):
                t.wait_ge(s_sw, p2["sw0"] + (u + 1) * NCC2 * 16)
                for gI in range(NP2):
                    gg = u * NP2 + gI
                    sl = gg % 2
                    t.wait_ge(*p2_gsem(gg))
                    for cc in range(NCC2):
                        slot_idx = u * NP2 * NCC2 + gI * NCC2 + cc
                        if slot_idx >= 2:
                            t.wait_ge(s_ac, p2["ac0"] + slot_idx - 1)
                        for dc in range(NDC):
                            mm = t.matmul(
                                ps[slot_idx % 2][:, 0:GP],
                                big[:, (cc * NDC + dc) * 128:(cc * NDC + dc + 1) * 128],
                                ar2[:, sl * NDC * GP + dc * GP:
                                    sl * NDC * GP + (dc + 1) * GP],
                                start=(dc == 0), stop=(dc == NDC - 1))
                        pe_i += 1
                        inc(mm, s_pe)
            nqk = NU * NP2 * NCC2
            for hv in range(NHV):
                t.wait_ge(s_sw, p2["sw0"] + (NU * NCC2 + hv + 1) * 16)
                for gI in range(NP2):
                    gg = (NU + hv) * NP2 + gI
                    sl = gg % 2
                    t.wait_ge(*p2_gsem(gg))
                    for tc in range(NTG):
                        vi = (hv * NP2 + gI) * NTG + tc
                        if vi >= 1:
                            t.wait_ge(s_ac, p2["ac0"] + nqk + vi)
                        for dc in range(NDC):
                            for nb in range(NVB):
                                w0 = nb * 512
                                w1 = min(VB2, w0 + 512)
                                mm = t.matmul(
                                    ps[nb][:, 0:w1 - w0],
                                    ar2[:, sl * NDC * GP + dc * GP + tc * 128:
                                        sl * NDC * GP + dc * GP + (tc + 1) * 128],
                                    big[:, dc * VB2 + w0:dc * VB2 + w1],
                                    start=(dc == 0), stop=(dc == NDC - 1))
                        inc(mm, s_pe)
        block.tensor(p2_pe)

        def p2_ac(sc):
            st_need = p2["st0"]
            for u in range(NU):
                for gI in range(NP2):
                    for cc in range(NCC2):
                        slot_idx = u * NP2 * NCC2 + gI * NCC2 + cc
                        sc.wait_ge(s_pe, p2["pe0"] + slot_idx + 1)
                        if slot_idx >= 2:
                            sc.wait_ge(s_st, p2["st0"] + slot_idx * 16)
                        slot = slot_idx % 2
                        sc.activation(evb[:, slot * GP:(slot + 1) * GP],
                                      ps[slot_idx % 2][:, 0:GP],
                                      AF.Copy, bias=0.0, scale=1.0)
                        inc(sc.drain(), s_ac)
            nqk = NU * NP2 * NCC2
            for hv in range(NHV):
                for gI in range(NP2):
                    for tc in range(NTG):
                        vi = (hv * NP2 + gI) * NTG + tc
                        sc.wait_ge(s_pe, p2["pe0"] + nqk + vi + 1)
                        if vi >= 1:
                            sc.wait_ge(s_st, p2["st0"] + (nqk + vi) * 16)
                        for nb in range(NVB):
                            w0 = nb * 512
                            w1 = min(VB2, w0 + 512)
                            sc.activation(evb[:, w0:w1], ps[nb][:, 0:w1 - w0],
                                          AF.Copy, bias=0.0, scale=1.0)
                        inc(sc.drain(), s_ac)
        block.scalar(p2_ac)

        p2_pe_end = cnt[id(s_pe)]
        p2_st_end = cnt[id(s_st)]
        p2_ac_end = cnt[id(s_ac)]
        p2_sw_end = cnt[id(s_sw)]
        p2_hw_end = cnt[id(s_hw)]

        if PH < 3:
            meta = dict(cfg=cfg, c=c)
            _finish(block, None)
            return nc, meta
        NQG = S // 512
        KCs = [4 * (qg + 1) for qg in range(NQG)]
        Ps = [k // 2 for k in KCs]
        per_pe = [p + 3 for p in Ps]
        per_ac = [p + 1 for p in Ps]
        per_ve = [p + 5 for p in Ps]
        PEQ = sum(per_pe)
        ACQ = sum(per_ac)
        VEQ = sum(per_ve)
        C_SCALE = float(cfg["SCALE"])
        C_MSHIFT = -float(cfg["SHIFT"])

        p3 = dict(hw0=cnt[id(s_hw)], hwB0=cnt[id(s_hwB)], pe0=cnt[id(s_pe)],
                  ac0=cnt[id(s_ac)], ve0=cnt[id(s_ve)], sw0=cnt[id(s_sw)])

        def p3_hsem(h):
            if h % 2 == 0:
                return s_hw, p3["hw0"] + (h // 2 + 1) * 48
            return s_hwB, p3["hwB0"] + (h // 2 + 1) * 48

        PTOT = sum(Ps)

        def pair_global(h, qg, pi):
            return h * PTOT + sum(Ps[:qg]) + pi

        def pair_from_global(idx):
            h, rem = divmod(idx, PTOT)
            qg = 0
            while rem >= Ps[qg]:
                rem -= Ps[qg]
                qg += 1
            return h, qg, rem

        def prev_gi(h, qg, back=1):
            gi = h * NQG + qg - back
            return divmod(gi, NQG) if gi >= 0 else None

        secs = [(h, qg) for h in range(HL) for qg in range(NQG)]
        pe_seq, ac_seq, ve_seq = [], [], []
        for i, (h, qg) in enumerate(secs):
            P = Ps[qg]
            prev = secs[i - 1] if i >= 1 else None
            pe_seq += [("sc", h, qg, 0), ("sc", h, qg, 1)]
            if prev:
                pe_seq.append(("sum",) + prev)
            for pi in range(P):
                pe_seq.append(("av", h, qg, pi))
                if pi == 0 and prev:
                    pe_seq.append(("bc",) + prev)
                if pi + 2 < P:
                    pe_seq.append(("sc", h, qg, pi + 2))
            ac_seq.append(("exp", h, qg, 0))
            if prev:
                ac_seq.append(("bce",) + prev)
            for pi in range(1, P):
                ac_seq.append(("exp", h, qg, pi))
            loc = []
            for pi in range(P - 2):
                loc.append(("fold", h, qg, pi))
            for pi in range(P - 2, P):
                loc += [("mask", h, qg, pi), ("fold", h, qg, pi)]
            loc.append(("cast", h, qg))
            ve_seq += [loc[0]]
            if prev:
                ve_seq += [("recip",) + prev, ("mul",) + prev]
            ve_seq += loc[1:]
        lasts = secs[-1]
        pe_seq += [("sum",) + lasts, ("bc",) + lasts]
        ac_seq.append(("bce",) + lasts)
        ve_seq += [("recip",) + lasts, ("mul",) + lasts]

        def _counts(seq, base):
            d = {}
            cc = base
            for op in seq:
                if op[0] == "av":
                    if op[3] == Ps[op[2]] - 1:
                        cc += 1
                        d[("av", op[1], op[2])] = cc
                else:
                    cc += 1
                    d[op] = cc
            return d

        PEC = _counts(pe_seq, p3["pe0"])
        ACC = _counts(ac_seq, p3["ac0"])
        VEC = _counts(ve_seq, p3["ve0"])

        def p3_pe_pair(h, qg, pi):
            return PEC[("sc", h, qg, pi)]

        def p3_pe_av(h, qg):
            return PEC[("av", h, qg)]

        def p3_pe_sum(h, qg):
            return PEC[("sum", h, qg)]

        def p3_pe_bc(h, qg):
            return PEC[("bc", h, qg)]

        def p3_ac_exp(h, qg, pi):
            return ACC[("exp", h, qg, pi)]

        def p3_ac_bce(h, qg):
            return ACC[("bce", h, qg)]

        def p3_ve_fold(h, qg, pi):
            return VEC[("fold", h, qg, pi)]

        def p3_ve_mask(h, qg, pi):
            return VEC[("mask", h, qg, pi)]

        def p3_ve_cast(h, qg):
            return VEC[("cast", h, qg)]

        def p3_ve_recip(h, qg):
            return VEC[("recip", h, qg)]

        def p3_ve_mul(h, qg):
            return VEC[("mul", h, qg)]

        def p3_q(h):
            return wkb[:, 0:S] if h % 2 == 0 else ar2[:, 0:S]

        def p3_k(h):
            return wkb2[:, 0:S] if h % 2 == 0 else ar2[:, S:2 * S]

        def p3_v(h):
            return vbuf[:, 0:S] if h % 2 == 0 else ar2[:, 2 * S:3 * S]

        def p3_sync(sync):
            def ld(h):
                if h == 0:
                    sync.wait_ge(s_st, p2_st_end)
                elif h >= 2:
                    sync.wait_ge(s_pe, p3["pe0"] + (h - 1) * PEQ)
                hs = s_hw if h % 2 == 0 else s_hwB
                inc(sync.dma_start(out=p3_q(h),
                                   in_=qT_h[h * 128:(h + 1) * 128, 0:S]), hs, 16)
                inc(sync.dma_start(out=p3_k(h),
                                   in_=kT_h[h * 128:(h + 1) * 128, 0:S]), hs, 16)
                inc(sync.dma_start(
                    out=p3_v(h).rearrange("p (kc i) -> p kc i", i=DV),
                    in_=v_h[0:S, h * DV:(h + 1) * DV]
                        .rearrange("(kc p) i -> p kc i", p=128)), hs, 16)
            ld(0)
            for h in range(1, HL):
                ld(h)
        block.sync(p3_sync)

        def p3_pe(t):
            curh = [-1]

            def headstart(h):
                if h != curh[0]:
                    t.wait_ge(*p3_hsem(h))
                    if h == 0:
                        t.wait_ge(s_ac, p2_ac_end)
                    curh[0] = h

            for op in pe_seq:
                kind, h, qg = op[0], op[1], op[2]
                gi = h * NQG + qg
                if kind == "sc":
                    headstart(h)
                    pi = op[3]
                    gidx = pair_global(h, qg, pi)
                    if gidx >= 2:
                        t.wait_ge(s_ac, p3_ac_exp(*pair_from_global(gidx - 2)))
                    sc = ps_sc0 if gidx % 2 == 0 else ps_sc1
                    for half in range(2):
                        kc = 2 * pi + half
                        mm = t.matmul(sc.ap()[:, half * 512:(half + 1) * 512],
                                      p3_k(h)[:, kc * 128:(kc + 1) * 128],
                                      p3_q(h)[:, qg * 512:(qg + 1) * 512],
                                      start=True, stop=True)
                    inc(mm, s_pe)
                elif kind == "av":
                    pi = op[3]
                    P = Ps[qg]
                    avp = ps[4] if gi % 2 == 0 else ps_av1[:, 0:512]
                    t.wait_ge(s_ac, p3_ac_exp(h, qg, pi))
                    if pi >= P - 2:
                        t.wait_ge(s_ve, p3_ve_mask(h, qg, pi))
                    if pi == 0:
                        pv2 = prev_gi(h, qg, 2)
                        if pv2 is not None:
                            t.wait_ge(s_ve, p3_ve_mul(*pv2))
                    gi2 = pair_global(h, qg, pi) % 2
                    for half in range(2):
                        kc = 2 * pi + half
                        pslice = pt2[:, gi2 * 1024 + half * 512:
                                     gi2 * 1024 + (half + 1) * 512]
                        last = (kc == KCs[qg] - 1)
                        mm = t.matmul(avp, p3_v(h)[:, kc * 128:(kc + 1) * 128],
                                      pslice, start=(kc == 0), stop=last)
                        if last:
                            inc(mm, s_pe)
                elif kind == "sum":
                    t.wait_ge(s_ve, p3_ve_cast(h, qg))
                    pv1 = prev_gi(h, qg, 1)
                    if pv1 is not None:
                        t.wait_ge(s_ve, p3_ve_recip(*pv1))
                    mm = t.matmul(ps_sumA[0:1, 0:512], onebc[:, :],
                                  faccB[:, (gi % 2) * 512:(gi % 2 + 1) * 512],
                                  start=True, stop=True)
                    inc(mm, s_pe)
                else:
                    t.wait_ge(s_ve, p3_ve_recip(h, qg))
                    pv1 = prev_gi(h, qg, 1)
                    if pv1 is not None:
                        t.wait_ge(s_ac, p3_ac_bce(*pv1))
                    mm = t.matmul(ps_bc[:, 0:512], onefr[:, :],
                                  st_m[0:1, 0:512], start=True, stop=True)
                    inc(mm, s_pe)
        block.tensor(p3_pe)

        def p3_ac(sc):
            for op in ac_seq:
                kind, h, qg = op[0], op[1], op[2]
                gi = h * NQG + qg
                if kind == "exp":
                    pi = op[3]
                    sc.wait_ge(s_pe, p3_pe_pair(h, qg, pi))
                    gidx = pair_global(h, qg, pi)
                    if gidx >= 2:
                        sc.wait_ge(s_ve, p3_ve_fold(*pair_from_global(gidx - 2)))
                    scp = ps_sc0 if gidx % 2 == 0 else ps_sc1
                    sc.activation(pt2[:, (gidx % 2) * 1024:(gidx % 2 + 1) * 1024],
                                  scp.ap()[:, :], AF.Exp,
                                  bias=C_MSHIFT, scale=C_SCALE)
                    inc(sc.drain(), s_ac)
                else:
                    sc.wait_ge(s_pe, p3_pe_bc(h, qg))
                    pv1 = prev_gi(h, qg, 1)
                    if pv1 is not None:
                        sc.wait_ge(s_ve, p3_ve_mul(*pv1))
                    sc.copy(bcv[:, (gi % 2) * 512:(gi % 2 + 1) * 512],
                            ps_bc[:, 0:512])
                    inc(sc.drain(), s_ac)
        block.scalar(p3_ac)

        def p3_ve(v):
            for op in ve_seq:
                kind, h, qg = op[0], op[1], op[2]
                gi = h * NQG + qg
                P = Ps[qg]
                if kind == "fold":
                    pi = op[3]
                    if pi < P - 2:
                        v.wait_ge(s_ac, p3_ac_exp(h, qg, pi))
                    gi2 = pair_global(h, qg, pi) % 2
                    h0 = pt2[:, gi2 * 1024:gi2 * 1024 + 512]
                    h1 = pt2[:, gi2 * 1024 + 512:(gi2 + 1) * 1024]
                    if pi == 0:
                        v.tensor_tensor(facc[:, 0:512], h0, h1, ALU.add)
                        inc(v.drain(), s_ve)
                    else:
                        v.tensor_tensor(pfb[:, 0:512], h0, h1, ALU.add)
                        v.drain()
                        v.tensor_tensor(facc[:, 0:512], facc[:, 0:512],
                                        pfb[:, 0:512], ALU.add)
                        inc(v.drain(), s_ve)
                elif kind == "mask":
                    pi = op[3]
                    v.wait_ge(s_ac, p3_ac_exp(h, qg, pi))
                    gi2 = pair_global(h, qg, pi) % 2
                    j0 = 2 * (pi - (P - 2))
                    v.tensor_tensor(pt2[:, gi2 * 1024:(gi2 + 1) * 1024],
                                    pt2[:, gi2 * 1024:(gi2 + 1) * 1024],
                                    cmv[:, j0 * 512:(j0 + 2) * 512], ALU.mult)
                    inc(v.drain(), s_ve)
                elif kind == "cast":
                    pv2 = prev_gi(h, qg, 2)
                    if pv2 is not None:
                        v.wait_ge(s_pe, p3_pe_sum(*pv2))
                    v.tensor_copy(faccB[:, (gi % 2) * 512:(gi % 2 + 1) * 512],
                                  facc[:, 0:512])
                    inc(v.drain(), s_ve)
                elif kind == "recip":
                    v.wait_ge(s_pe, p3_pe_sum(h, qg))
                    v.reciprocal(st_m[0:1, 0:512], ps_sumA[0:1, 0:512])
                    inc(v.drain(), s_ve)
                else:
                    avp = ps[4] if gi % 2 == 0 else ps_av1[:, 0:512]
                    v.wait_ge(s_pe, p3_pe_av(h, qg))
                    v.wait_ge(s_ac, p3_ac_bce(h, qg))
                    if gi >= 1:
                        v.wait_ge(s_sw, p3["sw0"] + gi * 16)
                    v.tensor_tensor(attc[:, (qg % 2) * 512:(qg % 2 + 1) * 512],
                                    avp, bcv[:, (gi % 2) * 512:(gi % 2 + 1) * 512],
                                    ALU.mult)
                    inc(v.drain(), s_ve)
        block.vector(p3_ve)

        HLR = HL * 128

        def p3_gp(g):
            g.wait_ge(s_sw, p2_sw_end)
            g.wait_ge(s_hw, M["p0_const"][1])
            with g.register("pregh") as preg:
                g.reg_load(preg, prmv[0:1, 8:9])
                for h in range(HL):
                    for qg in range(NQG):
                        g.wait_ge(s_ve, p3_ve_mul(h, qg))
                        inc(g.dma_start(
                            out=attnT_loc[h * 128:(h + 1) * 128,
                                          qg * 512:(qg + 1) * 512],
                            in_=attc[:, (qg % 2) * 512:(qg % 2 + 1) * 512]), s_sw, 16)
                    g.wait_ge(s_sw, cnt[id(s_sw)])
                    with g.If_eq(preg, 0):
                        g.dma_start(
                            out=sh_attnT[h * 128:(h + 1) * 128, 0:S],
                            in_=attnT_loc[h * 128:(h + 1) * 128, 0:S]).then_inc(s_sw, 16)
                    with g.Else():
                        g.dma_start(
                            out=sh_attnT[HLR + h * 128:HLR + (h + 1) * 128, 0:S],
                            in_=attnT_loc[h * 128:(h + 1) * 128, 0:S]).then_inc(s_sw, 16)
                    cnt[id(s_sw)] += 16
        block.gpsimd(p3_gp)

        p3_pe_end = cnt[id(s_pe)]
        p3_sw_end = cnt[id(s_sw)]
        p3_hw_end = cnt[id(s_hw)]
        p3_ve_end = cnt[id(s_ve)]
        p3_ac_end = cnt[id(s_ac)]

        if PH < 3.5:
            meta = dict(cfg=cfg, c=c)
            _finish(block, None)
            return nc, meta
        def p35_gp(g):
            g.wait_ge(s_sw, p3_sw_end)
            with g.register("preg") as preg, g.register("qreg") as qreg:
                g.reg_load(preg, prmv[0:1, 8:9])
                g.reg_load(qreg, prmv[0:1, 1:2])
                with g.If_eq(preg, 0):
                    g.dma_start(out=sh_flags[0:1, 0:2],
                                in_=prmv[0:1, 4:6]).then_inc(s_sw, 16)
                with g.Else():
                    g.dma_start(out=sh_flags[1:2, 0:2],
                                in_=prmv[0:1, 4:6]).then_inc(s_sw, 16)
                cnt[id(s_sw)] += 16
                g.wait_ge(s_sw, cnt[id(s_sw)])
                pr0, pr1 = (0, 1) if RL else (1, 0)
                with g.register("cond") as cond, g.register("tgt") as tgt, \
                     g.register("fa") as fa, g.register("fb") as fb, \
                     g.register("ea") as ea, g.register("eb") as eb:
                    g.reg_load(ea, prmv[0:1, 4:5])
                    g.reg_load(eb, prmv[0:1, 5:6])
                    g.reg_mov(cond, 1)
                    g.reg_mov(tgt, 0)
                    with g.While(cond):
                        g.reg_add(tgt, tgt, 16)
                        with g.If_eq(preg, 0):
                            g.dma_start(out=flgrd[0:1, 0:2],
                                        in_=sh_flags[pr0:pr0 + 1, 0:2]
                                        ).then_inc(s_po1, 16)
                        with g.Else():
                            g.dma_start(out=flgrd[0:1, 0:2],
                                        in_=sh_flags[pr1:pr1 + 1, 0:2]
                                        ).then_inc(s_po1, 16)
                        g.wait_ge(s_po1, tgt)
                        g.reg_load(fa, flgrd[0:1, 0:1])
                        g.reg_load(fb, flgrd[0:1, 1:2])
                        g.reg_alu(fa, fa, ea, ALU.is_equal)
                        g.reg_alu(fb, fb, eb, ALU.is_equal)
                        g.reg_alu(cond, fa, fb, ALU.bitwise_and)
                        g.reg_alu(cond, cond, 1, ALU.bitwise_xor)
                def pull(which, k):
                    d0 = which * QS
                    if not RL:
                        g.dma_start(out=attnT_l2[:, d0:d0 + QS],
                                    in_=sh_attnT[0:c.HDV, k * QS:(k + 1) * QS]
                                    ).then_inc(s_sw, 16)
                        return
                    HDV2 = c.HDV // 2
                    with g.If_eq(preg, 0):
                        g.dma_start(out=attnT_l2[0:HDV2, d0:d0 + QS],
                                    in_=sh_attnT[0:HDV2, k * QS:(k + 1) * QS]
                                    ).then_inc(s_sw, 16)
                        g.dma_start(out=attnT_l2[HDV2:c.HDV, d0:d0 + QS],
                                    in_=sh_attnT[0:HDV2, k * QS:(k + 1) * QS]
                                    ).then_inc(s_sw, 16)
                    with g.Else():
                        g.dma_start(out=attnT_l2[0:HDV2, d0:d0 + QS],
                                    in_=sh_attnT[HDV2:c.HDV, k * QS:(k + 1) * QS]
                                    ).then_inc(s_sw, 16)
                        g.dma_start(out=attnT_l2[HDV2:c.HDV, d0:d0 + QS],
                                    in_=sh_attnT[HDV2:c.HDV, k * QS:(k + 1) * QS]
                                    ).then_inc(s_sw, 16)

                for which in range(2):
                    with g.If_eq(qreg, 0):
                        pull(which, (0 + 2 * which) % 4)
                    with g.Else():
                        with g.If_eq(qreg, 1):
                            pull(which, (1 + 2 * which) % 4)
                        with g.Else():
                            with g.If_eq(qreg, 2):
                                pull(which, (2 + 2 * which) % 4)
                            with g.Else():
                                pull(which, (3 + 2 * which) % 4)
                    cnt[id(s_sw)] += 32 if RL else 16
                    g.wait_ge(s_sw, cnt[id(s_sw)])
                    g.sem_inc(s_gp, 1)
                    cnt[id(s_gp)] += 1
        block.gpsimd(p35_gp)
        mark("pair1_ready", s_gp)
        M["pull1_done"] = (s_gp, cnt[id(s_gp)] - 1)

        OQN4 = 4
        OQ4 = O // OQN4
        NHC = c.HDV // 128
        NMT = c.NMT
        NB4 = OQ4 // 512 if OQ4 >= 512 else 1
        W4 = min(OQ4, 512)

        p4 = dict(hw0=cnt[id(s_hw)], hwB0=cnt[id(s_hwB)], sw0=cnt[id(s_sw)],
                  pe0=cnt[id(s_pe)], ve0=cnt[id(s_ve)], st0=cnt[id(s_st)])

        def p4_usem(ui, k):
            if ui % 2 == 0:
                return s_hw, p4["hw0"] + (ui // 2) * 32 + k * 16
            return s_hwB, p4["hwB0"] + (ui // 2) * 32 + k * 16

        def p4_gp(g):
            for oq in range(OQN4):
                if oq >= 1:
                    g.wait_ge(s_pe, p4["pe0"] + oq * NMT)
                inc(g.dma_start(
                    out=big[:, 0:NHC * OQ4].rearrange("p (hc o) -> p hc o", o=OQ4),
                    in_=wo_full[0:c.HDV, oq * OQ4:(oq + 1) * OQ4]
                        .rearrange("(hc p) o -> p hc o", p=128)), s_sw, 16)
        block.gpsimd(p4_gp)

        def p4_sync(sync):
            st_i = p4["st0"]
            for oq in range(OQN4):
                for tc in range(NMT):
                    ui = oq * NMT + tc
                    if oq == 0:
                        wait(sync, "pull1_done" if tc < NMT // 2 else "pair1_ready")
                    if ui >= 2:
                        sync.wait_ge(s_pe, p4["pe0"] + ui - 1)
                    us = s_hw if ui % 2 == 0 else s_hwB
                    inc(sync.dma_start(
                        out=attsb[:, (ui % 2) * NHC * 128:(ui % 2 + 1) * NHC * 128]
                            .rearrange("p (hc i) -> p hc i", i=128),
                        in_=attnT_l2[0:c.HDV, tc * 128:(tc + 1) * 128]
                            .rearrange("(hc p) i -> p hc i", p=128)), us, 16)
                    if ui >= 2:
                        sync.wait_ge(s_ve, p4["ve0"] + ui - 1)
                    inc(sync.dma_start(
                        out=xres[:, (ui % 2) * OQ4:(ui % 2 + 1) * OQ4],
                        in_=x_strips[tc * 128:(tc + 1) * 128,
                                     oq * OQ4:(oq + 1) * OQ4]), us, 16)
                    if ui >= 1:
                        po, ptc = divmod(ui - 1, NMT)
                        sync.wait_ge(s_ve, p4["ve0"] + ui)
                        inc(sync.dma_start(
                            out=xaft_h[ptc * 128:(ptc + 1) * 128,
                                       po * OQ4:(po + 1) * OQ4],
                            in_=evf[:, ((ui - 1) % 2) * OQ4:((ui - 1) % 2 + 1) * OQ4]),
                            s_st, 16)
            ui = OQN4 * NMT
            po, ptc = divmod(ui - 1, NMT)
            sync.wait_ge(s_ve, p4["ve0"] + ui)
            inc(sync.dma_start(
                out=xaft_h[ptc * 128:(ptc + 1) * 128, po * OQ4:(po + 1) * OQ4],
                in_=evf[:, ((ui - 1) % 2) * OQ4:((ui - 1) % 2 + 1) * OQ4]), s_st, 16)
        block.sync(p4_sync)

        def p4_pe(t):
            t.wait_ge(s_ac, p3_ac_end)
            t.wait_ge(s_ve, p3_ve_end)
            for oq in range(OQN4):
                t.wait_ge(s_sw, p4["sw0"] + (oq + 1) * 16)
                for tc in range(NMT):
                    ui = oq * NMT + tc
                    t.wait_ge(*p4_usem(ui, 2))
                    if ui >= 2:
                        t.wait_ge(s_ve, p4["ve0"] + ui - 1)
                    pp = ui % 2
                    for hc in range(NHC):
                        for nb in range(NB4):
                            w0 = nb * 512
                            w1 = min(OQ4, w0 + 512)
                            mm = t.matmul(
                                ps[2 * pp + nb][:, 0:w1 - w0],
                                attsb[:, (ui % 2) * NHC * 128 + hc * 128:
                                      (ui % 2) * NHC * 128 + (hc + 1) * 128],
                                big[:, hc * OQ4 + w0:hc * OQ4 + w1],
                                start=(hc == 0), stop=(hc == NHC - 1))
                    inc(mm, s_pe)
        block.tensor(p4_pe)

        def p4_ve(v):
            for oq in range(OQN4):
                for tc in range(NMT):
                    ui = oq * NMT + tc
                    v.wait_ge(s_pe, p4["pe0"] + ui + 1)
                    v.wait_ge(*p4_usem(ui, 2))
                    if ui >= 2:
                        v.wait_ge(s_st, p4["st0"] + ui * 16)
                    pp = ui % 2
                    for nb in range(NB4):
                        w0 = nb * 512
                        w1 = min(OQ4, w0 + 512)
                        v.tensor_tensor(evf[:, pp * OQ4 + w0:pp * OQ4 + w1],
                                        ps[2 * pp + nb][:, 0:w1 - w0],
                                        xres[:, pp * OQ4 + w0:pp * OQ4 + w1],
                                        ALU.add)
                    inc(v.drain(), s_ve)
                    ev = evf[:, pp * OQ4:pp * OQ4 + OQ4]
                    v.reduce_sum(l2p[:, 4 * tc + oq:4 * tc + oq + 1], ev, AX.X)
                    v.tensor_tensor(bcv[:, 0:OQ4], ev, ev, ALU.mult)
                    v.drain()
                    v.reduce_sum(l2p[:, 64 + 4 * tc + oq:64 + 4 * tc + oq + 1],
                                 bcv[:, 0:OQ4], AX.X)
        block.vector(p4_ve)

        p4_pe_end = cnt[id(s_pe)]
        p4_st_end = cnt[id(s_st)]
        p4_ve_end = cnt[id(s_ve)]
        p4_hw_end = cnt[id(s_hw)]
        p4_sw_end = cnt[id(s_sw)]

        if PH < 4.5:
            meta = dict(cfg=cfg, c=c)
            _finish(block, None)
            return nc, meta
        p45 = dict(pe0=cnt[id(s_pe)], ve0=cnt[id(s_ve)], ac0=cnt[id(s_ac)],
                   st20=cnt[id(s_st2)], st2B0=cnt[id(s_st2B)],
                   ac20=cnt[id(s_ac2)], ac2B0=cnt[id(s_ac2B)])
        arf2f = attsb.bitcast(F32)

        def p45_stw(tc):
            if tc % 2 == 0:
                return s_st2, p45["st20"] + (tc // 2 + 1) * 16
            return s_st2B, p45["st2B0"] + (tc // 2 + 1) * 32

        def p45_ld(tc):
            if tc % 2 == 0:
                return s_ac2, p45["ac20"] + (tc // 2 + 1) * 16
            return s_ac2B, p45["ac2B0"] + (tc // 2 + 1) * 16

        def p45_tr(tc, dc):
            return p45["pe0"] + tc * NDC + dc + 1

        def p45_xb(tc):
            return arf if tc % 2 == 0 else arf2f

        def p45_ob(tc):
            return wkb if tc % 2 == 0 else vbuf

        def p45_ev_tgt(tc, dc):
            if tc % 2 == 0:
                return wkb2[:, dc * 128:(dc + 1) * 128]
            if dc < NDC // 2:
                return pt2[:, dc * 128:(dc + 1) * 128]
            return evb[:, (dc - NDC // 2) * 128:(dc - NDC // 2 + 1) * 128]

        evA = {}
        evV = {}
        apA = {}

        def p45_ve(v):
            v.drain()
            for tc in range(NMT):
                v.reduce_sum(l2p[:, 224 + tc:225 + tc],
                             l2p[:, 4 * tc:4 * tc + 4], AX.X)
                v.reduce_sum(l2p[:, 208 + tc:209 + tc],
                             l2p[:, 64 + 4 * tc:64 + 4 * tc + 4], AX.X)
            v.drain()
            NM = NMT
            v.tensor_scalar_mul(l2p[:, 128:128 + NM], l2p[:, 224:224 + NM],
                                1.0 / D)
            v.tensor_scalar_mul(l2p[:, 240:240 + NM], l2p[:, 208:208 + NM],
                                1.0 / D)
            v.drain()
            v.tensor_tensor(l2p[:, 192:192 + NM], l2p[:, 128:128 + NM],
                            l2p[:, 128:128 + NM], ALU.mult)
            v.drain()
            v.tensor_tensor(l2p[:, 144:144 + NM], l2p[:, 240:240 + NM],
                            l2p[:, 192:192 + NM], ALU.subtract)
            inc(v.drain(), s_ve)
            v.wait_ge(s_ac, p45["ac0"] + 1)
            v.tensor_tensor(l2p[:, 176:176 + NM], l2p[:, 128:128 + NM],
                            l2p[:, 160:160 + NM], ALU.mult)
            v.drain()
            v.tensor_scalar_mul(l2p[:, 176:176 + NM], l2p[:, 176:176 + NM],
                                -1.0)
            inc(v.drain(), s_ve)
            for tc in range(NMT):
                for dc in range(1, NDC, 2):
                    v.wait_ge(s_pe, p45_tr(tc, dc))
                    if tc >= 2 and dc == 1:
                        v.wait_ge(*p45_stw(tc - 2))
                    op = v.tensor_copy(p45_ev_tgt(tc, dc), ps_tr2[:, 0:128])
                    inc(op, s_ve)
                    evV[(tc, dc)] = cnt[id(s_ve)]
        block.vector(p45_ve)

        def p45_emit_apply(sc, tc):
            sc.wait_ge(*p45_ld(tc))
            if tc < 2:
                sc.wait_ge(s_ve, p45["ve0"] + 2)
            else:
                sc.wait_ge(s_pe, p45_tr(tc - 2, NDC - 1))
            sc.activation(p45_ob(tc)[:, 0:D], p45_xb(tc)[:, 0:D], AF.Identity,
                          bias=l2p[:, 176 + tc:177 + tc],
                          scale=l2p[:, 160 + tc:161 + tc])
            inc(sc.drain(), s_ac)
            apA[tc] = cnt[id(s_ac)]
            if tc + 2 < NMT:
                ldsem = p45_ld(tc + 2)[0]
                inc(sc.dma_start(out=p45_xb(tc + 2)[:, 0:D],
                                 in_=xaft_h[(tc + 2) * 128:(tc + 3) * 128, 0:D]),
                    ldsem, 16)

        def p45_ac(sc):
            sc.wait_ge(s_st, p4_st_end)
            inc(sc.dma_start(out=arf[:, 0:D], in_=xaft_h[0:128, 0:D]),
                s_ac2, 16)
            if NMT > 1:
                sc.wait_ge(s_pe, p4_pe_end)
                inc(sc.dma_start(out=arf2f[:, 0:D], in_=xaft_h[128:256, 0:D]),
                    s_ac2B, 16)
            sc.wait_ge(s_ve, p45["ve0"] + 1)
            sc.activation(l2p[:, 192:192 + NMT], l2p[:, 144:144 + NMT], AF.Ln,
                          bias=float(cfg["EPS"]), scale=1.0)
            sc.drain()
            sc.activation(l2p[:, 160:160 + NMT], l2p[:, 192:192 + NMT], AF.Exp,
                          bias=0.0, scale=-0.5)
            inc(sc.drain(), s_ac)
            p45_emit_apply(sc, 0)
            if NMT > 1:
                p45_emit_apply(sc, 1)
            for tc in range(NMT):
                for dc in range(0, NDC, 2):
                    sc.wait_ge(s_pe, p45_tr(tc, dc))
                    if tc >= 2 and dc == 0:
                        sc.wait_ge(*p45_stw(tc - 2))
                    sc.copy(p45_ev_tgt(tc, dc), ps_tr[:, 0:128])
                    inc(sc.drain(), s_ac)
                    evA[(tc, dc)] = cnt[id(s_ac)]
                if tc + 2 < NMT:
                    p45_emit_apply(sc, tc + 2)
        block.scalar(p45_ac)

        def p45_pe(t):
            for tc in range(NMT):
                t.wait_ge(s_ac, apA[tc])
                src = p45_ob(tc)
                for dc in range(NDC):
                    idx = tc * NDC + dc
                    if idx >= 2:
                        ptc, pdc = divmod(idx - 2, NDC)
                        if pdc % 2 == 0:
                            t.wait_ge(s_ac, evA[(ptc, pdc)])
                        else:
                            t.wait_ge(s_ve, evV[(ptc, pdc)])
                    pt = ps_tr if dc % 2 == 0 else ps_tr2
                    tr = t.transpose(pt[:, 0:128],
                                     src[:, dc * 128:(dc + 1) * 128], idn[:, :])
                    inc(tr, s_pe)
        block.tensor(p45_pe)

        def p45_sync(sync):
            for tc in range(NMT):
                sync.wait_ge(s_ac, evA[(tc, NDC - 2)])
                sync.wait_ge(s_ve, evV[(tc, NDC - 1)])
                if tc % 2 == 0:
                    inc(sync.dma_start(
                        out=ln2T_h[0:D, tc * 128:(tc + 1) * 128]
                            .rearrange("(dc p) t -> p dc t", p=128),
                        in_=wkb2[:, 0:NDC * 128]
                            .rearrange("p (dc t) -> p dc t", t=128)), s_st2, 16)
                else:
                    ND2 = NDC // 2
                    inc(sync.dma_start(
                        out=ln2T_h[0:ND2 * 128, tc * 128:(tc + 1) * 128]
                            .rearrange("(dc p) t -> p dc t", p=128),
                        in_=pt2[:, 0:ND2 * 128]
                            .rearrange("p (dc t) -> p dc t", t=128)), s_st2B, 16)
                    inc(sync.dma_start(
                        out=ln2T_h[ND2 * 128:D, tc * 128:(tc + 1) * 128]
                            .rearrange("(dc p) t -> p dc t", p=128),
                        in_=evb[:, 0:ND2 * 128]
                            .rearrange("p (dc t) -> p dc t", t=128)), s_st2B, 16)
        block.sync(p45_sync)

        p45_pe_end = cnt[id(s_pe)]
        p45_st2_end = cnt[id(s_st2)]
        p45_st2B_end = cnt[id(s_st2B)]
        p45_ac_end = cnt[id(s_ac)]
        p45_ve_end = cnt[id(s_ve)]

        def p45_st_all(eng):
            eng.wait_ge(s_st2, p45_st2_end)
            if p45_st2B_end > 0:
                eng.wait_ge(s_st2B, p45_st2B_end)

        OQ6 = c.OQ
        NB6 = max(1, OQ6 // 512)
        NFT = c.NFT
        arfB = arf.bitcast(BF16)
        wcaps = [(wkb2, S // OQ6), (arfB, (2 * NDC * c.P1GW) // OQ6),
                 (pt2, 2048 // OQ6), (evb, 2048 // OQ6)]
        wfrags = []
        _d0 = 0
        for _buf, _cap in wcaps:
            _nd = min(_cap, NDC - _d0)
            if _nd > 0:
                wfrags.append((_buf, _d0, _nd))
                _d0 += _nd
        assert _d0 == NDC, (_d0, NDC)
        m_w = {}
        gm = {}

        def emit_w(g, oq):
            ws = s_wv0 if oq % 2 == 0 else s_wv1
            if oq % 2 == 0:
                inc(g.dma_start(
                    out=ar2[:, 0:NDC * OQ6].rearrange("p (dc o) -> p dc o",
                                                      o=OQ6),
                    in_=WTt[0:D, oq * OQ6:(oq + 1) * OQ6]
                        .rearrange("(dc p) o -> p dc o", p=128)), ws, 16)
            else:
                for buf, d0, nd in wfrags:
                    inc(g.dma_start(
                        out=buf[:, 0:nd * OQ6].rearrange("p (dc o) -> p dc o",
                                                         o=OQ6),
                        in_=WTt[d0 * 128:(d0 + nd) * 128,
                                oq * OQ6:(oq + 1) * OQ6]
                            .rearrange("(dc p) o -> p dc o", p=128)), ws, 16)
            m_w[oq] = (ws, cnt[id(ws)])

        def w_rhs(oq, dc, w0, w1):
            if oq % 2 == 0:
                return ar2[:, dc * OQ6 + w0:dc * OQ6 + w1]
            for buf, d0, nd in wfrags:
                if d0 <= dc < d0 + nd:
                    return buf[:, (dc - d0) * OQ6 + w0:(dc - d0) * OQ6 + w1]
            raise AssertionError(dc)

        def p45_gp(g):
            g.wait_ge(s_pe, p3_pe_end)
            emit_w(g, 0)
            g.wait_ge(s_pe, p4_pe_end)
            h6 = NMT // 2
            g.wait_ge(s_st2, p45["st20"] + ((h6 + 1) // 2) * 16)
            if h6 // 2 > 0:
                g.wait_ge(s_st2B, p45["st2B0"] + (h6 // 2) * 32)
            inc(g.dma_start(
                out=big[:, 0:NDC * QS].rearrange("p (dc t) -> p dc t", t=QS),
                in_=ln2T_h[0:D, 0:QS].rearrange("(dc p) t -> p dc t", p=128)),
                s_g1, 16)
            g.wait_ge(s_pe, p45_pe_end)
            inc(g.dma_start(out=atv[:, 0:NDC * ER]
                            .rearrange("p (dc e) -> p dc e", e=ER),
                            in_=ATt.rearrange("(dc p) e -> p dc e", p=128)),
                s_g2, 16)
            g.wait_ge(s_ac, p45_ac_end)
            inc(g.dma_start(out=attsb[0:1, 0:O], in_=bTt[:, :]), s_g3, 16)
            inc(g.dma_start(out=attsb[:, O:2 * O], in_=BTt[:, :]), s_g3, 16)
        block.gpsimd(p45_gp)

        ttm = []

        def p6t_pe(t):
            t.wait_ge(s_g1, 16)
            t.wait_ge(s_g2, 16)
            t.wait_ge(s_ve, p4_ve_end)
            for tc in range(NFT):
                psd = ps_sc0.ap() if tc < 4 else ps_sc1.ap()
                col = (tc % 4) * 128
                for dc in range(NDC):
                    mm = t.matmul(
                        psd[:, col:col + ER],
                        big[:, dc * QS + tc * 128:dc * QS + (tc + 1) * 128],
                        atv[:, dc * ER:(dc + 1) * ER],
                        start=(dc == 0), stop=(dc == NDC - 1))
                inc(mm, s_pe)
                ttm.append(cnt[id(s_pe)])
        block.tensor(p6t_pe)

        if PH < 5:
            meta = dict(cfg=cfg, c=c)
            _finish(block, None)
            return nc, meta
        p5 = dict(hw0=cnt[id(s_hw)], hwB0=cnt[id(s_hwB)], sw0=cnt[id(s_sw)],
                  pe0=cnt[id(s_pe)], ve0=cnt[id(s_ve)], ac0=cnt[id(s_ac)],
                  gp0=cnt[id(s_gp)])

        def p5_dsem(dc):
            if dc % 2 == 0:
                return s_hw, p5["hw0"] + (dc // 2 + 1) * 16
            return s_hwB, p5["hwB0"] + (dc // 2 + 1) * 16

        def p5_sync(sync):
            p45_st_all(sync)
            for dc in range(NDC):
                if dc >= 2:
                    sync.wait_ge(s_ve, p5["ve0"] + dc - 1)
                dst = pt2 if dc % 2 == 0 else evb
                inc(sync.dma_start(
                    out=dst[:, 0:c.MYTOK],
                    in_=ln2T_h[dc * 128:(dc + 1) * 128, 0:c.MYTOK]),
                    p5_dsem(dc)[0], 16)
        block.sync(p5_sync)

        def p5_ve(v):
            v.wait_ge(s_sw, p5["sw0"] + 16)
            for dc in range(NDC):
                v.wait_ge(*p5_dsem(dc))
                src = pt2 if dc % 2 == 0 else evb
                v.tensor_tensor(arf[:, 0:c.MYTOK], src[:, 0:c.MYTOK],
                                wmv[:, 0:c.MYTOK], ALU.mult)
                v.drain()
                v.reduce_sum(qpart[:, dc:dc + 1], arf[:, 0:c.MYTOK], AX.X)
                inc(v.drain(), s_ve)
        block.vector(p5_ve)

        def p5_gp(g):
            p45_st_all(g)
            inc(g.dma_start(out=wmv[:, 0:c.MYTOK], in_=wmaskt[:, :]), s_sw, 16)
            g.wait_ge(s_ve, p5["ve0"] + NDC)
            with g.register("pr5") as preg:
                g.reg_load(preg, prmv[0:1, 8:9])
                with g.If_eq(preg, 0):
                    g.dma_start(out=sh_qpart[0:128, 0:NDC],
                                in_=qpart[:, 0:NDC]).then_inc(s_sw, 16)
                with g.Else():
                    g.dma_start(out=sh_qpart[128:256, 0:NDC],
                                in_=qpart[:, 0:NDC]).then_inc(s_sw, 16)
                cnt[id(s_sw)] += 16
                g.wait_ge(s_sw, cnt[id(s_sw)])
                with g.If_eq(preg, 0):
                    g.dma_start(out=sh_flags[0:1, 2:4],
                                in_=prmv[0:1, 6:8]).then_inc(s_sw, 16)
                with g.Else():
                    g.dma_start(out=sh_flags[1:2, 2:4],
                                in_=prmv[0:1, 6:8]).then_inc(s_sw, 16)
                cnt[id(s_sw)] += 16
                g.wait_ge(s_sw, cnt[id(s_sw)])
                with g.register("cond2") as cond, g.register("tgt2") as tgt, \
                     g.register("fa2") as fa, g.register("fb2") as fb, \
                     g.register("ea2") as ea, g.register("eb2") as eb:
                    g.reg_load(ea, prmv[0:1, 6:7])
                    g.reg_load(eb, prmv[0:1, 7:8])
                    g.reg_mov(cond, 1)
                    g.reg_mov(tgt, 0)
                    pr0, pr1 = (0, 1) if RL else (1, 0)
                    with g.While(cond):
                        g.reg_add(tgt, tgt, 16)
                        with g.If_eq(preg, 0):
                            g.dma_start(out=flgrd[0:1, 2:4],
                                        in_=sh_flags[pr0:pr0 + 1, 2:4]
                                        ).then_inc(s_po2, 16)
                        with g.Else():
                            g.dma_start(out=flgrd[0:1, 2:4],
                                        in_=sh_flags[pr1:pr1 + 1, 2:4]
                                        ).then_inc(s_po2, 16)
                        g.wait_ge(s_po2, tgt)
                        g.reg_load(fa, flgrd[0:1, 2:3])
                        g.reg_load(fb, flgrd[0:1, 3:4])
                        g.reg_alu(fa, fa, ea, ALU.is_equal)
                        g.reg_alu(fb, fb, eb, ALU.is_equal)
                        g.reg_alu(cond, fa, fb, ALU.bitwise_and)
                        g.reg_alu(cond, cond, 1, ALU.bitwise_xor)
                qr0, qr1 = (0, 128) if RL else (128, 0)
                with g.If_eq(preg, 0):
                    g.dma_start(out=qpeer[:, 0:NDC],
                                in_=sh_qpart[qr0:qr0 + 128, 0:NDC]
                                ).then_inc(s_sw, 16)
                with g.Else():
                    g.dma_start(out=qpeer[:, 0:NDC],
                                in_=sh_qpart[qr1:qr1 + 128, 0:NDC]
                                ).then_inc(s_sw, 16)
                cnt[id(s_sw)] += 16
                g.wait_ge(s_sw, cnt[id(s_sw)])
            g.sem_inc(s_gp, 1)
            cnt[id(s_gp)] += 1
        block.gpsimd(p5_gp)
        mark("pair2_ready", s_gp)

        def p5_ve2(v):
            wait(v, "pair2_ready")
            v.tensor_tensor(qpart[:, 0:NDC], qpart[:, 0:NDC], qpeer[:, 0:NDC],
                            ALU.add)
            v.drain()
            for e in range(E):
                v.tensor_tensor(wk1[:, 0:NDC], qpart[:, 0:NDC],
                                rwv[:, e * NDC:(e + 1) * NDC], ALU.mult)
                v.drain()
                v.reduce_sum(wk1[:, NDC + e:NDC + e + 1], wk1[:, 0:NDC], AX.X)
                v.drain()
            inc(v.engine_nop(), s_ve)
        block.vector(p5_ve2)
        mark("p5_dots", s_ve)

        def p5_pe(t):
            wait(t, "p5_dots")
            t.wait_ge(s_ac, p45_ac_end)
            mm = t.matmul(ps_sumA[0:1, 0:E], onefc[:, 0:1],
                          wk1[:, NDC:NDC + E], start=True, stop=True)
            inc(mm, s_pe)
        block.tensor(p5_pe)
        mark("p5_logits_mm", s_pe)

        def p5_ve3(v):
            wait(v, "p5_logits_mm")
            v.tensor_scalar_mul(r8[0:1, 0:E], ps_sumA[0:1, 0:E], 1.0 / c.WIN_N)
            v.drain()
            v.tensor_tensor(r8[0:1, 0:E], r8[0:1, 0:E], rbv[0:1, 0:E], ALU.add)
            v.drain()
            v.reduce_max(r8[0:1, E:E + 1], r8[0:1, 0:E], AX.X)
            v.drain()
            v.tensor_tensor(r8[0:1, 8:8 + E], r8[0:1, 0:E],
                            r8[0:1, E:E + 1].broadcast_to([1, E]), ALU.subtract)
            inc(v.drain(), s_ve)
        block.vector(p5_ve3)
        mark("p5_lg", s_ve)

        def p5_ac(sc):
            wait(sc, "p5_lg")
            sc.activation(r8[0:1, 16:16 + E], r8[0:1, 8:8 + E], AF.Exp,
                          bias=0.0, scale=1.0)
            inc(sc.drain(), s_ac)
        block.scalar(p5_ac)
        mark("p5_exp", s_ac)

        def p5_ve4(v):
            wait(v, "p5_exp")
            v.reduce_sum(r8[0:1, 24:25], r8[0:1, 16:16 + E], AX.X)
            v.drain()
            v.reciprocal(r8[0:1, 25:26], r8[0:1, 24:25])
            v.drain()
            v.tensor_tensor(r8[0:1, 32:32 + E], r8[0:1, 16:16 + E],
                            r8[0:1, 25:26].broadcast_to([1, E]), ALU.mult)
            v.drain()
            v.tensor_copy(r8b[0:1, 0:E], r8[0:1, 32:32 + E])
            inc(v.drain(), s_ve)
        block.vector(p5_ve4)
        mark("p5_routing", s_ve)

        def p5_pe2(t):
            wait(t, "p5_routing")
            t.wait_ge(s_ac, p45_ac_end)
            tr = t.transpose(ps_tr[0:E, 0:1], r8b[0:1, 0:E], idn[0:1, 0:1])
            inc(tr, s_pe)
        block.tensor(p5_pe2)
        mark("p5_rT", s_pe)

        def p5_ac2(sc):
            wait(sc, "p5_rT")
            sc.copy(r8b[0:E, 1:2], ps_tr[0:E, 0:1])
            inc(sc.drain(), s_ac)
        block.scalar(p5_ac2)
        mark("p5_rTs", s_ac)

        def p5_pe3(t):
            wait(t, "p5_rTs")
            mm = t.matmul(ps[4][:, 0:1], repv[0:E, 0:ER], r8b[0:E, 1:2],
                          start=True, stop=True)
            inc(mm, s_pe)
        block.tensor(p5_pe3)
        mark("p5_rexp_mm", s_pe)

        def p5_ac3(sc):
            wait(sc, "p5_rexp_mm")
            sc.copy(rexp[:, 0:1], ps[4][:, 0:1])
            inc(sc.drain(), s_ac)
        block.scalar(p5_ac3)
        mark("p5_rexp", s_ac)

        if PH < 6:
            meta = dict(cfg=cfg, c=c)
            _finish(block, None)
            return nc, meta
        assert OQ6 <= 512
        p6 = dict(hw0=cnt[id(s_hw)], hwB0=cnt[id(s_hwB)], sw0=cnt[id(s_sw)],
                  pe0=cnt[id(s_pe)], ve0=cnt[id(s_ve)], ac0=cnt[id(s_ac)],
                  st0=cnt[id(s_st)])

        def p6_usem(ui):
            if ui % 2 == 0:
                return s_hw, p6["hw0"] + (ui // 2 + 1) * 16
            return s_hwB, p6["hwB0"] + (ui // 2 + 1) * 16

        def p6_sync(sync):
            sync.wait_ge(s_ve, p4_ve_end)
            sync.wait_ge(s_ve, p5["ve0"] + NDC)
            for oq in range(c.OQN):
                for tc in range(NFT):
                    ui = oq * NFT + tc
                    if ui >= 2:
                        sync.wait_ge(s_ve, p6["ve0"] + ui - 1)
                    inc(sync.dma_start(
                        out=xres[:, (ui % 2) * OQ6:(ui % 2) * OQ6 + OQ6],
                        in_=xaft_h[tc * 128:(tc + 1) * 128,
                                   oq * OQ6:(oq + 1) * OQ6]),
                        s_hw if ui % 2 == 0 else s_hwB, 16)
                    if ui >= 1:
                        po, ptc = divmod(ui - 1, NFT)
                        sync.wait_ge(s_ve, p6["ve0"] + ui)
                        inc(sync.dma_start(
                            out=out_t[ptc * 128:(ptc + 1) * 128,
                                      po * OQ6:(po + 1) * OQ6],
                            in_=evf[:, ((ui - 1) % 2) * OQ6:((ui - 1) % 2) * OQ6 + OQ6]),
                            s_st, 16)
            ui = c.OQN * NFT
            po, ptc = divmod(ui - 1, NFT)
            sync.wait_ge(s_ve, p6["ve0"] + ui)
            inc(sync.dma_start(
                out=out_t[ptc * 128:(ptc + 1) * 128, po * OQ6:(po + 1) * OQ6],
                in_=evf[:, ((ui - 1) % 2) * OQ6:((ui - 1) % 2) * OQ6 + OQ6]), s_st, 16)
            sync.wait_ge(s_st, cnt[id(s_st)])
        block.sync(p6_sync)

        def p6_gp(g):
            for oq in range(1, c.OQN):
                if oq == 1:
                    g.wait_ge(s_ve, p5["ve0"] + NDC)
                else:
                    g.wait_ge(s_pe, p6["pe0"] + NFT + (oq - 1) * NFT)
                emit_w(g, oq)
        block.gpsimd(p6_gp)

        def p6_pe(t):
            for tc in range(NFT):
                t.wait_ge(s_ac, p6["ac0"] + 2 * tc + 1)
                tr = t.transpose(ps_tr[:, 0:128],
                                 wkb[:, tc * 128:(tc + 1) * 128], idn[:, :])
                inc(tr, s_pe)
            for oq in range(c.OQN):
                t.wait_ge(*m_w[oq])
                if oq == 0:
                    t.wait_ge(s_g3, 32)
                for tc in range(NFT):
                    ui = oq * NFT + tc
                    if ui < 4:
                        t.wait_ge(s_ac, p6["ac0"] + 2 * NFT - 1)
                    if ui >= 4:
                        t.wait_ge(s_ve, p6["ve0"] + ui - 3)
                    pb = ps[ui % 4]
                    t.wait_ge(s_ac, p6["ac0"] + 2 * tc + 2)
                    t.matmul(pb[:, 0:OQ6], onebr[0:1, :],
                             attsb[0:1, oq * OQ6:(oq + 1) * OQ6],
                             start=True, stop=False)
                    for dc in range(NDC):
                        t.matmul(pb[:, 0:OQ6],
                                 big[:, dc * QS + tc * 128:dc * QS + (tc + 1) * 128],
                                 w_rhs(oq, dc, 0, OQ6),
                                 start=False, stop=False)
                    mm = t.matmul(pb[:, 0:OQ6],
                                  tts[:, tc * 128:(tc + 1) * 128],
                                  attsb[:, O + oq * OQ6:O + (oq + 1) * OQ6],
                                  start=False, stop=True)
                    inc(mm, s_pe)
        block.tensor(p6_pe)

        def p6_ac(sc):
            sc.wait_ge(s_pe, p45_pe_end)
            for tc in range(NFT):
                psd = ps_sc0.ap() if tc < 4 else ps_sc1.ap()
                col = (tc % 4) * 128
                sc.wait_ge(s_pe, ttm[tc])
                sc.copy(wkb[:, tc * 128:(tc + 1) * 128], psd[:, col:col + ER])
                inc(sc.drain(), s_ac)
                sc.wait_ge(s_pe, p6["pe0"] + tc + 1)
                sc.activation(tts[:, tc * 128:(tc + 1) * 128], ps_tr[:, 0:128],
                              AF.Identity, bias=0.0, scale=rexp[:, 0:1])
                inc(sc.drain(), s_ac)
        block.scalar(p6_ac)

        def p6_ve(v):
            for oq in range(c.OQN):
                for tc in range(NFT):
                    ui = oq * NFT + tc
                    v.wait_ge(s_pe, p6["pe0"] + NFT + ui + 1)
                    v.wait_ge(*p6_usem(ui))
                    if ui >= 2:
                        v.wait_ge(s_st, p6["st0"] + ui * 16)
                    pp = ui % 2
                    v.tensor_tensor(evf[:, pp * OQ6:pp * OQ6 + OQ6],
                                    ps[ui % 4][:, 0:OQ6],
                                    xres[:, pp * OQ6:pp * OQ6 + OQ6],
                                    ALU.add)
                    inc(v.drain(), s_ve)
        block.vector(p6_ve)

        meta = dict(cfg=cfg, c=c)
    return nc, meta



def host_slices(cfg, inputs, core):
    c = C(cfg)
    b, p, q = core // 4, core % 2, core % 4
    s_loc = q // 2
    x = np.ascontiguousarray(inputs["L0_attn_hidden_states"][b])
    wq = inputs["L0_attn_wq"]
    wkv = inputs["L0_attn_wkv"]
    wo = inputs["L0_attn_wo"]
    W = inputs["L0_ffn_W"]
    bias = inputs["L0_ffn_b"]
    A = inputs["L0_ffn_A"]
    B = inputs["L0_ffn_B"]
    rW = inputs["L0_ffn_router_W"]
    rb = inputs["L0_ffn_router_b"]
    S, D, HL, DN, DV, E, R, O = c.S, c.D, c.HL, c.DN, c.DV, c.E, c.R, c.O
    NH = 2 * HL
    DQG = wq.shape[1] // NH
    myheads = list(range(p * HL, (p + 1) * HL))
    own = [p * c.QS, 2 * c.QS + p * c.QS]
    ffn0 = q * c.QS
    strips = [ffn0, own[0] if own[1] == ffn0 else own[1]]

    xT = np.ascontiguousarray(x.T)
    x_strips = np.ascontiguousarray(
        np.concatenate([x[s0:s0 + c.QS] for s0 in strips], axis=0))
    wq_l = np.ascontiguousarray(
        np.concatenate([wq[:, h * DQG:h * DQG + DN] for h in myheads], axis=1))
    wk_l = np.ascontiguousarray(
        np.concatenate([wkv[:, h * DN:(h + 1) * DN] for h in myheads], axis=1))
    vbase = NH * DN
    wv_l = np.ascontiguousarray(
        np.concatenate([wkv[:, vbase + h * DV:vbase + (h + 1) * DV] for h in myheads], axis=1))
    wo_full_a = np.ascontiguousarray(wo)
    WT = np.ascontiguousarray(W.T)
    bT = np.ascontiguousarray(bias[None, :])
    AT = np.ascontiguousarray(A.transpose(2, 0, 1).reshape(D, E * R))
    BT = np.ascontiguousarray(B.transpose(0, 2, 1).reshape(E * R, O))
    rwH = np.ascontiguousarray(
        rW.reshape(E, c.NDC, 128).transpose(2, 0, 1).reshape(128, E * c.NDC))
    rbT = np.ascontiguousarray(rb[None, :])
    gtok = np.concatenate([np.arange(s0, s0 + c.QS) for s0 in strips])
    wm = ((gtok >= c.W0) & (gtok < c.W1)).astype(np.float32)[None, :]
    wmask = np.ascontiguousarray(np.tile(wm, (128, 1)))
    salt = np.random.randint(1, 2**31 - 1, size=4, dtype=np.int32)
    prm = np.array([[p,
                     strips[0] // c.QS, strips[1] // c.QS,
                     0,
                     salt[0], salt[1], salt[2], salt[3],
                     p, 1 - p, 0, 0]], dtype=np.int32)
    return dict(xT=xT, x_strips=x_strips, wq_l=wq_l, wk_l=wk_l, wv_l=wv_l,
                wo_full=wo_full_a, WTt=WT, bTt=bT, ATt=AT, BTt=BT,
                rwHt=rwH, rbTt=rbT, wmaskt=wmask, prm=prm)


_SALT_SHARED = None


def kernel(**inputs):
    cfg = full_cfg()
    nc, meta = build_program(cfg)
    in_maps = []
    for corex in range(8):
        in_maps.append(host_slices(cfg, inputs, corex))
    for k in range(4):
        in_maps[2 * k + 1]["prm"][0, 4:8] = in_maps[2 * k]["prm"][0, 4:8]
    res = run_bass_kernel_spmd(nc, in_maps, core_ids=list(range(8)))
    c = C(cfg)
    out = np.zeros((2, cfg["S"], cfg["O"]), dtype=np.float32)
    for corex in range(8):
        b, q = corex // 4, corex % 4
        out[b, q * c.QS:(q + 1) * c.QS, :] = np.asarray(res.results[corex]["out"])
    return out


# version 15
# speedup vs baseline: 1.2753x; 1.2753x over previous
import contextlib

import numpy as np
import ml_dtypes

import concourse.bass as bass
import concourse.mybir as mybir
from concourse.bass_utils import run_bass_kernel_spmd

F32 = mybir.dt.float32
BF16 = mybir.dt.bfloat16
I32 = mybir.dt.int32
AF = mybir.ActivationFunctionType
ALU = mybir.AluOpType
AX = mybir.AxisListType


def full_cfg():
    return dict(
        S=4096, D=4096, HL=16, DN=128, DV=128, E=8, R=16, O=4096,
        W0=611, W1=4095,
        SCALE=192.0 ** -0.5, SHIFT=12.0, EPS=1e-5, LORA_SCALING=2.0,
        P1GW=128,
        P2GW=256,
        OQN=8,
        DEBUG_TAPS=(),
    )


def small_cfg():
    return dict(
        S=512, D=512, HL=2, DN=128, DV=128, E=8, R=16, O=512,
        W0=77, W1=511,
        SCALE=192.0 ** -0.5, SHIFT=12.0, EPS=1e-5, LORA_SCALING=2.0,
        P1GW=128, P2GW=256, OQN=2,
        DEBUG_TAPS=(),
    )


class C:

    def __init__(self, cfg):
        self.__dict__.update(cfg)
        self.cfg = cfg
        self.NDC = self.D // 128
        self.NTC = self.S // 128
        self.QS = self.S // 4
        self.HDV = 2 * self.HL * self.DV
        self.ER = self.E * self.R
        assert self.ER == 128
        self.WIN_N = self.W1 - self.W0
        self.MYTOK = 2 * self.QS
        self.NMT = self.MYTOK // 128
        self.NFT = self.QS // 128
        self.OQ = self.O // self.OQN
        self.NP1 = self.S // self.P1GW
        self.NP2 = self.S // self.P2GW
        self.HLDN = self.HL * self.DN
        self.NCC = self.HLDN // 128


def _finish(block, _):
    return


def build_program(cfg):
    c = C(cfg)
    S, D, HL, DN, DV = c.S, c.D, c.HL, c.DN, c.DV
    E, R, O, ER = c.E, c.R, c.O, c.ER
    NDC, NTC, QS = c.NDC, c.NTC, c.QS
    taps = set(cfg.get("DEBUG_TAPS", ()))
    RL = bool(cfg.get("RACE_LOCAL", False))

    nc = bass.Bass("TRN2", target_bir_lowering=False,
                   detect_race_conditions=cfg.get("RACE_DETECT", False))

    xT = nc.dram_tensor("xT", [D, S], F32, kind="ExternalInput").ap()
    x_strips = nc.dram_tensor("x_strips", [c.MYTOK, D], F32, kind="ExternalInput").ap()
    wq_l = nc.dram_tensor("wq_l", [D, c.HLDN], F32, kind="ExternalInput").ap()
    wk_l = nc.dram_tensor("wk_l", [D, c.HLDN], F32, kind="ExternalInput").ap()
    wv_l = nc.dram_tensor("wv_l", [D, HL * DV], F32, kind="ExternalInput").ap()
    wo_full = nc.dram_tensor("wo_full", [c.HDV, D], F32, kind="ExternalInput").ap()
    WTt = nc.dram_tensor("WTt", [D, O], F32, kind="ExternalInput").ap()
    bTt = nc.dram_tensor("bTt", [1, O], F32, kind="ExternalInput").ap()
    ATt = nc.dram_tensor("ATt", [D, ER], F32, kind="ExternalInput").ap()
    BTt = nc.dram_tensor("BTt", [ER, O], F32, kind="ExternalInput").ap()
    rwHt = nc.dram_tensor("rwHt", [128, E * NDC], F32, kind="ExternalInput").ap()
    rbTt = nc.dram_tensor("rbTt", [1, E], F32, kind="ExternalInput").ap()
    wmaskt = nc.dram_tensor("wmaskt", [128, c.MYTOK], F32, kind="ExternalInput").ap()
    prm = nc.dram_tensor("prm", [1, 12], I32, kind="ExternalInput").ap()

    out_t = nc.dram_tensor("out", [QS, O], F32, kind="ExternalOutput").ap()

    lnT_h = nc.dram_tensor("lnT_h", [D, S], BF16, kind="Internal").ap()
    qT_h = nc.dram_tensor("qT_h", [c.HLDN, S], BF16, kind="Internal").ap()
    kT_h = nc.dram_tensor("kT_h", [c.HLDN, S], BF16, kind="Internal").ap()
    v_h = nc.dram_tensor("v_h", [S, HL * DV], BF16, kind="Internal").ap()
    xaft_h = nc.dram_tensor("xaft_h", [c.MYTOK, D], F32, kind="Internal").ap()
    attnT_loc = nc.dram_tensor("attnT_loc", [c.HL * 128, S], BF16, kind="Internal").ap()
    attnT_l2 = nc.dram_tensor("attnT_l2", [c.HDV, c.MYTOK], BF16, kind="Internal").ap()
    ln2T_h = nc.dram_tensor("ln2T_h", [D, c.MYTOK], BF16, kind="Internal").ap()

    sh_attnT = nc.dram_tensor("sh_attnT", [c.HDV, S], BF16, kind="Internal",
                              addr_space="Shared").ap()
    sh_qpart = nc.dram_tensor("sh_qpart", [2 * 128, NDC], F32, kind="Internal",
                              addr_space="Shared").ap()
    sh_flags = nc.dram_tensor("sh_flags", [2, 16], I32, kind="Internal",
                              addr_space="Shared").ap()

    ident_c = nc.inline_tensor(np.eye(128, dtype=ml_dtypes.bfloat16), "ident_c").ap()
    ones_colb = nc.inline_tensor(np.ones((128, 1), dtype=ml_dtypes.bfloat16), "ones_colb").ap()
    ones_rowb = nc.inline_tensor(np.ones((1, 128), dtype=ml_dtypes.bfloat16), "ones_rowb").ap()
    ones_colf = nc.inline_tensor(np.ones((128, 1), dtype=np.float32), "ones_colf").ap()
    ones_rowf = nc.inline_tensor(np.ones((1, 128), dtype=np.float32), "ones_rowf").ap()
    cm = np.zeros((128, 4 * 512), dtype=ml_dtypes.bfloat16)
    for j in range(4):
        cm[:, j * 512:(j + 1) * 512] = (
            np.arange(128)[:, None] + j * 128 <= np.arange(512)[None, :])
    cmask_c = nc.inline_tensor(cm, "cmask_c").ap()
    rep = np.zeros((E, ER), dtype=ml_dtypes.bfloat16)
    for e in range(E):
        rep[e, e * R:(e + 1) * R] = np.float32(cfg["LORA_SCALING"])
    rep_c = nc.inline_tensor(rep, "rep_c").ap()

    CVALS = [0.0, 1.0, -1.0, float(cfg["SCALE"]), -float(cfg["SHIFT"]),
             float(cfg["EPS"]), -0.5, 1.0 / D, 1.0 / c.WIN_N, -1.0]
    cvals_c = nc.inline_tensor(
        np.tile(np.array([CVALS], dtype=np.float32), (128, 1)), "cvals_c").ap()

    st_ = contextlib.ExitStack()

    def sb(name, shape, dtype):
        return st_.enter_context(nc.sbuf_tensor(name, shape, dtype))

    def mkpsum(name, shape, dtype=F32):
        return st_.enter_context(nc.psum_tensor(name, shape, dtype))

    def mksem(name):
        return st_.enter_context(nc.semaphore(name))

    with st_:
        P1GW, P2GW = c.P1GW, c.P2GW
        big = sb("big", [128, 32768], BF16)
        ar2 = sb("ar2", [128, 16384], BF16)
        arf = sb("arf", [128, NDC * P1GW], F32)
        attsb = sb("attsb", [128, 2 * (c.HDV // 128) * 128], BF16)
        wkb = sb("wkb", [128, S], BF16)
        wkb2 = sb("wkb2", [128, S], BF16)
        vbuf = sb("vbuf", [128, S], BF16)
        pt2 = sb("pt2", [128, 2048], BF16)
        attc = sb("attc", [128, 1024], BF16)
        evb = sb("evb", [128, 2048], BF16)
        EVW = max(c.O // 4, c.OQ)
        evf = sb("evf", [128, 2 * EVW], F32)
        xres = sb("xres", [128, 2 * EVW], F32)
        wk1 = sb("wk1", [128, 512], F32)
        bcv = sb("bcv", [128, 1024], F32)
        st_m = sb("st_m", [1, 512], F32)
        st_sq = sb("st_sq", [1, 512], F32)
        st_var = sb("st_var", [1, 512], F32)
        st_rstd = sb("st_rstd", [1, 512], F32)
        cstv = sb("cstv", [128, len(CVALS)], F32)
        idn = sb("idn", [128, 128], BF16)
        onebc = sb("onebc", [128, 1], BF16)
        onebr = sb("onebr", [1, 128], BF16)
        onefr = sb("onefr", [1, 128], F32)
        onefc = sb("onefc", [128, 1], F32)
        cmv = sb("cmv", [128, 4 * 512], BF16)
        repv = sb("repv", [E, ER], BF16)
        rbv = sb("rbv", [1, E], F32)
        rwv = sb("rwv", [128, E * NDC], F32)
        prmv = sb("prmv", [1, 12], I32)
        flgrd = sb("flgrd", [1, 4], I32)
        wmv = wkb2
        atv = vbuf
        tts = sb("tts", [128, c.NFT * 128], BF16)
        pfb = sb("pfb", [128, 512], BF16)
        facc = sb("facc", [128, 512], F32)
        faccB = sb("faccB", [128, 1024], BF16)
        ln2s = sb("ln2s", [128, 16], F32)
        l2p = sb("l2p", [128, 256], F32)
        qpart = sb("qpart", [128, NDC], F32)
        qpeer = sb("qpeer", [128, NDC], F32)
        r8 = sb("r8", [1, 64], F32)
        r8b = sb("r8b", [8, 8], BF16)
        rexp = sb("rexp", [128, 1], F32)

        for i, v in enumerate(CVALS):
            nc.const_aps.register(F32, float(v), cstv[:, i:i + 1])

        ps_sc0 = mkpsum("ps_sc0", [128, 1024])
        ps_sc1 = mkpsum("ps_sc1", [128, 1024])
        ps4t = mkpsum("ps4t", [128, 512])
        ps = [ps_sc0.ap()[:, 0:512], ps_sc0.ap()[:, 512:1024],
              ps_sc1.ap()[:, 0:512], ps_sc1.ap()[:, 512:1024],
              ps4t.ap()[:, 0:512]]
        ps_sumA = mkpsum("ps_sumA", [1, 512])
        ps_tr = mkpsum("ps_tr", [128, 1024], BF16)
        ps_bc = ps_tr.ap().bitcast(F32)
        ps_tr2 = mkpsum("ps_tr2", [128, 1024], BF16)
        ps_av1 = ps_tr2.ap().bitcast(F32)

        s_hw = mksem("s_hw")
        s_hwB = mksem("s_hwB")
        s_st = mksem("s_st")
        s_st2 = mksem("s_st2")
        s_st2B = mksem("s_st2B")
        s_sw = mksem("s_sw")
        s_ac2 = mksem("s_ac2")
        s_ac2B = mksem("s_ac2B")
        s_pe = mksem("s_pe")
        s_ve = mksem("s_ve")
        s_ac = mksem("s_ac")
        s_gp = mksem("s_gp")
        s_wv0 = mksem("s_wv0")
        s_wv1 = mksem("s_wv1")
        s_g1 = mksem("s_g1")
        s_g2 = mksem("s_g2")
        s_g3 = mksem("s_g3")
        s_po1 = mksem("s_po1")
        s_po2 = mksem("s_po2")

        cnt = {}
        for s in (s_hw, s_hwB, s_st, s_st2, s_st2B, s_sw, s_ac2, s_ac2B,
                  s_pe, s_ve, s_ac, s_gp, s_wv0, s_wv1, s_g1, s_g2, s_g3):
            cnt[id(s)] = 0

        def inc(instr, s, n=1):
            cnt[id(s)] += n
            return instr.then_inc(s, n)

        M = {}

        def mark(name, s):
            M[name] = (s, cnt[id(s)])

        def wait(eng, name, extra=0):
            s, v = M[name]
            if v + extra > 0:
                eng.wait_ge(s, v + extra)

        block = st_.enter_context(nc.Block())
        gpregs = {}

        def p0_sync(sync):
            for dst, src in ((idn, ident_c), (onebc, ones_colb), (onebr, ones_rowb),
                             (onefr, ones_rowf), (onefc, ones_colf),
                             (cmv, cmask_c), (repv, rep_c),
                             (rbv, rbTt), (rwv, rwHt), (cstv, cvals_c),
                             (prmv, prm)):
                inc(sync.dma_start(out=dst[:, :], in_=src[:, :]), s_hw, 16)
        block.sync(p0_sync)
        mark("p0_const", s_hw)


        GW = c.P1GW
        NP1 = c.NP1
        assert 4 * GW <= 512
        arf2 = attsb.bitcast(F32)
        BGR = 2 * NDC * GW

        def p1_arf(par):
            return arf if par == 0 else arf2

        ve_ord = [("cast", 0), ("cast", 1)]
        ac_ord = [("sq", 0), ("sq", 1)]
        pe_ord = [("stat", 0)]
        hw_ord = [("ld", 0), ("ld", 1)]
        st_ord = []
        for g in range(NP1):
            ve_ord += [("vstat", g), ("vstat2", g), ("apply", g)]
            if g + 2 < NP1:
                ve_ord.append(("cast", g + 2))
            ac_ord += [("astat", g), ("bce", g)]
            if g + 2 < NP1:
                ac_ord.append(("sq", g + 2))
            if g + 1 < NP1:
                pe_ord.append(("stat", g + 1))
            pe_ord.append(("bc", g))
            st_ord.append(("st", g))
            if g + 2 < NP1:
                hw_ord.append(("ld", g + 2))

        p1b = dict(hw=cnt[id(s_hw)], st=cnt[id(s_st)], ve=cnt[id(s_ve)],
                   ac=cnt[id(s_ac)], pe=cnt[id(s_pe)])
        p1b["hwB"] = cnt[id(s_hwB)]
        VE_I = {k: p1b["ve"] + i + 1 for i, k in enumerate(ve_ord)}
        AC_I = {k: p1b["ac"] + i + 1 for i, k in enumerate(ac_ord)}
        PE_I = {k: p1b["pe"] + i + 1 for i, k in enumerate(pe_ord)}
        HW_I = {}
        _evn = _odd = 0
        for kind, g in hw_ord:
            if g % 2 == 0:
                _evn += 16
                HW_I[(kind, g)] = (s_hw, p1b["hw"] + _evn)
            else:
                _odd += 16
                HW_I[(kind, g)] = (s_hwB, p1b["hwB"] + _odd)
        ST_I = {k: p1b["st"] + (i + 1) * 16 for i, k in enumerate(st_ord)}

        def p1_sync(sync):
            wait(sync, "p0_const")
            for kind, g in hw_ord + []:
                pass
            def ld(g):
                par = g % 2
                if g >= 2:
                    sync.wait_ge(s_ve, VE_I[("apply", g - 2)])
                inc(sync.dma_start(
                    out=p1_arf(par)[:, 0:NDC * GW]
                        .rearrange("p (dc t) -> p dc t", t=GW),
                    in_=xT[0:D, g * GW:(g + 1) * GW]
                        .rearrange("(dc p) t -> p dc t", p=128)),
                    s_hw if par == 0 else s_hwB, 16)

            def stg(g):
                par = g % 2
                sync.wait_ge(s_ve, VE_I[("apply", g)])
                inc(sync.dma_start(
                    out=lnT_h[0:D, g * GW:(g + 1) * GW]
                        .rearrange("(dc p) t -> p dc t", p=128),
                    in_=ar2[:, par * NDC * GW:(par + 1) * NDC * GW]
                        .rearrange("p (dc t) -> p dc t", t=GW)), s_st, 16)

            ld(0)
            ld(1)
            for g in range(NP1):
                if g + 2 < NP1:
                    ld(g + 2)
                stg(g)
        block.sync(p1_sync)

        def p1_ve(v):
            def cast(g):
                par = g % 2
                v.wait_ge(*HW_I[("ld", g)])
                if g >= 2:
                    v.wait_ge(s_pe, PE_I[("stat", g - 2)])
                v.tensor_copy(big[:, par * BGR:par * BGR + NDC * GW],
                              p1_arf(par)[:, 0:NDC * GW])
                inc(v.drain(), s_ve)

            def vstat(g):
                par = g % 2
                v.wait_ge(s_pe, PE_I[("stat", g)])
                pss = ps_sumA[0:1, 0:512] if par == 0 else ps[4][0:1, 0:512]
                o = par * GW
                v.tensor_scalar_mul(st_m[0:1, o:o + GW], pss[0:1, 0:GW], 1.0 / D)
                v.tensor_scalar_mul(st_sq[0:1, o:o + GW], pss[0:1, GW:2 * GW], 1.0 / D)
                v.drain()
                v.tensor_tensor(st_var[0:1, o:o + GW], st_m[0:1, o:o + GW],
                                st_m[0:1, o:o + GW], ALU.mult)
                v.drain()
                v.tensor_tensor(st_var[0:1, o:o + GW], st_sq[0:1, o:o + GW],
                                st_var[0:1, o:o + GW], ALU.subtract)
                inc(v.drain(), s_ve)

            def vstat2(g):
                par = g % 2
                o = par * GW
                v.wait_ge(s_ac, AC_I[("astat", g)])
                v.tensor_tensor(st_m[0:1, o:o + GW], st_m[0:1, o:o + GW],
                                st_rstd[0:1, o:o + GW], ALU.mult)
                v.drain()
                v.tensor_scalar_mul(st_m[0:1, o:o + GW], st_m[0:1, o:o + GW], -1.0)
                inc(v.drain(), s_ve)

            def apply(g):
                par = g % 2
                o = par * 2 * GW
                v.wait_ge(s_ac, AC_I[("bce", g)])
                if g >= 1:
                    v.wait_ge(s_st, ST_I[("st", g - 1)])
                a3 = p1_arf(par)[:, 0:NDC * GW].rearrange("p (o t) -> p o t", t=GW)
                o3 = ar2[:, par * NDC * GW:(par + 1) * NDC * GW] \
                    .rearrange("p (o t) -> p o t", t=GW)
                rb = wk1[:, o:o + GW].rearrange("p (q t) -> p q t", q=1) \
                    .broadcast_to([128, NDC, GW])
                mb = wk1[:, o + GW:o + 2 * GW].rearrange("p (q t) -> p q t", q=1) \
                    .broadcast_to([128, NDC, GW])
                v.tensor_tensor(a3, a3, rb, ALU.mult)
                v.drain()
                v.tensor_tensor(o3, a3, mb, ALU.add)
                inc(v.drain(), s_ve)

            fns = dict(cast=cast, vstat=vstat, vstat2=vstat2, apply=apply)
            for kind, g in ve_ord:
                fns[kind](g)
        block.vector(p1_ve)

        def p1_ac(sc):
            def sq(g):
                par = g % 2
                sc.wait_ge(s_ve, VE_I[("cast", g)])
                if g >= 2:
                    sc.wait_ge(s_pe, PE_I[("stat", g - 2)])
                sc.activation(big[:, par * BGR + NDC * GW:par * BGR + 2 * NDC * GW],
                              big[:, par * BGR:par * BGR + NDC * GW],
                              AF.Square, bias=0.0, scale=1.0)
                inc(sc.drain(), s_ac)

            def astat(g):
                par = g % 2
                o = par * GW
                sc.wait_ge(s_ve, VE_I[("vstat", g)])
                sc.activation(st_rstd[0:1, o:o + GW], st_var[0:1, o:o + GW],
                              AF.Ln, bias=float(cfg["EPS"]), scale=1.0)
                sc.drain()
                sc.activation(st_rstd[0:1, o:o + GW], st_rstd[0:1, o:o + GW],
                              AF.Exp, bias=0.0, scale=-0.5)
                inc(sc.drain(), s_ac)

            def bce(g):
                par = g % 2
                o = par * 2 * GW
                sc.wait_ge(s_pe, PE_I[("bc", g)])
                if g >= 2:
                    sc.wait_ge(s_ve, VE_I[("apply", g - 2)])
                pa, pb = (ps[0], ps[1]) if par == 0 else (ps[2], ps[3])
                sc.copy(wk1[:, o:o + GW], pa[:, 0:GW])
                sc.copy(wk1[:, o + GW:o + 2 * GW], pb[:, 0:GW])
                inc(sc.drain(), s_ac)

            fns = dict(sq=sq, astat=astat, bce=bce)
            for kind, g in ac_ord:
                fns[kind](g)
        block.scalar(p1_ac)

        def p1_pe(t):
            def stat(g):
                par = g % 2
                t.wait_ge(s_ac, AC_I[("sq", g)])
                if g >= 2:
                    t.wait_ge(s_ve, VE_I[("vstat", g - 2)])
                pss = ps_sumA[0:1, 0:512] if par == 0 else ps[4][0:1, 0:512]
                for dc in range(NDC):
                    t.matmul(pss[0:1, 0:GW], onebc[:, :],
                             big[:, par * BGR + dc * GW:par * BGR + (dc + 1) * GW],
                             start=(dc == 0), stop=(dc == NDC - 1))
                for dc in range(NDC):
                    mm = t.matmul(pss[0:1, GW:2 * GW], onebc[:, :],
                                  big[:, par * BGR + (NDC + dc) * GW:
                                      par * BGR + (NDC + dc + 1) * GW],
                                  start=(dc == 0), stop=(dc == NDC - 1))
                inc(mm, s_pe)

            def bc(g):
                par = g % 2
                o = par * GW
                t.wait_ge(s_ve, VE_I[("vstat2", g)])
                if g >= 2:
                    t.wait_ge(s_ac, AC_I[("bce", g - 2)])
                pa, pb = (ps[0], ps[1]) if par == 0 else (ps[2], ps[3])
                t.matmul(pa[:, 0:GW], onefr[:, :], st_rstd[0:1, o:o + GW],
                         start=True, stop=True)
                mm = t.matmul(pb[:, 0:GW], onefr[:, :], st_m[0:1, o:o + GW],
                              start=True, stop=True)
                inc(mm, s_pe)

            fns = dict(stat=stat, bc=bc)
            for kind, g in pe_ord:
                fns[kind](g)
        block.tensor(p1_pe)

        for g in range(NP1):
            M[f"p1_apply_{g}"] = (s_ve, VE_I[("apply", g)])
            M[f"p1_st_{g}"] = (s_st, ST_I[("st", g)])
            M[f"p1_stat_{g}"] = (s_pe, PE_I[("stat", g)])

        mark("p1_done_hw", s_hw)
        mark("p1_done_ve", s_ve)

        PH = cfg.get("PHASES", 9)
        if PH < 2:
            meta = dict(cfg=cfg, c=c)
            _finish(block, None)
            return nc, meta
        GP = c.P2GW
        NP2 = c.NP2
        NCC = c.NCC
        NCC2 = max(1, NCC // 4)
        NHALF = NCC // NCC2
        VB = HL * DV
        NHV = max(1, (NDC * VB + 16383) // 16384)
        VB2 = VB // NHV
        NVB = (VB2 + 511) // 512
        assert NVB == 1
        NTG = GP // 128
        NU = 2 * NHALF
        USZ = NCC2 * NDC * 128
        VSZ = NDC * VB2
        NQK = NU * NP2 * NCC2
        GQK = NU * NP2
        GTOT = GQK + NHV * NP2

        p1_pe_end = cnt[id(s_pe)]
        p1_st_end = cnt[id(s_st)]
        p2 = dict(sw0=cnt[id(s_sw)], a20=cnt[id(s_ac2)], a2B0=cnt[id(s_ac2B)],
                  pe0=cnt[id(s_pe)], ac0=cnt[id(s_ac)], st0=cnt[id(s_st)])

        def p2_gsem(gg):
            if gg % 2 == 0:
                return s_ac2, p2["a20"] + (gg // 2 + 1) * 16
            return s_ac2B, p2["a2B0"] + (gg // 2 + 1) * 16

        def p2_gld(eng, gg):
            gI = gg % NP2
            sl = gg % 2
            inc(eng.dma_start(
                out=ar2[:, sl * NDC * GP:(sl + 1) * NDC * GP]
                    .rearrange("p (dc t) -> p dc t", t=GP),
                in_=lnT_h[0:D, gI * GP:(gI + 1) * GP]
                    .rearrange("(dc p) t -> p dc t", p=128)),
                p2_gsem(gg)[0], 16)

        p2_wm = {}

        def p2_gp(g):
            for u in range(NU):
                pi, hf = divmod(u, NHALF)
                wsrc = wq_l if pi == 0 else wk_l
                ws = s_wv0 if u % 2 == 0 else s_wv1
                if u < 2:
                    g.wait_ge(s_pe, p1_pe_end)
                else:
                    g.wait_ge(s_pe, p2["pe0"] + (u - 1) * NP2 * NCC2)
                for cc in range(NCC2):
                    ccg = hf * NCC2 + cc
                    inc(g.dma_start(
                        out=big[:, (u % 2) * USZ + cc * NDC * 128:
                                (u % 2) * USZ + (cc + 1) * NDC * 128]
                            .rearrange("p (dc i) -> p dc i", i=128),
                        in_=wsrc[0:D, ccg * 128:(ccg + 1) * 128]
                            .rearrange("(dc p) i -> p dc i", p=128)), ws, 16)
                p2_wm[u] = (ws, cnt[id(ws)])
            for hv in range(NHV):
                ws = s_wv0 if (NU + hv) % 2 == 0 else s_wv1
                if hv == 0:
                    g.wait_ge(s_pe, p2["pe0"] +
                              ((NU - 1) * NP2 * NCC2 if VSZ <= USZ else NQK))
                elif hv == 1:
                    g.wait_ge(s_pe, p2["pe0"] + NQK)
                else:
                    g.wait_ge(s_pe, p2["pe0"] + NQK + (hv - 1) * NP2 * NTG)
                inc(g.dma_start(
                    out=big[:, (hv % 2) * VSZ:(hv % 2) * VSZ + VSZ]
                        .rearrange("p (dc i) -> p dc i", i=VB2),
                    in_=wv_l[0:D, hv * VB2:(hv + 1) * VB2]
                        .rearrange("(dc p) i -> p dc i", p=128)), ws, 16)
                p2_wm[NU + hv] = (ws, cnt[id(ws)])
        block.gpsimd(p2_gp)

        def p2_sync(sync):
            ac_i = p2["ac0"]
            for u in range(NU):
                pi, hf = divmod(u, NHALF)
                dst = qT_h if pi == 0 else kT_h
                for gI in range(NP2):
                    for cc in range(NCC2):
                        ccg = hf * NCC2 + cc
                        ac_i += 1
                        sync.wait_ge(s_ac, ac_i)
                        slot = (u * NP2 * NCC2 + gI * NCC2 + cc) % 8
                        inc(sync.dma_start(
                            out=dst[ccg * 128:(ccg + 1) * 128, gI * GP:(gI + 1) * GP],
                            in_=evb[:, slot * GP:(slot + 1) * GP]), s_st, 16)
            for hv in range(NHV):
                for gI in range(NP2):
                    for tc in range(NTG):
                        vi = (hv * NP2 + gI) * NTG + tc
                        ac_i += 1
                        sync.wait_ge(s_ac, ac_i)
                        inc(sync.dma_start(
                            out=v_h[(gI * NTG + tc) * 128:(gI * NTG + tc + 1) * 128,
                                    hv * VB2:(hv + 1) * VB2],
                            in_=evb[:, (vi % 4) * 512:(vi % 4) * 512 + VB2]),
                            s_st, 16)
        block.sync(p2_sync)

        def p2_pe(t):
            for u in range(NU):
                t.wait_ge(*p2_wm[u])
                for gI in range(NP2):
                    gg = u * NP2 + gI
                    sl = gg % 2
                    t.wait_ge(*p2_gsem(gg))
                    for cc in range(NCC2):
                        slot_idx = u * NP2 * NCC2 + gI * NCC2 + cc
                        if slot_idx >= 4:
                            t.wait_ge(s_ac, p2["ac0"] + slot_idx - 3)
                        for dc in range(NDC):
                            mm = t.matmul(
                                ps[slot_idx % 4][:, 0:GP],
                                big[:, (u % 2) * USZ + (cc * NDC + dc) * 128:
                                    (u % 2) * USZ + (cc * NDC + dc + 1) * 128],
                                ar2[:, sl * NDC * GP + dc * GP:
                                    sl * NDC * GP + (dc + 1) * GP],
                                start=(dc == 0), stop=(dc == NDC - 1))
                        inc(mm, s_pe)
            for hv in range(NHV):
                t.wait_ge(*p2_wm[NU + hv])
                for gI in range(NP2):
                    gg = (NU + hv) * NP2 + gI
                    sl = gg % 2
                    t.wait_ge(*p2_gsem(gg))
                    for tc in range(NTG):
                        vi = (hv * NP2 + gI) * NTG + tc
                        if vi >= 4:
                            t.wait_ge(s_ac, p2["ac0"] + NQK + vi - 3)
                        for dc in range(NDC):
                            mm = t.matmul(
                                ps[vi % 4][:, 0:VB2],
                                ar2[:, sl * NDC * GP + dc * GP + tc * 128:
                                    sl * NDC * GP + dc * GP + (tc + 1) * 128],
                                big[:, (hv % 2) * VSZ + dc * VB2:
                                    (hv % 2) * VSZ + (dc + 1) * VB2],
                                start=(dc == 0), stop=(dc == NDC - 1))
                        inc(mm, s_pe)
        block.tensor(p2_pe)

        def p2_ac(sc):
            sc.wait_ge(s_st, p1_st_end)
            p2_gld(sc, 0)
            if GTOT > 1:
                p2_gld(sc, 1)
            for u in range(NU):
                for gI in range(NP2):
                    gg = u * NP2 + gI
                    for cc in range(NCC2):
                        slot_idx = u * NP2 * NCC2 + gI * NCC2 + cc
                        sc.wait_ge(s_pe, p2["pe0"] + slot_idx + 1)
                        if slot_idx >= 8:
                            sc.wait_ge(s_st, p2["st0"] + (slot_idx - 7) * 16)
                        slot = slot_idx % 8
                        sc.activation(evb[:, slot * GP:(slot + 1) * GP],
                                      ps[slot_idx % 4][:, 0:GP],
                                      AF.Copy, bias=0.0, scale=1.0)
                        inc(sc.drain(), s_ac)
                    if gg + 2 < GTOT:
                        p2_gld(sc, gg + 2)
            for hv in range(NHV):
                for gI in range(NP2):
                    gg = (NU + hv) * NP2 + gI
                    for tc in range(NTG):
                        vi = (hv * NP2 + gI) * NTG + tc
                        sc.wait_ge(s_pe, p2["pe0"] + NQK + vi + 1)
                        if vi < 4:
                            sc.wait_ge(s_st, p2["st0"] + NQK * 16)
                        else:
                            sc.wait_ge(s_st, p2["st0"] + (NQK + vi - 3) * 16)
                        sc.activation(evb[:, (vi % 4) * 512:(vi % 4) * 512 + VB2],
                                      ps[vi % 4][:, 0:VB2],
                                      AF.Copy, bias=0.0, scale=1.0)
                        inc(sc.drain(), s_ac)
                    if gg + 2 < GTOT:
                        p2_gld(sc, gg + 2)
        block.scalar(p2_ac)

        p2_pe_end = cnt[id(s_pe)]
        p2_st_end = cnt[id(s_st)]
        p2_ac_end = cnt[id(s_ac)]
        p2_sw_end = cnt[id(s_sw)]
        p2_hw_end = cnt[id(s_hw)]

        if PH < 3:
            meta = dict(cfg=cfg, c=c)
            _finish(block, None)
            return nc, meta
        NQG = S // 512
        KCs = [4 * (qg + 1) for qg in range(NQG)]
        Ps = [k // 2 for k in KCs]
        per_pe = [p + 3 for p in Ps]
        per_ac = [p + 1 for p in Ps]
        per_ve = [p + 5 for p in Ps]
        PEQ = sum(per_pe)
        ACQ = sum(per_ac)
        VEQ = sum(per_ve)
        C_SCALE = float(cfg["SCALE"])
        C_MSHIFT = -float(cfg["SHIFT"])

        p3 = dict(hw0=cnt[id(s_hw)], hwB0=cnt[id(s_hwB)], pe0=cnt[id(s_pe)],
                  ac0=cnt[id(s_ac)], ve0=cnt[id(s_ve)], sw0=cnt[id(s_sw)])

        def p3_hsem(h):
            if h % 2 == 0:
                return s_hw, p3["hw0"] + (h // 2 + 1) * 48
            return s_hwB, p3["hwB0"] + (h // 2 + 1) * 48

        PTOT = sum(Ps)

        def pair_global(h, qg, pi):
            return h * PTOT + sum(Ps[:qg]) + pi

        def pair_from_global(idx):
            h, rem = divmod(idx, PTOT)
            qg = 0
            while rem >= Ps[qg]:
                rem -= Ps[qg]
                qg += 1
            return h, qg, rem

        def prev_gi(h, qg, back=1):
            gi = h * NQG + qg - back
            return divmod(gi, NQG) if gi >= 0 else None

        secs = [(h, qg) for h in range(HL) for qg in range(NQG)]
        pe_seq, ac_seq, ve_seq = [], [], []
        for i, (h, qg) in enumerate(secs):
            P = Ps[qg]
            prev = secs[i - 1] if i >= 1 else None
            pe_seq += [("sc", h, qg, 0), ("sc", h, qg, 1)]
            if prev:
                pe_seq.append(("sum",) + prev)
            for pi in range(P):
                pe_seq.append(("av", h, qg, pi))
                if pi == 0 and prev:
                    pe_seq.append(("bc",) + prev)
                if pi + 2 < P:
                    pe_seq.append(("sc", h, qg, pi + 2))
            ac_seq.append(("exp", h, qg, 0))
            if prev:
                ac_seq.append(("bce",) + prev)
            for pi in range(1, P):
                ac_seq.append(("exp", h, qg, pi))
            loc = []
            for pi in range(P - 2):
                loc.append(("fold", h, qg, pi))
            for pi in range(P - 2, P):
                loc += [("mask", h, qg, pi), ("fold", h, qg, pi)]
            loc.append(("cast", h, qg))
            ve_seq += [loc[0]]
            if prev:
                ve_seq += [("recip",) + prev, ("mul",) + prev]
            ve_seq += loc[1:]
        lasts = secs[-1]
        pe_seq += [("sum",) + lasts, ("bc",) + lasts]
        ac_seq.append(("bce",) + lasts)
        ve_seq += [("recip",) + lasts, ("mul",) + lasts]

        def _counts(seq, base):
            d = {}
            cc = base
            for op in seq:
                if op[0] == "av":
                    if op[3] == Ps[op[2]] - 1:
                        cc += 1
                        d[("av", op[1], op[2])] = cc
                else:
                    cc += 1
                    d[op] = cc
            return d

        PEC = _counts(pe_seq, p3["pe0"])
        ACC = _counts(ac_seq, p3["ac0"])
        VEC = _counts(ve_seq, p3["ve0"])

        def p3_pe_pair(h, qg, pi):
            return PEC[("sc", h, qg, pi)]

        def p3_pe_av(h, qg):
            return PEC[("av", h, qg)]

        def p3_pe_sum(h, qg):
            return PEC[("sum", h, qg)]

        def p3_pe_bc(h, qg):
            return PEC[("bc", h, qg)]

        def p3_ac_exp(h, qg, pi):
            return ACC[("exp", h, qg, pi)]

        def p3_ac_bce(h, qg):
            return ACC[("bce", h, qg)]

        def p3_ve_fold(h, qg, pi):
            return VEC[("fold", h, qg, pi)]

        def p3_ve_mask(h, qg, pi):
            return VEC[("mask", h, qg, pi)]

        def p3_ve_cast(h, qg):
            return VEC[("cast", h, qg)]

        def p3_ve_recip(h, qg):
            return VEC[("recip", h, qg)]

        def p3_ve_mul(h, qg):
            return VEC[("mul", h, qg)]

        def p3_q(h):
            return wkb[:, 0:S] if h % 2 == 0 else ar2[:, 0:S]

        def p3_k(h):
            return wkb2[:, 0:S] if h % 2 == 0 else ar2[:, S:2 * S]

        def p3_v(h):
            return vbuf[:, 0:S] if h % 2 == 0 else ar2[:, 2 * S:3 * S]

        def p3_sync(sync):
            def ld(h):
                if h == 0:
                    sync.wait_ge(s_st, p2_st_end)
                elif h >= 2:
                    sync.wait_ge(s_pe, p3["pe0"] + (h - 1) * PEQ)
                hs = s_hw if h % 2 == 0 else s_hwB
                inc(sync.dma_start(out=p3_q(h),
                                   in_=qT_h[h * 128:(h + 1) * 128, 0:S]), hs, 16)
                inc(sync.dma_start(out=p3_k(h),
                                   in_=kT_h[h * 128:(h + 1) * 128, 0:S]), hs, 16)
                inc(sync.dma_start(
                    out=p3_v(h).rearrange("p (kc i) -> p kc i", i=DV),
                    in_=v_h[0:S, h * DV:(h + 1) * DV]
                        .rearrange("(kc p) i -> p kc i", p=128)), hs, 16)
            ld(0)
            for h in range(1, HL):
                ld(h)
        block.sync(p3_sync)

        def p3_pe(t):
            curh = [-1]

            def headstart(h):
                if h != curh[0]:
                    t.wait_ge(*p3_hsem(h))
                    if h == 0:
                        t.wait_ge(s_ac, p2_ac_end)
                    curh[0] = h

            for op in pe_seq:
                kind, h, qg = op[0], op[1], op[2]
                gi = h * NQG + qg
                if kind == "sc":
                    headstart(h)
                    pi = op[3]
                    gidx = pair_global(h, qg, pi)
                    if gidx >= 2:
                        t.wait_ge(s_ac, p3_ac_exp(*pair_from_global(gidx - 2)))
                    sc = ps_sc0 if gidx % 2 == 0 else ps_sc1
                    for half in range(2):
                        kc = 2 * pi + half
                        mm = t.matmul(sc.ap()[:, half * 512:(half + 1) * 512],
                                      p3_k(h)[:, kc * 128:(kc + 1) * 128],
                                      p3_q(h)[:, qg * 512:(qg + 1) * 512],
                                      start=True, stop=True)
                    inc(mm, s_pe)
                elif kind == "av":
                    pi = op[3]
                    P = Ps[qg]
                    avp = ps[4] if gi % 2 == 0 else ps_av1[:, 0:512]
                    t.wait_ge(s_ac, p3_ac_exp(h, qg, pi))
                    if pi >= P - 2:
                        t.wait_ge(s_ve, p3_ve_mask(h, qg, pi))
                    if pi == 0:
                        pv2 = prev_gi(h, qg, 2)
                        if pv2 is not None:
                            t.wait_ge(s_ve, p3_ve_mul(*pv2))
                    gi2 = pair_global(h, qg, pi) % 2
                    for half in range(2):
                        kc = 2 * pi + half
                        pslice = pt2[:, gi2 * 1024 + half * 512:
                                     gi2 * 1024 + (half + 1) * 512]
                        last = (kc == KCs[qg] - 1)
                        mm = t.matmul(avp, p3_v(h)[:, kc * 128:(kc + 1) * 128],
                                      pslice, start=(kc == 0), stop=last)
                        if last:
                            inc(mm, s_pe)
                elif kind == "sum":
                    t.wait_ge(s_ve, p3_ve_cast(h, qg))
                    pv1 = prev_gi(h, qg, 1)
                    if pv1 is not None:
                        t.wait_ge(s_ve, p3_ve_recip(*pv1))
                    mm = t.matmul(ps_sumA[0:1, 0:512], onebc[:, :],
                                  faccB[:, (gi % 2) * 512:(gi % 2 + 1) * 512],
                                  start=True, stop=True)
                    inc(mm, s_pe)
                else:
                    t.wait_ge(s_ve, p3_ve_recip(h, qg))
                    pv1 = prev_gi(h, qg, 1)
                    if pv1 is not None:
                        t.wait_ge(s_ac, p3_ac_bce(*pv1))
                    mm = t.matmul(ps_bc[:, 0:512], onefr[:, :],
                                  st_m[0:1, 0:512], start=True, stop=True)
                    inc(mm, s_pe)
        block.tensor(p3_pe)

        def p3_ac(sc):
            for op in ac_seq:
                kind, h, qg = op[0], op[1], op[2]
                gi = h * NQG + qg
                if kind == "exp":
                    pi = op[3]
                    sc.wait_ge(s_pe, p3_pe_pair(h, qg, pi))
                    gidx = pair_global(h, qg, pi)
                    if gidx >= 2:
                        sc.wait_ge(s_ve, p3_ve_fold(*pair_from_global(gidx - 2)))
                    scp = ps_sc0 if gidx % 2 == 0 else ps_sc1
                    sc.activation(pt2[:, (gidx % 2) * 1024:(gidx % 2 + 1) * 1024],
                                  scp.ap()[:, :], AF.Exp,
                                  bias=C_MSHIFT, scale=C_SCALE)
                    inc(sc.drain(), s_ac)
                else:
                    sc.wait_ge(s_pe, p3_pe_bc(h, qg))
                    pv1 = prev_gi(h, qg, 1)
                    if pv1 is not None:
                        sc.wait_ge(s_ve, p3_ve_mul(*pv1))
                    sc.copy(bcv[:, (gi % 2) * 512:(gi % 2 + 1) * 512],
                            ps_bc[:, 0:512])
                    inc(sc.drain(), s_ac)
        block.scalar(p3_ac)

        def p3_ve(v):
            for op in ve_seq:
                kind, h, qg = op[0], op[1], op[2]
                gi = h * NQG + qg
                P = Ps[qg]
                if kind == "fold":
                    pi = op[3]
                    if pi < P - 2:
                        v.wait_ge(s_ac, p3_ac_exp(h, qg, pi))
                    gi2 = pair_global(h, qg, pi) % 2
                    h0 = pt2[:, gi2 * 1024:gi2 * 1024 + 512]
                    h1 = pt2[:, gi2 * 1024 + 512:(gi2 + 1) * 1024]
                    if pi == 0:
                        v.tensor_tensor(facc[:, 0:512], h0, h1, ALU.add)
                        inc(v.drain(), s_ve)
                    else:
                        v.tensor_tensor(pfb[:, 0:512], h0, h1, ALU.add)
                        v.drain()
                        v.tensor_tensor(facc[:, 0:512], facc[:, 0:512],
                                        pfb[:, 0:512], ALU.add)
                        inc(v.drain(), s_ve)
                elif kind == "mask":
                    pi = op[3]
                    v.wait_ge(s_ac, p3_ac_exp(h, qg, pi))
                    gi2 = pair_global(h, qg, pi) % 2
                    j0 = 2 * (pi - (P - 2))
                    v.tensor_tensor(pt2[:, gi2 * 1024:(gi2 + 1) * 1024],
                                    pt2[:, gi2 * 1024:(gi2 + 1) * 1024],
                                    cmv[:, j0 * 512:(j0 + 2) * 512], ALU.mult)
                    inc(v.drain(), s_ve)
                elif kind == "cast":
                    pv2 = prev_gi(h, qg, 2)
                    if pv2 is not None:
                        v.wait_ge(s_pe, p3_pe_sum(*pv2))
                    v.tensor_copy(faccB[:, (gi % 2) * 512:(gi % 2 + 1) * 512],
                                  facc[:, 0:512])
                    inc(v.drain(), s_ve)
                elif kind == "recip":
                    v.wait_ge(s_pe, p3_pe_sum(h, qg))
                    v.reciprocal(st_m[0:1, 0:512], ps_sumA[0:1, 0:512])
                    inc(v.drain(), s_ve)
                else:
                    avp = ps[4] if gi % 2 == 0 else ps_av1[:, 0:512]
                    v.wait_ge(s_pe, p3_pe_av(h, qg))
                    v.wait_ge(s_ac, p3_ac_bce(h, qg))
                    if gi >= 1:
                        v.wait_ge(s_sw, p3["sw0"] + gi * 16)
                    v.tensor_tensor(attc[:, (qg % 2) * 512:(qg % 2 + 1) * 512],
                                    avp, bcv[:, (gi % 2) * 512:(gi % 2 + 1) * 512],
                                    ALU.mult)
                    inc(v.drain(), s_ve)
        block.vector(p3_ve)

        HLR = HL * 128

        def p3_gp(g):
            g.wait_ge(s_sw, p2_sw_end)
            g.wait_ge(s_hw, M["p0_const"][1])
            with g.register("pregh") as preg:
                g.reg_load(preg, prmv[0:1, 8:9])
                for h in range(HL):
                    for qg in range(NQG):
                        g.wait_ge(s_ve, p3_ve_mul(h, qg))
                        inc(g.dma_start(
                            out=attnT_loc[h * 128:(h + 1) * 128,
                                          qg * 512:(qg + 1) * 512],
                            in_=attc[:, (qg % 2) * 512:(qg % 2 + 1) * 512]), s_sw, 16)
                    g.wait_ge(s_sw, cnt[id(s_sw)])
                    with g.If_eq(preg, 0):
                        g.dma_start(
                            out=sh_attnT[h * 128:(h + 1) * 128, 0:S],
                            in_=attnT_loc[h * 128:(h + 1) * 128, 0:S]).then_inc(s_sw, 16)
                    with g.Else():
                        g.dma_start(
                            out=sh_attnT[HLR + h * 128:HLR + (h + 1) * 128, 0:S],
                            in_=attnT_loc[h * 128:(h + 1) * 128, 0:S]).then_inc(s_sw, 16)
                    cnt[id(s_sw)] += 16
        block.gpsimd(p3_gp)

        p3_pe_end = cnt[id(s_pe)]
        p3_sw_end = cnt[id(s_sw)]
        p3_hw_end = cnt[id(s_hw)]
        p3_ve_end = cnt[id(s_ve)]
        p3_ac_end = cnt[id(s_ac)]

        if PH < 3.5:
            meta = dict(cfg=cfg, c=c)
            _finish(block, None)
            return nc, meta
        def p35_gp(g):
            g.wait_ge(s_sw, p3_sw_end)
            with g.register("preg") as preg, g.register("qreg") as qreg:
                g.reg_load(preg, prmv[0:1, 8:9])
                g.reg_load(qreg, prmv[0:1, 1:2])
                with g.If_eq(preg, 0):
                    g.dma_start(out=sh_flags[0:1, 0:2],
                                in_=prmv[0:1, 4:6]).then_inc(s_sw, 16)
                with g.Else():
                    g.dma_start(out=sh_flags[1:2, 0:2],
                                in_=prmv[0:1, 4:6]).then_inc(s_sw, 16)
                cnt[id(s_sw)] += 16
                g.wait_ge(s_sw, cnt[id(s_sw)])
                pr0, pr1 = (0, 1) if RL else (1, 0)
                with g.register("cond") as cond, g.register("tgt") as tgt, \
                     g.register("fa") as fa, g.register("fb") as fb, \
                     g.register("ea") as ea, g.register("eb") as eb:
                    g.reg_load(ea, prmv[0:1, 4:5])
                    g.reg_load(eb, prmv[0:1, 5:6])
                    g.reg_mov(cond, 1)
                    g.reg_mov(tgt, 0)
                    with g.While(cond):
                        g.reg_add(tgt, tgt, 16)
                        with g.If_eq(preg, 0):
                            g.dma_start(out=flgrd[0:1, 0:2],
                                        in_=sh_flags[pr0:pr0 + 1, 0:2]
                                        ).then_inc(s_po1, 16)
                        with g.Else():
                            g.dma_start(out=flgrd[0:1, 0:2],
                                        in_=sh_flags[pr1:pr1 + 1, 0:2]
                                        ).then_inc(s_po1, 16)
                        g.wait_ge(s_po1, tgt)
                        g.reg_load(fa, flgrd[0:1, 0:1])
                        g.reg_load(fb, flgrd[0:1, 1:2])
                        g.reg_alu(fa, fa, ea, ALU.is_equal)
                        g.reg_alu(fb, fb, eb, ALU.is_equal)
                        g.reg_alu(cond, fa, fb, ALU.bitwise_and)
                        g.reg_alu(cond, cond, 1, ALU.bitwise_xor)
                def pull(which, k):
                    d0 = which * QS
                    if not RL:
                        g.dma_start(out=attnT_l2[:, d0:d0 + QS],
                                    in_=sh_attnT[0:c.HDV, k * QS:(k + 1) * QS]
                                    ).then_inc(s_sw, 16)
                        return
                    HDV2 = c.HDV // 2
                    with g.If_eq(preg, 0):
                        g.dma_start(out=attnT_l2[0:HDV2, d0:d0 + QS],
                                    in_=sh_attnT[0:HDV2, k * QS:(k + 1) * QS]
                                    ).then_inc(s_sw, 16)
                        g.dma_start(out=attnT_l2[HDV2:c.HDV, d0:d0 + QS],
                                    in_=sh_attnT[0:HDV2, k * QS:(k + 1) * QS]
                                    ).then_inc(s_sw, 16)
                    with g.Else():
                        g.dma_start(out=attnT_l2[0:HDV2, d0:d0 + QS],
                                    in_=sh_attnT[HDV2:c.HDV, k * QS:(k + 1) * QS]
                                    ).then_inc(s_sw, 16)
                        g.dma_start(out=attnT_l2[HDV2:c.HDV, d0:d0 + QS],
                                    in_=sh_attnT[HDV2:c.HDV, k * QS:(k + 1) * QS]
                                    ).then_inc(s_sw, 16)

                for which in range(2):
                    with g.If_eq(qreg, 0):
                        pull(which, (0 + 2 * which) % 4)
                    with g.Else():
                        with g.If_eq(qreg, 1):
                            pull(which, (1 + 2 * which) % 4)
                        with g.Else():
                            with g.If_eq(qreg, 2):
                                pull(which, (2 + 2 * which) % 4)
                            with g.Else():
                                pull(which, (3 + 2 * which) % 4)
                    cnt[id(s_sw)] += 32 if RL else 16
                    g.wait_ge(s_sw, cnt[id(s_sw)])
                    g.sem_inc(s_gp, 1)
                    cnt[id(s_gp)] += 1
        block.gpsimd(p35_gp)
        mark("pair1_ready", s_gp)
        M["pull1_done"] = (s_gp, cnt[id(s_gp)] - 1)

        OQN4 = 4
        OQ4 = O // OQN4
        NHC = c.HDV // 128
        NMT = c.NMT
        NB4 = OQ4 // 512 if OQ4 >= 512 else 1
        W4 = min(OQ4, 512)

        p4 = dict(hw0=cnt[id(s_hw)], hwB0=cnt[id(s_hwB)], sw0=cnt[id(s_sw)],
                  pe0=cnt[id(s_pe)], ve0=cnt[id(s_ve)], st0=cnt[id(s_st)])

        def p4_usem(ui, k):
            if ui % 2 == 0:
                return s_hw, p4["hw0"] + (ui // 2) * 32 + k * 16
            return s_hwB, p4["hwB0"] + (ui // 2) * 32 + k * 16

        def p4_gp(g):
            for oq in range(OQN4):
                if oq >= 1:
                    g.wait_ge(s_pe, p4["pe0"] + oq * NMT)
                inc(g.dma_start(
                    out=big[:, 0:NHC * OQ4].rearrange("p (hc o) -> p hc o", o=OQ4),
                    in_=wo_full[0:c.HDV, oq * OQ4:(oq + 1) * OQ4]
                        .rearrange("(hc p) o -> p hc o", p=128)), s_sw, 16)
        block.gpsimd(p4_gp)

        def p4_sync(sync):
            st_i = p4["st0"]
            for oq in range(OQN4):
                for tc in range(NMT):
                    ui = oq * NMT + tc
                    if oq == 0:
                        wait(sync, "pull1_done" if tc < NMT // 2 else "pair1_ready")
                    if ui >= 2:
                        sync.wait_ge(s_pe, p4["pe0"] + ui - 1)
                    us = s_hw if ui % 2 == 0 else s_hwB
                    inc(sync.dma_start(
                        out=attsb[:, (ui % 2) * NHC * 128:(ui % 2 + 1) * NHC * 128]
                            .rearrange("p (hc i) -> p hc i", i=128),
                        in_=attnT_l2[0:c.HDV, tc * 128:(tc + 1) * 128]
                            .rearrange("(hc p) i -> p hc i", p=128)), us, 16)
                    if ui >= 2:
                        sync.wait_ge(s_ve, p4["ve0"] + ui - 1)
                    inc(sync.dma_start(
                        out=xres[:, (ui % 2) * OQ4:(ui % 2 + 1) * OQ4],
                        in_=x_strips[tc * 128:(tc + 1) * 128,
                                     oq * OQ4:(oq + 1) * OQ4]), us, 16)
                    if ui >= 1:
                        po, ptc = divmod(ui - 1, NMT)
                        sync.wait_ge(s_ve, p4["ve0"] + ui)
                        inc(sync.dma_start(
                            out=xaft_h[ptc * 128:(ptc + 1) * 128,
                                       po * OQ4:(po + 1) * OQ4],
                            in_=evf[:, ((ui - 1) % 2) * OQ4:((ui - 1) % 2 + 1) * OQ4]),
                            s_st, 16)
            ui = OQN4 * NMT
            po, ptc = divmod(ui - 1, NMT)
            sync.wait_ge(s_ve, p4["ve0"] + ui)
            inc(sync.dma_start(
                out=xaft_h[ptc * 128:(ptc + 1) * 128, po * OQ4:(po + 1) * OQ4],
                in_=evf[:, ((ui - 1) % 2) * OQ4:((ui - 1) % 2 + 1) * OQ4]), s_st, 16)
        block.sync(p4_sync)

        def p4_pe(t):
            t.wait_ge(s_ac, p3_ac_end)
            t.wait_ge(s_ve, p3_ve_end)
            for oq in range(OQN4):
                t.wait_ge(s_sw, p4["sw0"] + (oq + 1) * 16)
                for tc in range(NMT):
                    ui = oq * NMT + tc
                    t.wait_ge(*p4_usem(ui, 2))
                    if ui >= 2:
                        t.wait_ge(s_ve, p4["ve0"] + ui - 1)
                    pp = ui % 2
                    for hc in range(NHC):
                        for nb in range(NB4):
                            w0 = nb * 512
                            w1 = min(OQ4, w0 + 512)
                            mm = t.matmul(
                                ps[2 * pp + nb][:, 0:w1 - w0],
                                attsb[:, (ui % 2) * NHC * 128 + hc * 128:
                                      (ui % 2) * NHC * 128 + (hc + 1) * 128],
                                big[:, hc * OQ4 + w0:hc * OQ4 + w1],
                                start=(hc == 0), stop=(hc == NHC - 1))
                    inc(mm, s_pe)
        block.tensor(p4_pe)

        def p4_ve(v):
            for oq in range(OQN4):
                for tc in range(NMT):
                    ui = oq * NMT + tc
                    v.wait_ge(s_pe, p4["pe0"] + ui + 1)
                    v.wait_ge(*p4_usem(ui, 2))
                    if ui >= 2:
                        v.wait_ge(s_st, p4["st0"] + ui * 16)
                    pp = ui % 2
                    for nb in range(NB4):
                        w0 = nb * 512
                        w1 = min(OQ4, w0 + 512)
                        v.tensor_tensor(evf[:, pp * OQ4 + w0:pp * OQ4 + w1],
                                        ps[2 * pp + nb][:, 0:w1 - w0],
                                        xres[:, pp * OQ4 + w0:pp * OQ4 + w1],
                                        ALU.add)
                    inc(v.drain(), s_ve)
                    ev = evf[:, pp * OQ4:pp * OQ4 + OQ4]
                    v.reduce_sum(l2p[:, 4 * tc + oq:4 * tc + oq + 1], ev, AX.X)
                    v.tensor_tensor(bcv[:, 0:OQ4], ev, ev, ALU.mult)
                    v.drain()
                    v.reduce_sum(l2p[:, 64 + 4 * tc + oq:64 + 4 * tc + oq + 1],
                                 bcv[:, 0:OQ4], AX.X)
        block.vector(p4_ve)

        p4_pe_end = cnt[id(s_pe)]
        p4_st_end = cnt[id(s_st)]
        p4_ve_end = cnt[id(s_ve)]
        p4_hw_end = cnt[id(s_hw)]
        p4_sw_end = cnt[id(s_sw)]

        if PH < 4.5:
            meta = dict(cfg=cfg, c=c)
            _finish(block, None)
            return nc, meta
        p45 = dict(pe0=cnt[id(s_pe)], ve0=cnt[id(s_ve)], ac0=cnt[id(s_ac)],
                   st20=cnt[id(s_st2)], st2B0=cnt[id(s_st2B)],
                   ac20=cnt[id(s_ac2)], ac2B0=cnt[id(s_ac2B)])
        arf2f = attsb.bitcast(F32)

        def p45_stw(tc):
            if tc % 2 == 0:
                return s_st2, p45["st20"] + (tc // 2 + 1) * 16
            return s_st2B, p45["st2B0"] + (tc // 2 + 1) * 32

        def p45_ld(tc):
            if tc % 2 == 0:
                return s_ac2, p45["ac20"] + (tc // 2 + 1) * 16
            return s_ac2B, p45["ac2B0"] + (tc // 2 + 1) * 16

        def p45_tr(tc, dc):
            return p45["pe0"] + tc * NDC + dc + 1

        def p45_xb(tc):
            return arf if tc % 2 == 0 else arf2f

        def p45_ob(tc):
            return wkb if tc % 2 == 0 else vbuf

        def p45_ev_tgt(tc, dc):
            if tc % 2 == 0:
                return wkb2[:, dc * 128:(dc + 1) * 128]
            if dc < NDC // 2:
                return pt2[:, dc * 128:(dc + 1) * 128]
            return evb[:, (dc - NDC // 2) * 128:(dc - NDC // 2 + 1) * 128]

        evA = {}
        evV = {}
        apA = {}

        def p45_ve(v):
            v.drain()
            for tc in range(NMT):
                v.reduce_sum(l2p[:, 224 + tc:225 + tc],
                             l2p[:, 4 * tc:4 * tc + 4], AX.X)
                v.reduce_sum(l2p[:, 208 + tc:209 + tc],
                             l2p[:, 64 + 4 * tc:64 + 4 * tc + 4], AX.X)
            v.drain()
            NM = NMT
            v.tensor_scalar_mul(l2p[:, 128:128 + NM], l2p[:, 224:224 + NM],
                                1.0 / D)
            v.tensor_scalar_mul(l2p[:, 240:240 + NM], l2p[:, 208:208 + NM],
                                1.0 / D)
            v.drain()
            v.tensor_tensor(l2p[:, 192:192 + NM], l2p[:, 128:128 + NM],
                            l2p[:, 128:128 + NM], ALU.mult)
            v.drain()
            v.tensor_tensor(l2p[:, 144:144 + NM], l2p[:, 240:240 + NM],
                            l2p[:, 192:192 + NM], ALU.subtract)
            inc(v.drain(), s_ve)
            v.wait_ge(s_ac, p45["ac0"] + 1)
            v.tensor_tensor(l2p[:, 176:176 + NM], l2p[:, 128:128 + NM],
                            l2p[:, 160:160 + NM], ALU.mult)
            v.drain()
            v.tensor_scalar_mul(l2p[:, 176:176 + NM], l2p[:, 176:176 + NM],
                                -1.0)
            inc(v.drain(), s_ve)
            for tc in range(NMT):
                for dc in range(1, NDC, 2):
                    v.wait_ge(s_pe, p45_tr(tc, dc))
                    if tc >= 2 and dc == 1:
                        v.wait_ge(*p45_stw(tc - 2))
                    op = v.tensor_copy(p45_ev_tgt(tc, dc), ps_tr2[:, 0:128])
                    inc(op, s_ve)
                    evV[(tc, dc)] = cnt[id(s_ve)]
        block.vector(p45_ve)

        def p45_emit_apply(sc, tc):
            sc.wait_ge(*p45_ld(tc))
            if tc < 2:
                sc.wait_ge(s_ve, p45["ve0"] + 2)
            else:
                sc.wait_ge(s_pe, p45_tr(tc - 2, NDC - 1))
            sc.activation(p45_ob(tc)[:, 0:D], p45_xb(tc)[:, 0:D], AF.Identity,
                          bias=l2p[:, 176 + tc:177 + tc],
                          scale=l2p[:, 160 + tc:161 + tc])
            inc(sc.drain(), s_ac)
            apA[tc] = cnt[id(s_ac)]
            if tc + 2 < NMT:
                ldsem = p45_ld(tc + 2)[0]
                inc(sc.dma_start(out=p45_xb(tc + 2)[:, 0:D],
                                 in_=xaft_h[(tc + 2) * 128:(tc + 3) * 128, 0:D]),
                    ldsem, 16)

        def p45_ac(sc):
            sc.wait_ge(s_st, p4_st_end)
            inc(sc.dma_start(out=arf[:, 0:D], in_=xaft_h[0:128, 0:D]),
                s_ac2, 16)
            if NMT > 1:
                sc.wait_ge(s_pe, p4_pe_end)
                inc(sc.dma_start(out=arf2f[:, 0:D], in_=xaft_h[128:256, 0:D]),
                    s_ac2B, 16)
            sc.wait_ge(s_ve, p45["ve0"] + 1)
            sc.activation(l2p[:, 192:192 + NMT], l2p[:, 144:144 + NMT], AF.Ln,
                          bias=float(cfg["EPS"]), scale=1.0)
            sc.drain()
            sc.activation(l2p[:, 160:160 + NMT], l2p[:, 192:192 + NMT], AF.Exp,
                          bias=0.0, scale=-0.5)
            inc(sc.drain(), s_ac)
            p45_emit_apply(sc, 0)
            if NMT > 1:
                p45_emit_apply(sc, 1)
            for tc in range(NMT):
                for dc in range(0, NDC, 2):
                    sc.wait_ge(s_pe, p45_tr(tc, dc))
                    if tc >= 2 and dc == 0:
                        sc.wait_ge(*p45_stw(tc - 2))
                    sc.copy(p45_ev_tgt(tc, dc), ps_tr[:, 0:128])
                    inc(sc.drain(), s_ac)
                    evA[(tc, dc)] = cnt[id(s_ac)]
                if tc + 2 < NMT:
                    p45_emit_apply(sc, tc + 2)
        block.scalar(p45_ac)

        def p45_pe(t):
            for tc in range(NMT):
                t.wait_ge(s_ac, apA[tc])
                src = p45_ob(tc)
                for dc in range(NDC):
                    idx = tc * NDC + dc
                    if idx >= 2:
                        ptc, pdc = divmod(idx - 2, NDC)
                        if pdc % 2 == 0:
                            t.wait_ge(s_ac, evA[(ptc, pdc)])
                        else:
                            t.wait_ge(s_ve, evV[(ptc, pdc)])
                    pt = ps_tr if dc % 2 == 0 else ps_tr2
                    tr = t.transpose(pt[:, 0:128],
                                     src[:, dc * 128:(dc + 1) * 128], idn[:, :])
                    inc(tr, s_pe)
        block.tensor(p45_pe)

        def p45_sync(sync):
            for tc in range(NMT):
                sync.wait_ge(s_ac, evA[(tc, NDC - 2)])
                sync.wait_ge(s_ve, evV[(tc, NDC - 1)])
                if tc % 2 == 0:
                    inc(sync.dma_start(
                        out=ln2T_h[0:D, tc * 128:(tc + 1) * 128]
                            .rearrange("(dc p) t -> p dc t", p=128),
                        in_=wkb2[:, 0:NDC * 128]
                            .rearrange("p (dc t) -> p dc t", t=128)), s_st2, 16)
                else:
                    ND2 = NDC // 2
                    inc(sync.dma_start(
                        out=ln2T_h[0:ND2 * 128, tc * 128:(tc + 1) * 128]
                            .rearrange("(dc p) t -> p dc t", p=128),
                        in_=pt2[:, 0:ND2 * 128]
                            .rearrange("p (dc t) -> p dc t", t=128)), s_st2B, 16)
                    inc(sync.dma_start(
                        out=ln2T_h[ND2 * 128:D, tc * 128:(tc + 1) * 128]
                            .rearrange("(dc p) t -> p dc t", p=128),
                        in_=evb[:, 0:ND2 * 128]
                            .rearrange("p (dc t) -> p dc t", t=128)), s_st2B, 16)
        block.sync(p45_sync)

        p45_pe_end = cnt[id(s_pe)]
        p45_st2_end = cnt[id(s_st2)]
        p45_st2B_end = cnt[id(s_st2B)]
        p45_ac_end = cnt[id(s_ac)]
        p45_ve_end = cnt[id(s_ve)]

        def p45_st_all(eng):
            eng.wait_ge(s_st2, p45_st2_end)
            if p45_st2B_end > 0:
                eng.wait_ge(s_st2B, p45_st2B_end)

        OQ6 = c.OQ
        NB6 = max(1, OQ6 // 512)
        NFT = c.NFT
        arfB = arf.bitcast(BF16)
        wcaps = [(wkb2, S // OQ6), (arfB, (2 * NDC * c.P1GW) // OQ6),
                 (pt2, 2048 // OQ6), (evb, 2048 // OQ6)]
        wfrags = []
        _d0 = 0
        for _buf, _cap in wcaps:
            _nd = min(_cap, NDC - _d0)
            if _nd > 0:
                wfrags.append((_buf, _d0, _nd))
                _d0 += _nd
        assert _d0 == NDC, (_d0, NDC)
        m_w = {}
        gm = {}

        def emit_w(g, oq):
            ws = s_wv0 if oq % 2 == 0 else s_wv1
            if oq % 2 == 0:
                inc(g.dma_start(
                    out=ar2[:, 0:NDC * OQ6].rearrange("p (dc o) -> p dc o",
                                                      o=OQ6),
                    in_=WTt[0:D, oq * OQ6:(oq + 1) * OQ6]
                        .rearrange("(dc p) o -> p dc o", p=128)), ws, 16)
            else:
                for buf, d0, nd in wfrags:
                    inc(g.dma_start(
                        out=buf[:, 0:nd * OQ6].rearrange("p (dc o) -> p dc o",
                                                         o=OQ6),
                        in_=WTt[d0 * 128:(d0 + nd) * 128,
                                oq * OQ6:(oq + 1) * OQ6]
                            .rearrange("(dc p) o -> p dc o", p=128)), ws, 16)
            m_w[oq] = (ws, cnt[id(ws)])

        def w_rhs(oq, dc, w0, w1):
            if oq % 2 == 0:
                return ar2[:, dc * OQ6 + w0:dc * OQ6 + w1]
            for buf, d0, nd in wfrags:
                if d0 <= dc < d0 + nd:
                    return buf[:, (dc - d0) * OQ6 + w0:(dc - d0) * OQ6 + w1]
            raise AssertionError(dc)

        def p45_gp(g):
            g.wait_ge(s_pe, p3_pe_end)
            emit_w(g, 0)
            g.wait_ge(s_pe, p4_pe_end)
            h6 = NMT // 2
            g.wait_ge(s_st2, p45["st20"] + ((h6 + 1) // 2) * 16)
            if h6 // 2 > 0:
                g.wait_ge(s_st2B, p45["st2B0"] + (h6 // 2) * 32)
            inc(g.dma_start(
                out=big[:, 0:NDC * QS].rearrange("p (dc t) -> p dc t", t=QS),
                in_=ln2T_h[0:D, 0:QS].rearrange("(dc p) t -> p dc t", p=128)),
                s_g1, 16)
            g.wait_ge(s_pe, p45_pe_end)
            inc(g.dma_start(out=atv[:, 0:NDC * ER]
                            .rearrange("p (dc e) -> p dc e", e=ER),
                            in_=ATt.rearrange("(dc p) e -> p dc e", p=128)),
                s_g2, 16)
            g.wait_ge(s_ac, p45_ac_end)
            inc(g.dma_start(out=attsb[0:1, 0:O], in_=bTt[:, :]), s_g3, 16)
            inc(g.dma_start(out=attsb[:, O:2 * O], in_=BTt[:, :]), s_g3, 16)
        block.gpsimd(p45_gp)

        ttm = []

        def p6t_pe(t):
            t.wait_ge(s_g1, 16)
            t.wait_ge(s_g2, 16)
            t.wait_ge(s_ve, p4_ve_end)
            for tc in range(NFT):
                psd = ps_sc0.ap() if tc < 4 else ps_sc1.ap()
                col = (tc % 4) * 128
                for dc in range(NDC):
                    mm = t.matmul(
                        psd[:, col:col + ER],
                        big[:, dc * QS + tc * 128:dc * QS + (tc + 1) * 128],
                        atv[:, dc * ER:(dc + 1) * ER],
                        start=(dc == 0), stop=(dc == NDC - 1))
                inc(mm, s_pe)
                ttm.append(cnt[id(s_pe)])
        block.tensor(p6t_pe)

        if PH < 5:
            meta = dict(cfg=cfg, c=c)
            _finish(block, None)
            return nc, meta
        p5 = dict(hw0=cnt[id(s_hw)], hwB0=cnt[id(s_hwB)], sw0=cnt[id(s_sw)],
                  pe0=cnt[id(s_pe)], ve0=cnt[id(s_ve)], ac0=cnt[id(s_ac)],
                  gp0=cnt[id(s_gp)])

        def p5_dsem(dc):
            if dc % 2 == 0:
                return s_hw, p5["hw0"] + (dc // 2 + 1) * 16
            return s_hwB, p5["hwB0"] + (dc // 2 + 1) * 16

        def p5_sync(sync):
            p45_st_all(sync)
            for dc in range(NDC):
                if dc >= 2:
                    sync.wait_ge(s_ve, p5["ve0"] + dc - 1)
                dst = pt2 if dc % 2 == 0 else evb
                inc(sync.dma_start(
                    out=dst[:, 0:c.MYTOK],
                    in_=ln2T_h[dc * 128:(dc + 1) * 128, 0:c.MYTOK]),
                    p5_dsem(dc)[0], 16)
        block.sync(p5_sync)

        def p5_ve(v):
            v.wait_ge(s_sw, p5["sw0"] + 16)
            for dc in range(NDC):
                v.wait_ge(*p5_dsem(dc))
                src = pt2 if dc % 2 == 0 else evb
                v.tensor_tensor(arf[:, 0:c.MYTOK], src[:, 0:c.MYTOK],
                                wmv[:, 0:c.MYTOK], ALU.mult)
                v.drain()
                v.reduce_sum(qpart[:, dc:dc + 1], arf[:, 0:c.MYTOK], AX.X)
                inc(v.drain(), s_ve)
        block.vector(p5_ve)

        def p5_gp(g):
            p45_st_all(g)
            inc(g.dma_start(out=wmv[:, 0:c.MYTOK], in_=wmaskt[:, :]), s_sw, 16)
            g.wait_ge(s_ve, p5["ve0"] + NDC)
            with g.register("pr5") as preg:
                g.reg_load(preg, prmv[0:1, 8:9])
                with g.If_eq(preg, 0):
                    g.dma_start(out=sh_qpart[0:128, 0:NDC],
                                in_=qpart[:, 0:NDC]).then_inc(s_sw, 16)
                with g.Else():
                    g.dma_start(out=sh_qpart[128:256, 0:NDC],
                                in_=qpart[:, 0:NDC]).then_inc(s_sw, 16)
                cnt[id(s_sw)] += 16
                g.wait_ge(s_sw, cnt[id(s_sw)])
                with g.If_eq(preg, 0):
                    g.dma_start(out=sh_flags[0:1, 2:4],
                                in_=prmv[0:1, 6:8]).then_inc(s_sw, 16)
                with g.Else():
                    g.dma_start(out=sh_flags[1:2, 2:4],
                                in_=prmv[0:1, 6:8]).then_inc(s_sw, 16)
                cnt[id(s_sw)] += 16
                g.wait_ge(s_sw, cnt[id(s_sw)])
                with g.register("cond2") as cond, g.register("tgt2") as tgt, \
                     g.register("fa2") as fa, g.register("fb2") as fb, \
                     g.register("ea2") as ea, g.register("eb2") as eb:
                    g.reg_load(ea, prmv[0:1, 6:7])
                    g.reg_load(eb, prmv[0:1, 7:8])
                    g.reg_mov(cond, 1)
                    g.reg_mov(tgt, 0)
                    pr0, pr1 = (0, 1) if RL else (1, 0)
                    with g.While(cond):
                        g.reg_add(tgt, tgt, 16)
                        with g.If_eq(preg, 0):
                            g.dma_start(out=flgrd[0:1, 2:4],
                                        in_=sh_flags[pr0:pr0 + 1, 2:4]
                                        ).then_inc(s_po2, 16)
                        with g.Else():
                            g.dma_start(out=flgrd[0:1, 2:4],
                                        in_=sh_flags[pr1:pr1 + 1, 2:4]
                                        ).then_inc(s_po2, 16)
                        g.wait_ge(s_po2, tgt)
                        g.reg_load(fa, flgrd[0:1, 2:3])
                        g.reg_load(fb, flgrd[0:1, 3:4])
                        g.reg_alu(fa, fa, ea, ALU.is_equal)
                        g.reg_alu(fb, fb, eb, ALU.is_equal)
                        g.reg_alu(cond, fa, fb, ALU.bitwise_and)
                        g.reg_alu(cond, cond, 1, ALU.bitwise_xor)
                qr0, qr1 = (0, 128) if RL else (128, 0)
                with g.If_eq(preg, 0):
                    g.dma_start(out=qpeer[:, 0:NDC],
                                in_=sh_qpart[qr0:qr0 + 128, 0:NDC]
                                ).then_inc(s_sw, 16)
                with g.Else():
                    g.dma_start(out=qpeer[:, 0:NDC],
                                in_=sh_qpart[qr1:qr1 + 128, 0:NDC]
                                ).then_inc(s_sw, 16)
                cnt[id(s_sw)] += 16
                g.wait_ge(s_sw, cnt[id(s_sw)])
            g.sem_inc(s_gp, 1)
            cnt[id(s_gp)] += 1
        block.gpsimd(p5_gp)
        mark("pair2_ready", s_gp)

        def p5_ve2(v):
            wait(v, "pair2_ready")
            v.tensor_tensor(qpart[:, 0:NDC], qpart[:, 0:NDC], qpeer[:, 0:NDC],
                            ALU.add)
            v.drain()
            for e in range(E):
                v.tensor_tensor(wk1[:, 0:NDC], qpart[:, 0:NDC],
                                rwv[:, e * NDC:(e + 1) * NDC], ALU.mult)
                v.drain()
                v.reduce_sum(wk1[:, NDC + e:NDC + e + 1], wk1[:, 0:NDC], AX.X)
                v.drain()
            inc(v.engine_nop(), s_ve)
        block.vector(p5_ve2)
        mark("p5_dots", s_ve)

        def p5_pe(t):
            wait(t, "p5_dots")
            t.wait_ge(s_ac, p45_ac_end)
            mm = t.matmul(ps_sumA[0:1, 0:E], onefc[:, 0:1],
                          wk1[:, NDC:NDC + E], start=True, stop=True)
            inc(mm, s_pe)
        block.tensor(p5_pe)
        mark("p5_logits_mm", s_pe)

        def p5_ve3(v):
            wait(v, "p5_logits_mm")
            v.tensor_scalar_mul(r8[0:1, 0:E], ps_sumA[0:1, 0:E], 1.0 / c.WIN_N)
            v.drain()
            v.tensor_tensor(r8[0:1, 0:E], r8[0:1, 0:E], rbv[0:1, 0:E], ALU.add)
            v.drain()
            v.reduce_max(r8[0:1, E:E + 1], r8[0:1, 0:E], AX.X)
            v.drain()
            v.tensor_tensor(r8[0:1, 8:8 + E], r8[0:1, 0:E],
                            r8[0:1, E:E + 1].broadcast_to([1, E]), ALU.subtract)
            inc(v.drain(), s_ve)
        block.vector(p5_ve3)
        mark("p5_lg", s_ve)

        def p5_ac(sc):
            wait(sc, "p5_lg")
            sc.activation(r8[0:1, 16:16 + E], r8[0:1, 8:8 + E], AF.Exp,
                          bias=0.0, scale=1.0)
            inc(sc.drain(), s_ac)
        block.scalar(p5_ac)
        mark("p5_exp", s_ac)

        def p5_ve4(v):
            wait(v, "p5_exp")
            v.reduce_sum(r8[0:1, 24:25], r8[0:1, 16:16 + E], AX.X)
            v.drain()
            v.reciprocal(r8[0:1, 25:26], r8[0:1, 24:25])
            v.drain()
            v.tensor_tensor(r8[0:1, 32:32 + E], r8[0:1, 16:16 + E],
                            r8[0:1, 25:26].broadcast_to([1, E]), ALU.mult)
            v.drain()
            v.tensor_copy(r8b[0:1, 0:E], r8[0:1, 32:32 + E])
            inc(v.drain(), s_ve)
        block.vector(p5_ve4)
        mark("p5_routing", s_ve)

        def p5_pe2(t):
            wait(t, "p5_routing")
            t.wait_ge(s_ac, p45_ac_end)
            tr = t.transpose(ps_tr[0:E, 0:1], r8b[0:1, 0:E], idn[0:1, 0:1])
            inc(tr, s_pe)
        block.tensor(p5_pe2)
        mark("p5_rT", s_pe)

        def p5_ac2(sc):
            wait(sc, "p5_rT")
            sc.copy(r8b[0:E, 1:2], ps_tr[0:E, 0:1])
            inc(sc.drain(), s_ac)
        block.scalar(p5_ac2)
        mark("p5_rTs", s_ac)

        def p5_pe3(t):
            wait(t, "p5_rTs")
            mm = t.matmul(ps[4][:, 0:1], repv[0:E, 0:ER], r8b[0:E, 1:2],
                          start=True, stop=True)
            inc(mm, s_pe)
        block.tensor(p5_pe3)
        mark("p5_rexp_mm", s_pe)

        def p5_ac3(sc):
            wait(sc, "p5_rexp_mm")
            sc.copy(rexp[:, 0:1], ps[4][:, 0:1])
            inc(sc.drain(), s_ac)
        block.scalar(p5_ac3)
        mark("p5_rexp", s_ac)

        if PH < 6:
            meta = dict(cfg=cfg, c=c)
            _finish(block, None)
            return nc, meta
        assert OQ6 <= 512
        p6 = dict(hw0=cnt[id(s_hw)], hwB0=cnt[id(s_hwB)], sw0=cnt[id(s_sw)],
                  pe0=cnt[id(s_pe)], ve0=cnt[id(s_ve)], ac0=cnt[id(s_ac)],
                  st0=cnt[id(s_st)])

        def p6_usem(ui):
            if ui % 2 == 0:
                return s_hw, p6["hw0"] + (ui // 2 + 1) * 16
            return s_hwB, p6["hwB0"] + (ui // 2 + 1) * 16

        def p6_sync(sync):
            sync.wait_ge(s_ve, p4_ve_end)
            sync.wait_ge(s_ve, p5["ve0"] + NDC)
            for oq in range(c.OQN):
                for tc in range(NFT):
                    ui = oq * NFT + tc
                    if ui >= 2:
                        sync.wait_ge(s_ve, p6["ve0"] + ui - 1)
                    inc(sync.dma_start(
                        out=xres[:, (ui % 2) * OQ6:(ui % 2) * OQ6 + OQ6],
                        in_=xaft_h[tc * 128:(tc + 1) * 128,
                                   oq * OQ6:(oq + 1) * OQ6]),
                        s_hw if ui % 2 == 0 else s_hwB, 16)
                    if ui >= 1:
                        po, ptc = divmod(ui - 1, NFT)
                        sync.wait_ge(s_ve, p6["ve0"] + ui)
                        inc(sync.dma_start(
                            out=out_t[ptc * 128:(ptc + 1) * 128,
                                      po * OQ6:(po + 1) * OQ6],
                            in_=evf[:, ((ui - 1) % 2) * OQ6:((ui - 1) % 2) * OQ6 + OQ6]),
                            s_st, 16)
            ui = c.OQN * NFT
            po, ptc = divmod(ui - 1, NFT)
            sync.wait_ge(s_ve, p6["ve0"] + ui)
            inc(sync.dma_start(
                out=out_t[ptc * 128:(ptc + 1) * 128, po * OQ6:(po + 1) * OQ6],
                in_=evf[:, ((ui - 1) % 2) * OQ6:((ui - 1) % 2) * OQ6 + OQ6]), s_st, 16)
            sync.wait_ge(s_st, cnt[id(s_st)])
        block.sync(p6_sync)

        def p6_gp(g):
            for oq in range(1, c.OQN):
                if oq == 1:
                    g.wait_ge(s_ve, p5["ve0"] + NDC)
                else:
                    g.wait_ge(s_pe, p6["pe0"] + NFT + (oq - 1) * NFT)
                emit_w(g, oq)
        block.gpsimd(p6_gp)

        def p6_pe(t):
            for tc in range(NFT):
                t.wait_ge(s_ac, p6["ac0"] + 2 * tc + 1)
                tr = t.transpose(ps_tr[:, 0:128],
                                 wkb[:, tc * 128:(tc + 1) * 128], idn[:, :])
                inc(tr, s_pe)
            for oq in range(c.OQN):
                t.wait_ge(*m_w[oq])
                if oq == 0:
                    t.wait_ge(s_g3, 32)
                for tc in range(NFT):
                    ui = oq * NFT + tc
                    if ui < 4:
                        t.wait_ge(s_ac, p6["ac0"] + 2 * NFT - 1)
                    if ui >= 4:
                        t.wait_ge(s_ve, p6["ve0"] + ui - 3)
                    pb = ps[ui % 4]
                    t.wait_ge(s_ac, p6["ac0"] + 2 * tc + 2)
                    t.matmul(pb[:, 0:OQ6], onebr[0:1, :],
                             attsb[0:1, oq * OQ6:(oq + 1) * OQ6],
                             start=True, stop=False)
                    for dc in range(NDC):
                        t.matmul(pb[:, 0:OQ6],
                                 big[:, dc * QS + tc * 128:dc * QS + (tc + 1) * 128],
                                 w_rhs(oq, dc, 0, OQ6),
                                 start=False, stop=False)
                    mm = t.matmul(pb[:, 0:OQ6],
                                  tts[:, tc * 128:(tc + 1) * 128],
                                  attsb[:, O + oq * OQ6:O + (oq + 1) * OQ6],
                                  start=False, stop=True)
                    inc(mm, s_pe)
        block.tensor(p6_pe)

        def p6_ac(sc):
            sc.wait_ge(s_pe, p45_pe_end)
            for tc in range(NFT):
                psd = ps_sc0.ap() if tc < 4 else ps_sc1.ap()
                col = (tc % 4) * 128
                sc.wait_ge(s_pe, ttm[tc])
                sc.copy(wkb[:, tc * 128:(tc + 1) * 128], psd[:, col:col + ER])
                inc(sc.drain(), s_ac)
                sc.wait_ge(s_pe, p6["pe0"] + tc + 1)
                sc.activation(tts[:, tc * 128:(tc + 1) * 128], ps_tr[:, 0:128],
                              AF.Identity, bias=0.0, scale=rexp[:, 0:1])
                inc(sc.drain(), s_ac)
        block.scalar(p6_ac)

        def p6_ve(v):
            for oq in range(c.OQN):
                for tc in range(NFT):
                    ui = oq * NFT + tc
                    v.wait_ge(s_pe, p6["pe0"] + NFT + ui + 1)
                    v.wait_ge(*p6_usem(ui))
                    if ui >= 2:
                        v.wait_ge(s_st, p6["st0"] + ui * 16)
                    pp = ui % 2
                    v.tensor_tensor(evf[:, pp * OQ6:pp * OQ6 + OQ6],
                                    ps[ui % 4][:, 0:OQ6],
                                    xres[:, pp * OQ6:pp * OQ6 + OQ6],
                                    ALU.add)
                    inc(v.drain(), s_ve)
        block.vector(p6_ve)

        meta = dict(cfg=cfg, c=c)
    return nc, meta



def host_slices(cfg, inputs, core):
    c = C(cfg)
    b, p, q = core // 4, core % 2, core % 4
    s_loc = q // 2
    x = np.ascontiguousarray(inputs["L0_attn_hidden_states"][b])
    wq = inputs["L0_attn_wq"]
    wkv = inputs["L0_attn_wkv"]
    wo = inputs["L0_attn_wo"]
    W = inputs["L0_ffn_W"]
    bias = inputs["L0_ffn_b"]
    A = inputs["L0_ffn_A"]
    B = inputs["L0_ffn_B"]
    rW = inputs["L0_ffn_router_W"]
    rb = inputs["L0_ffn_router_b"]
    S, D, HL, DN, DV, E, R, O = c.S, c.D, c.HL, c.DN, c.DV, c.E, c.R, c.O
    NH = 2 * HL
    DQG = wq.shape[1] // NH
    myheads = list(range(p * HL, (p + 1) * HL))
    own = [p * c.QS, 2 * c.QS + p * c.QS]
    ffn0 = q * c.QS
    strips = [ffn0, own[0] if own[1] == ffn0 else own[1]]

    xT = np.ascontiguousarray(x.T)
    x_strips = np.ascontiguousarray(
        np.concatenate([x[s0:s0 + c.QS] for s0 in strips], axis=0))
    wq_l = np.ascontiguousarray(
        np.concatenate([wq[:, h * DQG:h * DQG + DN] for h in myheads], axis=1))
    wk_l = np.ascontiguousarray(
        np.concatenate([wkv[:, h * DN:(h + 1) * DN] for h in myheads], axis=1))
    vbase = NH * DN
    wv_l = np.ascontiguousarray(
        np.concatenate([wkv[:, vbase + h * DV:vbase + (h + 1) * DV] for h in myheads], axis=1))
    wo_full_a = np.ascontiguousarray(wo)
    WT = np.ascontiguousarray(W.T)
    bT = np.ascontiguousarray(bias[None, :])
    AT = np.ascontiguousarray(A.transpose(2, 0, 1).reshape(D, E * R))
    BT = np.ascontiguousarray(B.transpose(0, 2, 1).reshape(E * R, O))
    rwH = np.ascontiguousarray(
        rW.reshape(E, c.NDC, 128).transpose(2, 0, 1).reshape(128, E * c.NDC))
    rbT = np.ascontiguousarray(rb[None, :])
    gtok = np.concatenate([np.arange(s0, s0 + c.QS) for s0 in strips])
    wm = ((gtok >= c.W0) & (gtok < c.W1)).astype(np.float32)[None, :]
    wmask = np.ascontiguousarray(np.tile(wm, (128, 1)))
    salt = np.random.randint(1, 2**31 - 1, size=4, dtype=np.int32)
    prm = np.array([[p,
                     strips[0] // c.QS, strips[1] // c.QS,
                     0,
                     salt[0], salt[1], salt[2], salt[3],
                     p, 1 - p, 0, 0]], dtype=np.int32)
    return dict(xT=xT, x_strips=x_strips, wq_l=wq_l, wk_l=wk_l, wv_l=wv_l,
                wo_full=wo_full_a, WTt=WT, bTt=bT, ATt=AT, BTt=BT,
                rwHt=rwH, rbTt=rbT, wmaskt=wmask, prm=prm)


_SALT_SHARED = None


def kernel(**inputs):
    cfg = full_cfg()
    nc, meta = build_program(cfg)
    in_maps = []
    for corex in range(8):
        in_maps.append(host_slices(cfg, inputs, corex))
    for k in range(4):
        in_maps[2 * k + 1]["prm"][0, 4:8] = in_maps[2 * k]["prm"][0, 4:8]
    res = run_bass_kernel_spmd(nc, in_maps, core_ids=list(range(8)))
    c = C(cfg)
    out = np.zeros((2, cfg["S"], cfg["O"]), dtype=np.float32)
    for corex in range(8):
        b, q = corex // 4, corex % 4
        out[b, q * c.QS:(q + 1) * c.QS, :] = np.asarray(res.results[corex]["out"])
    return out


# version 16
# speedup vs baseline: 1.8529x; 1.4530x over previous
import contextlib

import numpy as np
import ml_dtypes

import concourse.bass as bass
import concourse.mybir as mybir
from concourse.bass_utils import run_bass_kernel_spmd

F32 = mybir.dt.float32
BF16 = mybir.dt.bfloat16
I32 = mybir.dt.int32
AF = mybir.ActivationFunctionType
ALU = mybir.AluOpType
AX = mybir.AxisListType


def full_cfg():
    return dict(
        S=4096, D=4096, HL=16, DN=128, DV=128, E=8, R=16, O=4096,
        W0=611, W1=4095,
        SCALE=192.0 ** -0.5, SHIFT=12.0, EPS=1e-5, LORA_SCALING=2.0,
        P1GW=128,
        P2GW=256,
        OQN=8,
        DEBUG_TAPS=(),
    )


def small_cfg():
    return dict(
        S=512, D=512, HL=2, DN=128, DV=128, E=8, R=16, O=512,
        W0=77, W1=511,
        SCALE=192.0 ** -0.5, SHIFT=12.0, EPS=1e-5, LORA_SCALING=2.0,
        P1GW=128, P2GW=256, OQN=2,
        DEBUG_TAPS=(),
    )


class C:

    def __init__(self, cfg):
        self.__dict__.update(cfg)
        self.cfg = cfg
        self.NDC = self.D // 128
        self.NTC = self.S // 128
        self.QS = self.S // 4
        self.HDV = 2 * self.HL * self.DV
        self.ER = self.E * self.R
        assert self.ER == 128
        self.WIN_N = self.W1 - self.W0
        self.MYTOK = 2 * self.QS
        self.NMT = self.MYTOK // 128
        self.NFT = self.QS // 128
        self.OQ = self.O // self.OQN
        self.NP1 = self.S // self.P1GW
        self.NP2 = self.S // self.P2GW
        self.HLDN = self.HL * self.DN
        self.NCC = self.HLDN // 128


def _finish(block, _):
    return


def build_program(cfg):
    c = C(cfg)
    S, D, HL, DN, DV = c.S, c.D, c.HL, c.DN, c.DV
    E, R, O, ER = c.E, c.R, c.O, c.ER
    NDC, NTC, QS = c.NDC, c.NTC, c.QS
    taps = set(cfg.get("DEBUG_TAPS", ()))
    RL = bool(cfg.get("RACE_LOCAL", False))

    nc = bass.Bass("TRN2", target_bir_lowering=False,
                   detect_race_conditions=cfg.get("RACE_DETECT", False))

    xT = nc.dram_tensor("xT", [D, S], F32, kind="ExternalInput").ap()
    x_strips = nc.dram_tensor("x_strips", [c.MYTOK, D], F32, kind="ExternalInput").ap()
    wq_l = nc.dram_tensor("wq_l", [D, c.HLDN], F32, kind="ExternalInput").ap()
    wk_l = nc.dram_tensor("wk_l", [D, c.HLDN], F32, kind="ExternalInput").ap()
    wv_l = nc.dram_tensor("wv_l", [D, HL * DV], F32, kind="ExternalInput").ap()
    wo_full = nc.dram_tensor("wo_full", [c.HDV, D], F32, kind="ExternalInput").ap()
    WTt = nc.dram_tensor("WTt", [D, O], F32, kind="ExternalInput").ap()
    bTt = nc.dram_tensor("bTt", [1, O], F32, kind="ExternalInput").ap()
    ATt = nc.dram_tensor("ATt", [D, ER], F32, kind="ExternalInput").ap()
    BTt = nc.dram_tensor("BTt", [ER, O], F32, kind="ExternalInput").ap()
    rwHt = nc.dram_tensor("rwHt", [128, E * NDC], F32, kind="ExternalInput").ap()
    rbTt = nc.dram_tensor("rbTt", [1, E], F32, kind="ExternalInput").ap()
    wmaskt = nc.dram_tensor("wmaskt", [128, c.MYTOK], F32, kind="ExternalInput").ap()
    prm = nc.dram_tensor("prm", [1, 12], I32, kind="ExternalInput").ap()

    out_t = nc.dram_tensor("out", [QS, O], F32, kind="ExternalOutput").ap()

    lnT_h = nc.dram_tensor("lnT_h", [D, S], BF16, kind="Internal").ap()
    qT_h = nc.dram_tensor("qT_h", [c.HLDN, S], BF16, kind="Internal").ap()
    kT_h = nc.dram_tensor("kT_h", [c.HLDN, S], BF16, kind="Internal").ap()
    v_h = nc.dram_tensor("v_h", [S, HL * DV], BF16, kind="Internal").ap()
    xaft_h = nc.dram_tensor("xaft_h", [c.MYTOK, D], F32, kind="Internal").ap()
    attnT_loc = nc.dram_tensor("attnT_loc", [c.HL * 128, S], BF16, kind="Internal").ap()
    attnT_l2 = nc.dram_tensor("attnT_l2", [c.HDV, c.MYTOK], BF16, kind="Internal").ap()
    ln2T_h = nc.dram_tensor("ln2T_h", [D, c.MYTOK], BF16, kind="Internal").ap()

    sh_attnT = nc.dram_tensor("sh_attnT", [c.HDV, S], BF16, kind="Internal",
                              addr_space="Shared").ap()
    sh_qpart = nc.dram_tensor("sh_qpart", [2 * 128, NDC], F32, kind="Internal",
                              addr_space="Shared").ap()
    sh_flags = nc.dram_tensor("sh_flags", [2, 16], I32, kind="Internal",
                              addr_space="Shared").ap()

    ident_c = nc.inline_tensor(np.eye(128, dtype=ml_dtypes.bfloat16), "ident_c").ap()
    ones_colb = nc.inline_tensor(np.ones((128, 1), dtype=ml_dtypes.bfloat16), "ones_colb").ap()
    ones_rowb = nc.inline_tensor(np.ones((1, 128), dtype=ml_dtypes.bfloat16), "ones_rowb").ap()
    ones_colf = nc.inline_tensor(np.ones((128, 1), dtype=np.float32), "ones_colf").ap()
    ones_rowf = nc.inline_tensor(np.ones((1, 128), dtype=np.float32), "ones_rowf").ap()
    cm = np.zeros((128, 4 * 512), dtype=ml_dtypes.bfloat16)
    for j in range(4):
        cm[:, j * 512:(j + 1) * 512] = (
            np.arange(128)[:, None] + j * 128 <= np.arange(512)[None, :])
    cmask_c = nc.inline_tensor(cm, "cmask_c").ap()
    rep = np.zeros((E, ER), dtype=ml_dtypes.bfloat16)
    for e in range(E):
        rep[e, e * R:(e + 1) * R] = np.float32(cfg["LORA_SCALING"])
    rep_c = nc.inline_tensor(rep, "rep_c").ap()

    CVALS = [0.0, 1.0, -1.0, float(cfg["SCALE"]), -float(cfg["SHIFT"]),
             float(cfg["EPS"]), -0.5, 1.0 / D, 1.0 / c.WIN_N, -1.0]
    cvals_c = nc.inline_tensor(
        np.tile(np.array([CVALS], dtype=np.float32), (128, 1)), "cvals_c").ap()

    st_ = contextlib.ExitStack()

    def sb(name, shape, dtype):
        return st_.enter_context(nc.sbuf_tensor(name, shape, dtype))

    def mkpsum(name, shape, dtype=F32):
        return st_.enter_context(nc.psum_tensor(name, shape, dtype))

    def mksem(name):
        return st_.enter_context(nc.semaphore(name))

    with st_:
        P1GW, P2GW = c.P1GW, c.P2GW
        big = sb("big", [128, 32768], BF16)
        ar2 = sb("ar2", [128, 16384], BF16)
        arf = sb("arf", [128, NDC * P1GW], F32)
        attsb = sb("attsb", [128, 2 * (c.HDV // 128) * 128], BF16)
        wkb = sb("wkb", [128, S], BF16)
        wkb2 = sb("wkb2", [128, S], BF16)
        vbuf = sb("vbuf", [128, S], BF16)
        pt2 = sb("pt2", [128, 2048], BF16)
        attc = sb("attc", [128, 1024], BF16)
        evb = sb("evb", [128, 2048], BF16)
        EVW = max(c.O // 4, c.OQ)
        evf = sb("evf", [128, 2 * EVW], F32)
        xres = sb("xres", [128, 2 * EVW], F32)
        wk1 = sb("wk1", [128, 512], F32)
        bcv = sb("bcv", [128, 1024], F32)
        st_m = sb("st_m", [1, 512], F32)
        st_sq = sb("st_sq", [1, 512], F32)
        st_var = sb("st_var", [1, 512], F32)
        st_rstd = sb("st_rstd", [1, 512], F32)
        cstv = sb("cstv", [128, len(CVALS)], F32)
        idn = sb("idn", [128, 128], BF16)
        onebc = sb("onebc", [128, 1], BF16)
        onebr = sb("onebr", [1, 128], BF16)
        onefr = sb("onefr", [1, 128], F32)
        onefc = sb("onefc", [128, 1], F32)
        cmv = sb("cmv", [128, 4 * 512], BF16)
        repv = sb("repv", [E, ER], BF16)
        rbv = sb("rbv", [1, E], F32)
        rwv = sb("rwv", [128, E * NDC], F32)
        prmv = sb("prmv", [1, 12], I32)
        flgrd = sb("flgrd", [1, 4], I32)
        wmv = wkb2
        atv = vbuf
        tts = sb("tts", [128, c.NFT * 128], BF16)
        pfb = sb("pfb", [128, 512], BF16)
        facc = sb("facc", [128, 512], F32)
        faccB = sb("faccB", [128, 1024], BF16)
        ln2s = sb("ln2s", [128, 16], F32)
        l2p = sb("l2p", [128, 256], F32)
        qpart = sb("qpart", [128, NDC], F32)
        qpeer = sb("qpeer", [128, NDC], F32)
        r8 = sb("r8", [1, 64], F32)
        r8b = sb("r8b", [8, 8], BF16)
        rexp = sb("rexp", [128, 1], F32)

        for i, v in enumerate(CVALS):
            nc.const_aps.register(F32, float(v), cstv[:, i:i + 1])

        ps_sc0 = mkpsum("ps_sc0", [128, 1024])
        ps_sc1 = mkpsum("ps_sc1", [128, 1024])
        ps4t = mkpsum("ps4t", [128, 512])
        ps = [ps_sc0.ap()[:, 0:512], ps_sc0.ap()[:, 512:1024],
              ps_sc1.ap()[:, 0:512], ps_sc1.ap()[:, 512:1024],
              ps4t.ap()[:, 0:512]]
        ps_sumA = mkpsum("ps_sumA", [1, 512])
        ps_tr = mkpsum("ps_tr", [128, 1024], BF16)
        ps_bc = ps_tr.ap().bitcast(F32)
        ps_tr2 = mkpsum("ps_tr2", [128, 1024], BF16)
        ps_av1 = ps_tr2.ap().bitcast(F32)

        s_hw = mksem("s_hw")
        s_hwB = mksem("s_hwB")
        s_st = mksem("s_st")
        s_st2 = mksem("s_st2")
        s_st2B = mksem("s_st2B")
        s_sw = mksem("s_sw")
        s_ac2 = mksem("s_ac2")
        s_ac2B = mksem("s_ac2B")
        s_pe = mksem("s_pe")
        s_ve = mksem("s_ve")
        s_ac = mksem("s_ac")
        s_gp = mksem("s_gp")
        s_wv0 = mksem("s_wv0")
        s_wv1 = mksem("s_wv1")
        s_g1 = mksem("s_g1")
        s_g2 = mksem("s_g2")
        s_g3 = mksem("s_g3")
        s_stP1 = mksem("s_stP1")
        s_stP1B = mksem("s_stP1B")
        s_po1 = mksem("s_po1")
        s_po2 = mksem("s_po2")

        cnt = {}
        for s in (s_hw, s_hwB, s_st, s_st2, s_st2B, s_sw, s_ac2, s_ac2B,
                  s_pe, s_ve, s_ac, s_gp, s_wv0, s_wv1, s_g1, s_g2, s_g3,
                  s_stP1, s_stP1B):
            cnt[id(s)] = 0

        def inc(instr, s, n=1):
            cnt[id(s)] += n
            return instr.then_inc(s, n)

        M = {}

        def mark(name, s):
            M[name] = (s, cnt[id(s)])

        def wait(eng, name, extra=0):
            s, v = M[name]
            if v + extra > 0:
                eng.wait_ge(s, v + extra)

        block = st_.enter_context(nc.Block())
        gpregs = {}

        def p0_sync(sync):
            for dst, src in ((idn, ident_c), (onebc, ones_colb), (onebr, ones_rowb),
                             (onefr, ones_rowf), (onefc, ones_colf),
                             (cmv, cmask_c), (repv, rep_c),
                             (rbv, rbTt), (rwv, rwHt), (cstv, cvals_c),
                             (prmv, prm)):
                inc(sync.dma_start(out=dst[:, :], in_=src[:, :]), s_hw, 16)
        block.sync(p0_sync)
        mark("p0_const", s_hw)


        GW = c.P1GW
        NP1 = c.NP1
        assert 4 * GW <= 512
        arf2 = attsb.bitcast(F32)
        BGR = 2 * NDC * GW

        def p1_arf(par):
            return arf if par == 0 else arf2

        ve_ord = [("cast", 0), ("cast", 1)]
        ac_ord = [("sq", 0), ("sq", 1)]
        pe_ord = [("stat", 0)]
        st_ord = []
        for g in range(NP1):
            ve_ord += [("vstat", g), ("vstat2", g), ("apply", g)]
            if g + 2 < NP1:
                ve_ord.append(("cast", g + 2))
            ac_ord += [("astat", g), ("bce", g)]
            if g + 2 < NP1:
                ac_ord.append(("sq", g + 2))
            if g >= 1:
                ac_ord.append(("st", g - 1))
            if g + 1 < NP1:
                pe_ord.append(("stat", g + 1))
            pe_ord.append(("bc", g))
            st_ord.append(("st", g))
        ac_ord.append(("st", NP1 - 1))

        p1b = dict(hw=cnt[id(s_hw)], stP=cnt[id(s_stP1)],
                   stPB=cnt[id(s_stP1B)], ve=cnt[id(s_ve)],
                   ac=cnt[id(s_ac)], pe=cnt[id(s_pe)], g2=cnt[id(s_g2)])
        VE_I = {k: p1b["ve"] + i + 1 for i, k in enumerate(ve_ord)}
        AC_I = {}
        _aci = p1b["ac"]
        for k in ac_ord:
            if k[0] != "st":
                _aci += 1
                AC_I[k] = _aci
        PE_I = {k: p1b["pe"] + i + 1 for i, k in enumerate(pe_ord)}
        HW_I = {}
        for g in range(NP1):
            if g % 2 == 0:
                HW_I[("ld", g)] = (s_hw, p1b["hw"] + (g // 2 + 1) * 16)
            else:
                HW_I[("ld", g)] = (s_g2, p1b["g2"] + (g // 2 + 1) * 16)
        ST_I = {}
        for g in range(NP1):
            if g % 2 == 0:
                ST_I[("st", g)] = (s_stP1, p1b["stP"] + (g // 2 + 1) * 16)
            else:
                ST_I[("st", g)] = (s_stP1B, p1b["stPB"] + (g // 2 + 1) * 16)

        def p1_ld(eng, g):
            par = g % 2
            if g >= 2:
                eng.wait_ge(s_ve, VE_I[("apply", g - 2)])
            inc(eng.dma_start(
                out=p1_arf(par)[:, 0:NDC * GW]
                    .rearrange("p (dc t) -> p dc t", t=GW),
                in_=xT[0:D, g * GW:(g + 1) * GW]
                    .rearrange("(dc p) t -> p dc t", p=128)),
                HW_I[("ld", g)][0], 16)

        def p1_sync(sync):
            wait(sync, "p0_const")
            for g in range(0, NP1, 2):
                p1_ld(sync, g)
        block.sync(p1_sync)

        def p1_gp(gps):
            for g in range(1, NP1, 2):
                p1_ld(gps, g)
        block.gpsimd(p1_gp)

        def p1_ve(v):
            def cast(g):
                par = g % 2
                v.wait_ge(*HW_I[("ld", g)])
                if g >= 2:
                    v.wait_ge(s_pe, PE_I[("stat", g - 2)])
                v.tensor_copy(big[:, par * BGR:par * BGR + NDC * GW],
                              p1_arf(par)[:, 0:NDC * GW])
                inc(v.drain(), s_ve)

            def vstat(g):
                par = g % 2
                v.wait_ge(s_pe, PE_I[("stat", g)])
                pss = ps_sumA[0:1, 0:512] if par == 0 else ps[4][0:1, 0:512]
                o = par * GW
                v.tensor_scalar_mul(st_m[0:1, o:o + GW], pss[0:1, 0:GW], 1.0 / D)
                v.tensor_scalar_mul(st_sq[0:1, o:o + GW], pss[0:1, GW:2 * GW], 1.0 / D)
                v.drain()
                v.tensor_tensor(st_var[0:1, o:o + GW], st_m[0:1, o:o + GW],
                                st_m[0:1, o:o + GW], ALU.mult)
                v.drain()
                v.tensor_tensor(st_var[0:1, o:o + GW], st_sq[0:1, o:o + GW],
                                st_var[0:1, o:o + GW], ALU.subtract)
                inc(v.drain(), s_ve)

            def vstat2(g):
                par = g % 2
                o = par * GW
                v.wait_ge(s_ac, AC_I[("astat", g)])
                v.tensor_tensor(st_m[0:1, o:o + GW], st_m[0:1, o:o + GW],
                                st_rstd[0:1, o:o + GW], ALU.mult)
                v.drain()
                v.tensor_scalar_mul(st_m[0:1, o:o + GW], st_m[0:1, o:o + GW], -1.0)
                inc(v.drain(), s_ve)

            def apply(g):
                par = g % 2
                o = par * 2 * GW
                v.wait_ge(s_ac, AC_I[("bce", g)])
                if g >= 2:
                    v.wait_ge(*ST_I[("st", g - 2)])
                a3 = p1_arf(par)[:, 0:NDC * GW].rearrange("p (o t) -> p o t", t=GW)
                o3 = ar2[:, par * NDC * GW:(par + 1) * NDC * GW] \
                    .rearrange("p (o t) -> p o t", t=GW)
                rb = wk1[:, o:o + GW].rearrange("p (q t) -> p q t", q=1) \
                    .broadcast_to([128, NDC, GW])
                mb = wk1[:, o + GW:o + 2 * GW].rearrange("p (q t) -> p q t", q=1) \
                    .broadcast_to([128, NDC, GW])
                v.tensor_tensor(a3, a3, rb, ALU.mult)
                v.drain()
                v.tensor_tensor(o3, a3, mb, ALU.add)
                inc(v.drain(), s_ve)

            fns = dict(cast=cast, vstat=vstat, vstat2=vstat2, apply=apply)
            for kind, g in ve_ord:
                fns[kind](g)
        block.vector(p1_ve)

        def p1_ac(sc):
            def sq(g):
                par = g % 2
                sc.wait_ge(s_ve, VE_I[("cast", g)])
                if g >= 2:
                    sc.wait_ge(s_pe, PE_I[("stat", g - 2)])
                sc.activation(big[:, par * BGR + NDC * GW:par * BGR + 2 * NDC * GW],
                              big[:, par * BGR:par * BGR + NDC * GW],
                              AF.Square, bias=0.0, scale=1.0)
                inc(sc.drain(), s_ac)

            def astat(g):
                par = g % 2
                o = par * GW
                sc.wait_ge(s_ve, VE_I[("vstat", g)])
                sc.activation(st_rstd[0:1, o:o + GW], st_var[0:1, o:o + GW],
                              AF.Ln, bias=float(cfg["EPS"]), scale=1.0)
                sc.drain()
                sc.activation(st_rstd[0:1, o:o + GW], st_rstd[0:1, o:o + GW],
                              AF.Exp, bias=0.0, scale=-0.5)
                inc(sc.drain(), s_ac)

            def bce(g):
                par = g % 2
                o = par * 2 * GW
                sc.wait_ge(s_pe, PE_I[("bc", g)])
                if g >= 2:
                    sc.wait_ge(s_ve, VE_I[("apply", g - 2)])
                pa, pb = (ps[0], ps[1]) if par == 0 else (ps[2], ps[3])
                sc.copy(wk1[:, o:o + GW], pa[:, 0:GW])
                sc.copy(wk1[:, o + GW:o + 2 * GW], pb[:, 0:GW])
                inc(sc.drain(), s_ac)

            def stg(g):
                par = g % 2
                sc.wait_ge(s_ve, VE_I[("apply", g)])
                inc(sc.dma_start(
                    out=lnT_h[0:D, g * GW:(g + 1) * GW]
                        .rearrange("(dc p) t -> p dc t", p=128),
                    in_=ar2[:, par * NDC * GW:(par + 1) * NDC * GW]
                        .rearrange("p (dc t) -> p dc t", t=GW)),
                    ST_I[("st", g)][0], 16)

            fns = dict(sq=sq, astat=astat, bce=bce, st=stg)
            for kind, g in ac_ord:
                fns[kind](g)
        block.scalar(p1_ac)

        def p1_pe(t):
            def stat(g):
                par = g % 2
                t.wait_ge(s_ac, AC_I[("sq", g)])
                if g >= 2:
                    t.wait_ge(s_ve, VE_I[("vstat", g - 2)])
                pss = ps_sumA[0:1, 0:512] if par == 0 else ps[4][0:1, 0:512]
                for dc in range(NDC):
                    t.matmul(pss[0:1, 0:GW], onebc[:, :],
                             big[:, par * BGR + dc * GW:par * BGR + (dc + 1) * GW],
                             start=(dc == 0), stop=(dc == NDC - 1))
                for dc in range(NDC):
                    mm = t.matmul(pss[0:1, GW:2 * GW], onebc[:, :],
                                  big[:, par * BGR + (NDC + dc) * GW:
                                      par * BGR + (NDC + dc + 1) * GW],
                                  start=(dc == 0), stop=(dc == NDC - 1))
                inc(mm, s_pe)

            def bc(g):
                par = g % 2
                o = par * GW
                t.wait_ge(s_ve, VE_I[("vstat2", g)])
                if g >= 2:
                    t.wait_ge(s_ac, AC_I[("bce", g - 2)])
                pa, pb = (ps[0], ps[1]) if par == 0 else (ps[2], ps[3])
                t.matmul(pa[:, 0:GW], onefr[:, :], st_rstd[0:1, o:o + GW],
                         start=True, stop=True)
                mm = t.matmul(pb[:, 0:GW], onefr[:, :], st_m[0:1, o:o + GW],
                              start=True, stop=True)
                inc(mm, s_pe)

            fns = dict(stat=stat, bc=bc)
            for kind, g in pe_ord:
                fns[kind](g)
        block.tensor(p1_pe)

        for g in range(NP1):
            M[f"p1_apply_{g}"] = (s_ve, VE_I[("apply", g)])
            M[f"p1_st_{g}"] = ST_I[("st", g)]
            M[f"p1_stat_{g}"] = (s_pe, PE_I[("stat", g)])

        mark("p1_done_hw", s_hw)
        mark("p1_done_ve", s_ve)

        PH = cfg.get("PHASES", 9)
        if PH < 2:
            meta = dict(cfg=cfg, c=c)
            _finish(block, None)
            return nc, meta
        GP = c.P2GW
        NP2 = c.NP2
        NCC = c.NCC
        NCC2 = max(1, NCC // 4)
        NHALF = NCC // NCC2
        VB = HL * DV
        NHV = max(1, (NDC * VB + 16383) // 16384)
        VB2 = VB // NHV
        NVB = (VB2 + 511) // 512
        assert NVB == 1
        NTG = GP // 128
        NU = 2 * NHALF
        USZ = NCC2 * NDC * 128
        VSZ = NDC * VB2
        NQK = NU * NP2 * NCC2
        GQK = NU * NP2
        GTOT = GQK + NHV * NP2

        p1_pe_end = cnt[id(s_pe)]
        p1_stP_end = cnt[id(s_stP1)]
        p1_stPB_end = cnt[id(s_stP1B)]
        p2 = dict(sw0=cnt[id(s_sw)], a20=cnt[id(s_ac2)], a2B0=cnt[id(s_ac2B)],
                  pe0=cnt[id(s_pe)], ac0=cnt[id(s_ac)], st0=cnt[id(s_st)])

        def p2_gsem(gg):
            if gg % 2 == 0:
                return s_ac2, p2["a20"] + (gg // 2 + 1) * 16
            return s_ac2B, p2["a2B0"] + (gg // 2 + 1) * 16

        def p2_gld(eng, gg):
            gI = gg % NP2
            sl = gg % 2
            inc(eng.dma_start(
                out=ar2[:, sl * NDC * GP:(sl + 1) * NDC * GP]
                    .rearrange("p (dc t) -> p dc t", t=GP),
                in_=lnT_h[0:D, gI * GP:(gI + 1) * GP]
                    .rearrange("(dc p) t -> p dc t", p=128)),
                p2_gsem(gg)[0], 16)

        p2_wm = {}

        def p2_gp(g):
            for u in range(NU):
                pi, hf = divmod(u, NHALF)
                wsrc = wq_l if pi == 0 else wk_l
                ws = s_wv0 if u % 2 == 0 else s_wv1
                if u < 2:
                    g.wait_ge(s_pe, p1_pe_end)
                else:
                    g.wait_ge(s_pe, p2["pe0"] + (u - 1) * NP2 * NCC2)
                for cc in range(NCC2):
                    ccg = hf * NCC2 + cc
                    inc(g.dma_start(
                        out=big[:, (u % 2) * USZ + cc * NDC * 128:
                                (u % 2) * USZ + (cc + 1) * NDC * 128]
                            .rearrange("p (dc i) -> p dc i", i=128),
                        in_=wsrc[0:D, ccg * 128:(ccg + 1) * 128]
                            .rearrange("(dc p) i -> p dc i", p=128)), ws, 16)
                p2_wm[u] = (ws, cnt[id(ws)])
            for hv in range(NHV):
                ws = s_wv0 if (NU + hv) % 2 == 0 else s_wv1
                if hv == 0:
                    g.wait_ge(s_pe, p2["pe0"] +
                              ((NU - 1) * NP2 * NCC2 if VSZ <= USZ else NQK))
                elif hv == 1:
                    g.wait_ge(s_pe, p2["pe0"] + NQK)
                else:
                    g.wait_ge(s_pe, p2["pe0"] + NQK + (hv - 1) * NP2 * NTG)
                inc(g.dma_start(
                    out=big[:, (hv % 2) * VSZ:(hv % 2) * VSZ + VSZ]
                        .rearrange("p (dc i) -> p dc i", i=VB2),
                    in_=wv_l[0:D, hv * VB2:(hv + 1) * VB2]
                        .rearrange("(dc p) i -> p dc i", p=128)), ws, 16)
                p2_wm[NU + hv] = (ws, cnt[id(ws)])
        block.gpsimd(p2_gp)

        def p2_sync(sync):
            ac_i = p2["ac0"]
            for u in range(NU):
                pi, hf = divmod(u, NHALF)
                dst = qT_h if pi == 0 else kT_h
                for gI in range(NP2):
                    for cc in range(NCC2):
                        ccg = hf * NCC2 + cc
                        ac_i += 1
                        sync.wait_ge(s_ac, ac_i)
                        slot = (u * NP2 * NCC2 + gI * NCC2 + cc) % 8
                        inc(sync.dma_start(
                            out=dst[ccg * 128:(ccg + 1) * 128, gI * GP:(gI + 1) * GP],
                            in_=evb[:, slot * GP:(slot + 1) * GP]), s_st, 16)
            for hv in range(NHV):
                for gI in range(NP2):
                    for tc in range(NTG):
                        vi = (hv * NP2 + gI) * NTG + tc
                        ac_i += 1
                        sync.wait_ge(s_ac, ac_i)
                        inc(sync.dma_start(
                            out=v_h[(gI * NTG + tc) * 128:(gI * NTG + tc + 1) * 128,
                                    hv * VB2:(hv + 1) * VB2],
                            in_=evb[:, (vi % 4) * 512:(vi % 4) * 512 + VB2]),
                            s_st, 16)
        block.sync(p2_sync)

        def p2_pe(t):
            for u in range(NU):
                t.wait_ge(*p2_wm[u])
                for gI in range(NP2):
                    gg = u * NP2 + gI
                    sl = gg % 2
                    t.wait_ge(*p2_gsem(gg))
                    for cc in range(NCC2):
                        slot_idx = u * NP2 * NCC2 + gI * NCC2 + cc
                        if slot_idx >= 4:
                            t.wait_ge(s_ac, p2["ac0"] + slot_idx - 3)
                        for dc in range(NDC):
                            mm = t.matmul(
                                ps[slot_idx % 4][:, 0:GP],
                                big[:, (u % 2) * USZ + (cc * NDC + dc) * 128:
                                    (u % 2) * USZ + (cc * NDC + dc + 1) * 128],
                                ar2[:, sl * NDC * GP + dc * GP:
                                    sl * NDC * GP + (dc + 1) * GP],
                                start=(dc == 0), stop=(dc == NDC - 1))
                        inc(mm, s_pe)
            for hv in range(NHV):
                t.wait_ge(*p2_wm[NU + hv])
                for gI in range(NP2):
                    gg = (NU + hv) * NP2 + gI
                    sl = gg % 2
                    t.wait_ge(*p2_gsem(gg))
                    for tc in range(NTG):
                        vi = (hv * NP2 + gI) * NTG + tc
                        if vi >= 4:
                            t.wait_ge(s_ac, p2["ac0"] + NQK + vi - 3)
                        for dc in range(NDC):
                            mm = t.matmul(
                                ps[vi % 4][:, 0:VB2],
                                ar2[:, sl * NDC * GP + dc * GP + tc * 128:
                                    sl * NDC * GP + dc * GP + (tc + 1) * 128],
                                big[:, (hv % 2) * VSZ + dc * VB2:
                                    (hv % 2) * VSZ + (dc + 1) * VB2],
                                start=(dc == 0), stop=(dc == NDC - 1))
                        inc(mm, s_pe)
        block.tensor(p2_pe)

        def p2_ac(sc):
            sc.wait_ge(s_stP1, p1_stP_end)
            if p1_stPB_end > 0:
                sc.wait_ge(s_stP1B, p1_stPB_end)
            p2_gld(sc, 0)
            if GTOT > 1:
                p2_gld(sc, 1)
            for u in range(NU):
                for gI in range(NP2):
                    gg = u * NP2 + gI
                    for cc in range(NCC2):
                        slot_idx = u * NP2 * NCC2 + gI * NCC2 + cc
                        sc.wait_ge(s_pe, p2["pe0"] + slot_idx + 1)
                        if slot_idx >= 8:
                            sc.wait_ge(s_st, p2["st0"] + (slot_idx - 7) * 16)
                        slot = slot_idx % 8
                        inc(sc.activation(evb[:, slot * GP:(slot + 1) * GP],
                                          ps[slot_idx % 4][:, 0:GP],
                                          AF.Copy, bias=0.0, scale=1.0), s_ac)
                    if gg + 2 < GTOT:
                        p2_gld(sc, gg + 2)
            for hv in range(NHV):
                for gI in range(NP2):
                    gg = (NU + hv) * NP2 + gI
                    for tc in range(NTG):
                        vi = (hv * NP2 + gI) * NTG + tc
                        sc.wait_ge(s_pe, p2["pe0"] + NQK + vi + 1)
                        if vi < 4:
                            sc.wait_ge(s_st, p2["st0"] + NQK * 16)
                        else:
                            sc.wait_ge(s_st, p2["st0"] + (NQK + vi - 3) * 16)
                        inc(sc.activation(
                            evb[:, (vi % 4) * 512:(vi % 4) * 512 + VB2],
                            ps[vi % 4][:, 0:VB2],
                            AF.Copy, bias=0.0, scale=1.0), s_ac)
                    if gg + 2 < GTOT:
                        p2_gld(sc, gg + 2)
        block.scalar(p2_ac)

        p2_pe_end = cnt[id(s_pe)]
        p2_st_end = cnt[id(s_st)]
        p2_ac_end = cnt[id(s_ac)]
        p2_sw_end = cnt[id(s_sw)]
        p2_hw_end = cnt[id(s_hw)]

        if PH < 3:
            meta = dict(cfg=cfg, c=c)
            _finish(block, None)
            return nc, meta
        NQG = S // 512
        KCs = [4 * (qg + 1) for qg in range(NQG)]
        Ps = [k // 2 for k in KCs]
        per_pe = [p + 3 for p in Ps]
        per_ac = [p + 1 for p in Ps]
        per_ve = [p + 5 for p in Ps]
        PEQ = sum(per_pe)
        ACQ = sum(per_ac)
        VEQ = sum(per_ve)
        C_SCALE = float(cfg["SCALE"])
        C_MSHIFT = -float(cfg["SHIFT"])

        p3 = dict(hw0=cnt[id(s_hw)], hwB0=cnt[id(s_hwB)], pe0=cnt[id(s_pe)],
                  ac0=cnt[id(s_ac)], ve0=cnt[id(s_ve)], sw0=cnt[id(s_sw)])

        def p3_hsem(h):
            if h % 2 == 0:
                return s_hw, p3["hw0"] + (h // 2 + 1) * 48
            return s_hwB, p3["hwB0"] + (h // 2 + 1) * 48

        PTOT = sum(Ps)

        def pair_global(h, qg, pi):
            return h * PTOT + sum(Ps[:qg]) + pi

        def pair_from_global(idx):
            h, rem = divmod(idx, PTOT)
            qg = 0
            while rem >= Ps[qg]:
                rem -= Ps[qg]
                qg += 1
            return h, qg, rem

        def prev_gi(h, qg, back=1):
            gi = h * NQG + qg - back
            return divmod(gi, NQG) if gi >= 0 else None

        secs = [(h, qg) for h in range(HL) for qg in range(NQG)]
        pe_seq, ac_seq, ve_seq = [], [], []
        for i, (h, qg) in enumerate(secs):
            P = Ps[qg]
            prev = secs[i - 1] if i >= 1 else None
            pe_seq += [("sc", h, qg, 0), ("sc", h, qg, 1)]
            if prev:
                pe_seq.append(("sum",) + prev)
            for pi in range(P):
                pe_seq.append(("av", h, qg, pi))
                if pi == 0 and prev:
                    pe_seq.append(("bc",) + prev)
                if pi + 2 < P:
                    pe_seq.append(("sc", h, qg, pi + 2))
            ac_seq.append(("exp", h, qg, 0))
            if prev:
                ac_seq.append(("bce",) + prev)
            for pi in range(1, P):
                ac_seq.append(("exp", h, qg, pi))
            loc = []
            for pi in range(P - 2):
                loc.append(("fold", h, qg, pi))
            for pi in range(P - 2, P):
                loc += [("mask", h, qg, pi), ("fold", h, qg, pi)]
            loc.append(("cast", h, qg))
            ve_seq += [loc[0]]
            if prev:
                ve_seq += [("recip",) + prev, ("mul",) + prev]
            ve_seq += loc[1:]
        lasts = secs[-1]
        pe_seq += [("sum",) + lasts, ("bc",) + lasts]
        ac_seq.append(("bce",) + lasts)
        ve_seq += [("recip",) + lasts, ("mul",) + lasts]

        def _counts(seq, base):
            d = {}
            cc = base
            for op in seq:
                if op[0] == "av":
                    if op[3] == Ps[op[2]] - 1:
                        cc += 1
                        d[("av", op[1], op[2])] = cc
                else:
                    cc += 1
                    d[op] = cc
            return d

        PEC = _counts(pe_seq, p3["pe0"])
        ACC = _counts(ac_seq, p3["ac0"])
        VEC = _counts(ve_seq, p3["ve0"])

        def p3_pe_pair(h, qg, pi):
            return PEC[("sc", h, qg, pi)]

        def p3_pe_av(h, qg):
            return PEC[("av", h, qg)]

        def p3_pe_sum(h, qg):
            return PEC[("sum", h, qg)]

        def p3_pe_bc(h, qg):
            return PEC[("bc", h, qg)]

        def p3_ac_exp(h, qg, pi):
            return ACC[("exp", h, qg, pi)]

        def p3_ac_bce(h, qg):
            return ACC[("bce", h, qg)]

        def p3_ve_fold(h, qg, pi):
            return VEC[("fold", h, qg, pi)]

        def p3_ve_mask(h, qg, pi):
            return VEC[("mask", h, qg, pi)]

        def p3_ve_cast(h, qg):
            return VEC[("cast", h, qg)]

        def p3_ve_recip(h, qg):
            return VEC[("recip", h, qg)]

        def p3_ve_mul(h, qg):
            return VEC[("mul", h, qg)]

        def p3_q(h):
            return wkb[:, 0:S] if h % 2 == 0 else ar2[:, 0:S]

        def p3_k(h):
            return wkb2[:, 0:S] if h % 2 == 0 else ar2[:, S:2 * S]

        def p3_v(h):
            return vbuf[:, 0:S] if h % 2 == 0 else ar2[:, 2 * S:3 * S]

        def p3_sync(sync):
            def ld(h):
                if h == 0:
                    sync.wait_ge(s_st, p2_st_end)
                elif h >= 2:
                    sync.wait_ge(s_pe, p3["pe0"] + (h - 1) * PEQ)
                hs = s_hw if h % 2 == 0 else s_hwB
                inc(sync.dma_start(out=p3_q(h),
                                   in_=qT_h[h * 128:(h + 1) * 128, 0:S]), hs, 16)
                inc(sync.dma_start(out=p3_k(h),
                                   in_=kT_h[h * 128:(h + 1) * 128, 0:S]), hs, 16)
                inc(sync.dma_start(
                    out=p3_v(h).rearrange("p (kc i) -> p kc i", i=DV),
                    in_=v_h[0:S, h * DV:(h + 1) * DV]
                        .rearrange("(kc p) i -> p kc i", p=128)), hs, 16)
            ld(0)
            for h in range(1, HL):
                ld(h)
        block.sync(p3_sync)

        def p3_pe(t):
            curh = [-1]

            def headstart(h):
                if h != curh[0]:
                    t.wait_ge(*p3_hsem(h))
                    if h == 0:
                        t.wait_ge(s_ac, p2_ac_end)
                    curh[0] = h

            for op in pe_seq:
                kind, h, qg = op[0], op[1], op[2]
                gi = h * NQG + qg
                if kind == "sc":
                    headstart(h)
                    pi = op[3]
                    gidx = pair_global(h, qg, pi)
                    if gidx >= 2:
                        t.wait_ge(s_ac, p3_ac_exp(*pair_from_global(gidx - 2)))
                    sc = ps_sc0 if gidx % 2 == 0 else ps_sc1
                    for half in range(2):
                        kc = 2 * pi + half
                        mm = t.matmul(sc.ap()[:, half * 512:(half + 1) * 512],
                                      p3_k(h)[:, kc * 128:(kc + 1) * 128],
                                      p3_q(h)[:, qg * 512:(qg + 1) * 512],
                                      start=True, stop=True)
                    inc(mm, s_pe)
                elif kind == "av":
                    pi = op[3]
                    P = Ps[qg]
                    avp = ps[4] if gi % 2 == 0 else ps_av1[:, 0:512]
                    t.wait_ge(s_ac, p3_ac_exp(h, qg, pi))
                    if pi >= P - 2:
                        t.wait_ge(s_ve, p3_ve_mask(h, qg, pi))
                    if pi == 0:
                        pv2 = prev_gi(h, qg, 2)
                        if pv2 is not None:
                            t.wait_ge(s_ve, p3_ve_mul(*pv2))
                    gi2 = pair_global(h, qg, pi) % 2
                    for half in range(2):
                        kc = 2 * pi + half
                        pslice = pt2[:, gi2 * 1024 + half * 512:
                                     gi2 * 1024 + (half + 1) * 512]
                        last = (kc == KCs[qg] - 1)
                        mm = t.matmul(avp, p3_v(h)[:, kc * 128:(kc + 1) * 128],
                                      pslice, start=(kc == 0), stop=last)
                        if last:
                            inc(mm, s_pe)
                elif kind == "sum":
                    t.wait_ge(s_ve, p3_ve_cast(h, qg))
                    pv1 = prev_gi(h, qg, 1)
                    if pv1 is not None:
                        t.wait_ge(s_ve, p3_ve_recip(*pv1))
                    mm = t.matmul(ps_sumA[0:1, 0:512], onebc[:, :],
                                  faccB[:, (gi % 2) * 512:(gi % 2 + 1) * 512],
                                  start=True, stop=True)
                    inc(mm, s_pe)
                else:
                    t.wait_ge(s_ve, p3_ve_recip(h, qg))
                    pv1 = prev_gi(h, qg, 1)
                    if pv1 is not None:
                        t.wait_ge(s_ac, p3_ac_bce(*pv1))
                    mm = t.matmul(ps_bc[:, 0:512], onefr[:, :],
                                  st_m[0:1, 0:512], start=True, stop=True)
                    inc(mm, s_pe)
        block.tensor(p3_pe)

        def p3_ac(sc):
            for op in ac_seq:
                kind, h, qg = op[0], op[1], op[2]
                gi = h * NQG + qg
                if kind == "exp":
                    pi = op[3]
                    sc.wait_ge(s_pe, p3_pe_pair(h, qg, pi))
                    gidx = pair_global(h, qg, pi)
                    if gidx >= 2:
                        sc.wait_ge(s_ve, p3_ve_fold(*pair_from_global(gidx - 2)))
                    scp = ps_sc0 if gidx % 2 == 0 else ps_sc1
                    inc(sc.activation(
                        pt2[:, (gidx % 2) * 1024:(gidx % 2 + 1) * 1024],
                        scp.ap()[:, :], AF.Exp,
                        bias=C_MSHIFT, scale=C_SCALE), s_ac)
                else:
                    sc.wait_ge(s_pe, p3_pe_bc(h, qg))
                    pv1 = prev_gi(h, qg, 1)
                    if pv1 is not None:
                        sc.wait_ge(s_ve, p3_ve_mul(*pv1))
                    inc(sc.copy(bcv[:, (gi % 2) * 512:(gi % 2 + 1) * 512],
                                ps_bc[:, 0:512]), s_ac)
        block.scalar(p3_ac)

        def p3_ve(v):
            for op in ve_seq:
                kind, h, qg = op[0], op[1], op[2]
                gi = h * NQG + qg
                P = Ps[qg]
                if kind == "fold":
                    pi = op[3]
                    if pi < P - 2:
                        v.wait_ge(s_ac, p3_ac_exp(h, qg, pi))
                    else:
                        v.drain()
                    gi2 = pair_global(h, qg, pi) % 2
                    h0 = pt2[:, gi2 * 1024:gi2 * 1024 + 512]
                    h1 = pt2[:, gi2 * 1024 + 512:(gi2 + 1) * 1024]
                    if pi == 0:
                        inc(v.tensor_tensor(facc[:, 0:512], h0, h1, ALU.add),
                            s_ve)
                    else:
                        v.tensor_tensor(pfb[:, 0:512], h0, h1, ALU.add)
                        v.drain()
                        inc(v.tensor_tensor(facc[:, 0:512], facc[:, 0:512],
                                            pfb[:, 0:512], ALU.add), s_ve)
                elif kind == "mask":
                    pi = op[3]
                    v.wait_ge(s_ac, p3_ac_exp(h, qg, pi))
                    gi2 = pair_global(h, qg, pi) % 2
                    j0 = 2 * (pi - (P - 2))
                    inc(v.tensor_tensor(pt2[:, gi2 * 1024:(gi2 + 1) * 1024],
                                        pt2[:, gi2 * 1024:(gi2 + 1) * 1024],
                                        cmv[:, j0 * 512:(j0 + 2) * 512],
                                        ALU.mult), s_ve)
                elif kind == "cast":
                    v.drain()
                    pv2 = prev_gi(h, qg, 2)
                    if pv2 is not None:
                        v.wait_ge(s_pe, p3_pe_sum(*pv2))
                    inc(v.tensor_copy(faccB[:, (gi % 2) * 512:(gi % 2 + 1) * 512],
                                      facc[:, 0:512]), s_ve)
                elif kind == "recip":
                    v.wait_ge(s_pe, p3_pe_sum(h, qg))
                    inc(v.reciprocal(st_m[0:1, 0:512], ps_sumA[0:1, 0:512]),
                        s_ve)
                else:
                    avp = ps[4] if gi % 2 == 0 else ps_av1[:, 0:512]
                    v.wait_ge(s_pe, p3_pe_av(h, qg))
                    v.wait_ge(s_ac, p3_ac_bce(h, qg))
                    if gi >= 1:
                        v.wait_ge(s_sw, p3["sw0"] + gi * 16)
                    inc(v.tensor_tensor(attc[:, (qg % 2) * 512:(qg % 2 + 1) * 512],
                                        avp,
                                        bcv[:, (gi % 2) * 512:(gi % 2 + 1) * 512],
                                        ALU.mult), s_ve)
        block.vector(p3_ve)

        HLR = HL * 128

        def p3_gp(g):
            g.wait_ge(s_sw, p2_sw_end)
            g.wait_ge(s_hw, M["p0_const"][1])
            with g.register("pregh") as preg:
                g.reg_load(preg, prmv[0:1, 8:9])
                for h in range(HL):
                    for qg in range(NQG):
                        g.wait_ge(s_ve, p3_ve_mul(h, qg))
                        inc(g.dma_start(
                            out=attnT_loc[h * 128:(h + 1) * 128,
                                          qg * 512:(qg + 1) * 512],
                            in_=attc[:, (qg % 2) * 512:(qg % 2 + 1) * 512]), s_sw, 16)
                    g.wait_ge(s_sw, cnt[id(s_sw)])
                    with g.If_eq(preg, 0):
                        g.dma_start(
                            out=sh_attnT[h * 128:(h + 1) * 128, 0:S],
                            in_=attnT_loc[h * 128:(h + 1) * 128, 0:S]).then_inc(s_sw, 16)
                    with g.Else():
                        g.dma_start(
                            out=sh_attnT[HLR + h * 128:HLR + (h + 1) * 128, 0:S],
                            in_=attnT_loc[h * 128:(h + 1) * 128, 0:S]).then_inc(s_sw, 16)
                    cnt[id(s_sw)] += 16
        block.gpsimd(p3_gp)

        p3_pe_end = cnt[id(s_pe)]
        p3_sw_end = cnt[id(s_sw)]
        p3_hw_end = cnt[id(s_hw)]
        p3_ve_end = cnt[id(s_ve)]
        p3_ac_end = cnt[id(s_ac)]

        if PH < 3.5:
            meta = dict(cfg=cfg, c=c)
            _finish(block, None)
            return nc, meta
        def p35_gp(g):
            g.wait_ge(s_sw, p3_sw_end)
            with g.register("preg") as preg, g.register("qreg") as qreg:
                g.reg_load(preg, prmv[0:1, 8:9])
                g.reg_load(qreg, prmv[0:1, 1:2])
                with g.If_eq(preg, 0):
                    g.dma_start(out=sh_flags[0:1, 0:2],
                                in_=prmv[0:1, 4:6]).then_inc(s_sw, 16)
                with g.Else():
                    g.dma_start(out=sh_flags[1:2, 0:2],
                                in_=prmv[0:1, 4:6]).then_inc(s_sw, 16)
                cnt[id(s_sw)] += 16
                g.wait_ge(s_sw, cnt[id(s_sw)])
                pr0, pr1 = (0, 1) if RL else (1, 0)
                with g.register("cond") as cond, g.register("tgt") as tgt, \
                     g.register("fa") as fa, g.register("fb") as fb, \
                     g.register("ea") as ea, g.register("eb") as eb:
                    g.reg_load(ea, prmv[0:1, 4:5])
                    g.reg_load(eb, prmv[0:1, 5:6])
                    g.reg_mov(cond, 1)
                    g.reg_mov(tgt, 0)
                    with g.While(cond):
                        g.reg_add(tgt, tgt, 16)
                        with g.If_eq(preg, 0):
                            g.dma_start(out=flgrd[0:1, 0:2],
                                        in_=sh_flags[pr0:pr0 + 1, 0:2]
                                        ).then_inc(s_po1, 16)
                        with g.Else():
                            g.dma_start(out=flgrd[0:1, 0:2],
                                        in_=sh_flags[pr1:pr1 + 1, 0:2]
                                        ).then_inc(s_po1, 16)
                        g.wait_ge(s_po1, tgt)
                        g.reg_load(fa, flgrd[0:1, 0:1])
                        g.reg_load(fb, flgrd[0:1, 1:2])
                        g.reg_alu(fa, fa, ea, ALU.is_equal)
                        g.reg_alu(fb, fb, eb, ALU.is_equal)
                        g.reg_alu(cond, fa, fb, ALU.bitwise_and)
                        g.reg_alu(cond, cond, 1, ALU.bitwise_xor)
                def pull(which, k):
                    d0 = which * QS
                    if not RL:
                        g.dma_start(out=attnT_l2[:, d0:d0 + QS],
                                    in_=sh_attnT[0:c.HDV, k * QS:(k + 1) * QS]
                                    ).then_inc(s_sw, 16)
                        return
                    HDV2 = c.HDV // 2
                    with g.If_eq(preg, 0):
                        g.dma_start(out=attnT_l2[0:HDV2, d0:d0 + QS],
                                    in_=sh_attnT[0:HDV2, k * QS:(k + 1) * QS]
                                    ).then_inc(s_sw, 16)
                        g.dma_start(out=attnT_l2[HDV2:c.HDV, d0:d0 + QS],
                                    in_=sh_attnT[0:HDV2, k * QS:(k + 1) * QS]
                                    ).then_inc(s_sw, 16)
                    with g.Else():
                        g.dma_start(out=attnT_l2[0:HDV2, d0:d0 + QS],
                                    in_=sh_attnT[HDV2:c.HDV, k * QS:(k + 1) * QS]
                                    ).then_inc(s_sw, 16)
                        g.dma_start(out=attnT_l2[HDV2:c.HDV, d0:d0 + QS],
                                    in_=sh_attnT[HDV2:c.HDV, k * QS:(k + 1) * QS]
                                    ).then_inc(s_sw, 16)

                for which in range(2):
                    with g.If_eq(qreg, 0):
                        pull(which, (0 + 2 * which) % 4)
                    with g.Else():
                        with g.If_eq(qreg, 1):
                            pull(which, (1 + 2 * which) % 4)
                        with g.Else():
                            with g.If_eq(qreg, 2):
                                pull(which, (2 + 2 * which) % 4)
                            with g.Else():
                                pull(which, (3 + 2 * which) % 4)
                    cnt[id(s_sw)] += 32 if RL else 16
                    g.wait_ge(s_sw, cnt[id(s_sw)])
                    g.sem_inc(s_gp, 1)
                    cnt[id(s_gp)] += 1
        block.gpsimd(p35_gp)
        mark("pair1_ready", s_gp)
        M["pull1_done"] = (s_gp, cnt[id(s_gp)] - 1)

        OQN4 = 4
        OQ4 = O // OQN4
        NHC = c.HDV // 128
        NMT = c.NMT
        NB4 = OQ4 // 512 if OQ4 >= 512 else 1
        W4 = min(OQ4, 512)

        p4 = dict(hw0=cnt[id(s_hw)], hwB0=cnt[id(s_hwB)], sw0=cnt[id(s_sw)],
                  pe0=cnt[id(s_pe)], ve0=cnt[id(s_ve)], st0=cnt[id(s_st)])

        def p4_usem(ui, k):
            if ui % 2 == 0:
                return s_hw, p4["hw0"] + (ui // 2) * 32 + k * 16
            return s_hwB, p4["hwB0"] + (ui // 2) * 32 + k * 16

        def p4_gp(g):
            for oq in range(OQN4):
                if oq >= 1:
                    g.wait_ge(s_pe, p4["pe0"] + oq * NMT)
                inc(g.dma_start(
                    out=big[:, 0:NHC * OQ4].rearrange("p (hc o) -> p hc o", o=OQ4),
                    in_=wo_full[0:c.HDV, oq * OQ4:(oq + 1) * OQ4]
                        .rearrange("(hc p) o -> p hc o", p=128)), s_sw, 16)
        block.gpsimd(p4_gp)

        def p4_sync(sync):
            st_i = p4["st0"]
            for oq in range(OQN4):
                for tc in range(NMT):
                    ui = oq * NMT + tc
                    if oq == 0:
                        wait(sync, "pull1_done" if tc < NMT // 2 else "pair1_ready")
                    if ui >= 2:
                        sync.wait_ge(s_pe, p4["pe0"] + ui - 1)
                    us = s_hw if ui % 2 == 0 else s_hwB
                    inc(sync.dma_start(
                        out=attsb[:, (ui % 2) * NHC * 128:(ui % 2 + 1) * NHC * 128]
                            .rearrange("p (hc i) -> p hc i", i=128),
                        in_=attnT_l2[0:c.HDV, tc * 128:(tc + 1) * 128]
                            .rearrange("(hc p) i -> p hc i", p=128)), us, 16)
                    if ui >= 2:
                        sync.wait_ge(s_ve, p4["ve0"] + ui - 1)
                    inc(sync.dma_start(
                        out=xres[:, (ui % 2) * OQ4:(ui % 2 + 1) * OQ4],
                        in_=x_strips[tc * 128:(tc + 1) * 128,
                                     oq * OQ4:(oq + 1) * OQ4]), us, 16)
                    if ui >= 1:
                        po, ptc = divmod(ui - 1, NMT)
                        sync.wait_ge(s_ve, p4["ve0"] + ui)
                        inc(sync.dma_start(
                            out=xaft_h[ptc * 128:(ptc + 1) * 128,
                                       po * OQ4:(po + 1) * OQ4],
                            in_=evf[:, ((ui - 1) % 2) * OQ4:((ui - 1) % 2 + 1) * OQ4]),
                            s_st, 16)
            ui = OQN4 * NMT
            po, ptc = divmod(ui - 1, NMT)
            sync.wait_ge(s_ve, p4["ve0"] + ui)
            inc(sync.dma_start(
                out=xaft_h[ptc * 128:(ptc + 1) * 128, po * OQ4:(po + 1) * OQ4],
                in_=evf[:, ((ui - 1) % 2) * OQ4:((ui - 1) % 2 + 1) * OQ4]), s_st, 16)
        block.sync(p4_sync)

        def p4_pe(t):
            t.wait_ge(s_ac, p3_ac_end)
            t.wait_ge(s_ve, p3_ve_end)
            for oq in range(OQN4):
                t.wait_ge(s_sw, p4["sw0"] + (oq + 1) * 16)
                for tc in range(NMT):
                    ui = oq * NMT + tc
                    t.wait_ge(*p4_usem(ui, 2))
                    if ui >= 2:
                        t.wait_ge(s_ve, p4["ve0"] + ui - 1)
                    pp = ui % 2
                    for hc in range(NHC):
                        for nb in range(NB4):
                            w0 = nb * 512
                            w1 = min(OQ4, w0 + 512)
                            mm = t.matmul(
                                ps[2 * pp + nb][:, 0:w1 - w0],
                                attsb[:, (ui % 2) * NHC * 128 + hc * 128:
                                      (ui % 2) * NHC * 128 + (hc + 1) * 128],
                                big[:, hc * OQ4 + w0:hc * OQ4 + w1],
                                start=(hc == 0), stop=(hc == NHC - 1))
                    inc(mm, s_pe)
        block.tensor(p4_pe)

        def p4_ve(v):
            for oq in range(OQN4):
                for tc in range(NMT):
                    ui = oq * NMT + tc
                    v.wait_ge(s_pe, p4["pe0"] + ui + 1)
                    v.wait_ge(*p4_usem(ui, 2))
                    if ui >= 2:
                        v.wait_ge(s_st, p4["st0"] + ui * 16)
                    pp = ui % 2
                    for nb in range(NB4):
                        w0 = nb * 512
                        w1 = min(OQ4, w0 + 512)
                        v.tensor_tensor(evf[:, pp * OQ4 + w0:pp * OQ4 + w1],
                                        ps[2 * pp + nb][:, 0:w1 - w0],
                                        xres[:, pp * OQ4 + w0:pp * OQ4 + w1],
                                        ALU.add)
                    inc(v.drain(), s_ve)
                    ev = evf[:, pp * OQ4:pp * OQ4 + OQ4]
                    v.reduce_sum(l2p[:, 4 * tc + oq:4 * tc + oq + 1], ev, AX.X)
                    v.tensor_tensor(bcv[:, 0:OQ4], ev, ev, ALU.mult)
                    v.drain()
                    v.reduce_sum(l2p[:, 64 + 4 * tc + oq:64 + 4 * tc + oq + 1],
                                 bcv[:, 0:OQ4], AX.X)
        block.vector(p4_ve)

        p4_pe_end = cnt[id(s_pe)]
        p4_st_end = cnt[id(s_st)]
        p4_ve_end = cnt[id(s_ve)]
        p4_hw_end = cnt[id(s_hw)]
        p4_sw_end = cnt[id(s_sw)]

        if PH < 4.5:
            meta = dict(cfg=cfg, c=c)
            _finish(block, None)
            return nc, meta
        p45 = dict(pe0=cnt[id(s_pe)], ve0=cnt[id(s_ve)], ac0=cnt[id(s_ac)],
                   st20=cnt[id(s_st2)], st2B0=cnt[id(s_st2B)],
                   ac20=cnt[id(s_ac2)], ac2B0=cnt[id(s_ac2B)])
        arf2f = attsb.bitcast(F32)

        def p45_stw(tc):
            if tc % 2 == 0:
                return s_st2, p45["st20"] + (tc // 2 + 1) * 16
            return s_st2B, p45["st2B0"] + (tc // 2 + 1) * 32

        def p45_ld(tc):
            if tc % 2 == 0:
                return s_ac2, p45["ac20"] + (tc // 2 + 1) * 16
            return s_ac2B, p45["ac2B0"] + (tc // 2 + 1) * 16

        def p45_tr(tc, dc):
            return p45["pe0"] + tc * NDC + dc + 1

        def p45_xb(tc):
            return arf if tc % 2 == 0 else arf2f

        def p45_ob(tc):
            return wkb if tc % 2 == 0 else vbuf

        def p45_ev_tgt(tc, dc):
            if tc % 2 == 0:
                return wkb2[:, dc * 128:(dc + 1) * 128]
            if dc < NDC // 2:
                return pt2[:, dc * 128:(dc + 1) * 128]
            return evb[:, (dc - NDC // 2) * 128:(dc - NDC // 2 + 1) * 128]

        evA = {}
        evV = {}
        apA = {}

        def p45_ve(v):
            v.drain()
            for tc in range(NMT):
                v.reduce_sum(l2p[:, 224 + tc:225 + tc],
                             l2p[:, 4 * tc:4 * tc + 4], AX.X)
                v.reduce_sum(l2p[:, 208 + tc:209 + tc],
                             l2p[:, 64 + 4 * tc:64 + 4 * tc + 4], AX.X)
            v.drain()
            NM = NMT
            v.tensor_scalar_mul(l2p[:, 128:128 + NM], l2p[:, 224:224 + NM],
                                1.0 / D)
            v.tensor_scalar_mul(l2p[:, 240:240 + NM], l2p[:, 208:208 + NM],
                                1.0 / D)
            v.drain()
            v.tensor_tensor(l2p[:, 192:192 + NM], l2p[:, 128:128 + NM],
                            l2p[:, 128:128 + NM], ALU.mult)
            v.drain()
            v.tensor_tensor(l2p[:, 144:144 + NM], l2p[:, 240:240 + NM],
                            l2p[:, 192:192 + NM], ALU.subtract)
            inc(v.drain(), s_ve)
            v.wait_ge(s_ac, p45["ac0"] + 1)
            v.tensor_tensor(l2p[:, 176:176 + NM], l2p[:, 128:128 + NM],
                            l2p[:, 160:160 + NM], ALU.mult)
            v.drain()
            v.tensor_scalar_mul(l2p[:, 176:176 + NM], l2p[:, 176:176 + NM],
                                -1.0)
            inc(v.drain(), s_ve)
            for tc in range(NMT):
                for dc in range(1, NDC, 2):
                    v.wait_ge(s_pe, p45_tr(tc, dc))
                    if tc >= 2 and dc == 1:
                        v.wait_ge(*p45_stw(tc - 2))
                    op = v.tensor_copy(p45_ev_tgt(tc, dc), ps_tr2[:, 0:128])
                    inc(op, s_ve)
                    evV[(tc, dc)] = cnt[id(s_ve)]
        block.vector(p45_ve)

        def p45_emit_apply(sc, tc):
            sc.wait_ge(*p45_ld(tc))
            if tc < 2:
                sc.wait_ge(s_ve, p45["ve0"] + 2)
            else:
                sc.wait_ge(s_pe, p45_tr(tc - 2, NDC - 1))
            sc.activation(p45_ob(tc)[:, 0:D], p45_xb(tc)[:, 0:D], AF.Identity,
                          bias=l2p[:, 176 + tc:177 + tc],
                          scale=l2p[:, 160 + tc:161 + tc])
            inc(sc.drain(), s_ac)
            apA[tc] = cnt[id(s_ac)]
            if tc + 2 < NMT:
                ldsem = p45_ld(tc + 2)[0]
                inc(sc.dma_start(out=p45_xb(tc + 2)[:, 0:D],
                                 in_=xaft_h[(tc + 2) * 128:(tc + 3) * 128, 0:D]),
                    ldsem, 16)

        def p45_ac(sc):
            sc.wait_ge(s_st, p4_st_end)
            inc(sc.dma_start(out=arf[:, 0:D], in_=xaft_h[0:128, 0:D]),
                s_ac2, 16)
            if NMT > 1:
                sc.wait_ge(s_pe, p4_pe_end)
                inc(sc.dma_start(out=arf2f[:, 0:D], in_=xaft_h[128:256, 0:D]),
                    s_ac2B, 16)
            sc.wait_ge(s_ve, p45["ve0"] + 1)
            sc.activation(l2p[:, 192:192 + NMT], l2p[:, 144:144 + NMT], AF.Ln,
                          bias=float(cfg["EPS"]), scale=1.0)
            sc.drain()
            sc.activation(l2p[:, 160:160 + NMT], l2p[:, 192:192 + NMT], AF.Exp,
                          bias=0.0, scale=-0.5)
            inc(sc.drain(), s_ac)
            p45_emit_apply(sc, 0)
            if NMT > 1:
                p45_emit_apply(sc, 1)
            for tc in range(NMT):
                for dc in range(0, NDC, 2):
                    sc.wait_ge(s_pe, p45_tr(tc, dc))
                    if tc >= 2 and dc == 0:
                        sc.wait_ge(*p45_stw(tc - 2))
                    sc.copy(p45_ev_tgt(tc, dc), ps_tr[:, 0:128])
                    inc(sc.drain(), s_ac)
                    evA[(tc, dc)] = cnt[id(s_ac)]
                if tc + 2 < NMT:
                    p45_emit_apply(sc, tc + 2)
        block.scalar(p45_ac)

        def p45_pe(t):
            for tc in range(NMT):
                t.wait_ge(s_ac, apA[tc])
                src = p45_ob(tc)
                for dc in range(NDC):
                    idx = tc * NDC + dc
                    if idx >= 2:
                        ptc, pdc = divmod(idx - 2, NDC)
                        if pdc % 2 == 0:
                            t.wait_ge(s_ac, evA[(ptc, pdc)])
                        else:
                            t.wait_ge(s_ve, evV[(ptc, pdc)])
                    pt = ps_tr if dc % 2 == 0 else ps_tr2
                    tr = t.transpose(pt[:, 0:128],
                                     src[:, dc * 128:(dc + 1) * 128], idn[:, :])
                    inc(tr, s_pe)
        block.tensor(p45_pe)

        def p45_sync(sync):
            for tc in range(NMT):
                sync.wait_ge(s_ac, evA[(tc, NDC - 2)])
                sync.wait_ge(s_ve, evV[(tc, NDC - 1)])
                if tc % 2 == 0:
                    inc(sync.dma_start(
                        out=ln2T_h[0:D, tc * 128:(tc + 1) * 128]
                            .rearrange("(dc p) t -> p dc t", p=128),
                        in_=wkb2[:, 0:NDC * 128]
                            .rearrange("p (dc t) -> p dc t", t=128)), s_st2, 16)
                else:
                    ND2 = NDC // 2
                    inc(sync.dma_start(
                        out=ln2T_h[0:ND2 * 128, tc * 128:(tc + 1) * 128]
                            .rearrange("(dc p) t -> p dc t", p=128),
                        in_=pt2[:, 0:ND2 * 128]
                            .rearrange("p (dc t) -> p dc t", t=128)), s_st2B, 16)
                    inc(sync.dma_start(
                        out=ln2T_h[ND2 * 128:D, tc * 128:(tc + 1) * 128]
                            .rearrange("(dc p) t -> p dc t", p=128),
                        in_=evb[:, 0:ND2 * 128]
                            .rearrange("p (dc t) -> p dc t", t=128)), s_st2B, 16)
        block.sync(p45_sync)

        p45_pe_end = cnt[id(s_pe)]
        p45_st2_end = cnt[id(s_st2)]
        p45_st2B_end = cnt[id(s_st2B)]
        p45_ac_end = cnt[id(s_ac)]
        p45_ve_end = cnt[id(s_ve)]

        def p45_st_all(eng):
            eng.wait_ge(s_st2, p45_st2_end)
            if p45_st2B_end > 0:
                eng.wait_ge(s_st2B, p45_st2B_end)

        OQ6 = c.OQ
        NB6 = max(1, OQ6 // 512)
        NFT = c.NFT
        arfB = arf.bitcast(BF16)
        wcaps = [(wkb2, S // OQ6), (arfB, (2 * NDC * c.P1GW) // OQ6),
                 (pt2, 2048 // OQ6), (evb, 2048 // OQ6)]
        wfrags = []
        _d0 = 0
        for _buf, _cap in wcaps:
            _nd = min(_cap, NDC - _d0)
            if _nd > 0:
                wfrags.append((_buf, _d0, _nd))
                _d0 += _nd
        assert _d0 == NDC, (_d0, NDC)
        m_w = {}
        gm = {}

        def emit_w(g, oq):
            ws = s_wv0 if oq % 2 == 0 else s_wv1
            if oq % 2 == 0:
                inc(g.dma_start(
                    out=ar2[:, 0:NDC * OQ6].rearrange("p (dc o) -> p dc o",
                                                      o=OQ6),
                    in_=WTt[0:D, oq * OQ6:(oq + 1) * OQ6]
                        .rearrange("(dc p) o -> p dc o", p=128)), ws, 16)
            else:
                for buf, d0, nd in wfrags:
                    inc(g.dma_start(
                        out=buf[:, 0:nd * OQ6].rearrange("p (dc o) -> p dc o",
                                                         o=OQ6),
                        in_=WTt[d0 * 128:(d0 + nd) * 128,
                                oq * OQ6:(oq + 1) * OQ6]
                            .rearrange("(dc p) o -> p dc o", p=128)), ws, 16)
            m_w[oq] = (ws, cnt[id(ws)])

        def w_rhs(oq, dc, w0, w1):
            if oq % 2 == 0:
                return ar2[:, dc * OQ6 + w0:dc * OQ6 + w1]
            for buf, d0, nd in wfrags:
                if d0 <= dc < d0 + nd:
                    return buf[:, (dc - d0) * OQ6 + w0:(dc - d0) * OQ6 + w1]
            raise AssertionError(dc)

        def p45_gp(g):
            g.wait_ge(s_pe, p3_pe_end)
            emit_w(g, 0)
            g.wait_ge(s_pe, p4_pe_end)
            h6 = NMT // 2
            g.wait_ge(s_st2, p45["st20"] + ((h6 + 1) // 2) * 16)
            if h6 // 2 > 0:
                g.wait_ge(s_st2B, p45["st2B0"] + (h6 // 2) * 32)
            inc(g.dma_start(
                out=big[:, 0:NDC * QS].rearrange("p (dc t) -> p dc t", t=QS),
                in_=ln2T_h[0:D, 0:QS].rearrange("(dc p) t -> p dc t", p=128)),
                s_g1, 16)
            gm["ln2f"] = cnt[id(s_g1)]
            g.wait_ge(s_pe, p45_pe_end)
            inc(g.dma_start(out=atv[:, 0:NDC * ER]
                            .rearrange("p (dc e) -> p dc e", e=ER),
                            in_=ATt.rearrange("(dc p) e -> p dc e", p=128)),
                s_g2, 16)
            gm["atv"] = cnt[id(s_g2)]
            g.wait_ge(s_ac, p45_ac_end)
            inc(g.dma_start(out=attsb[0:1, 0:O], in_=bTt[:, :]), s_g3, 16)
            inc(g.dma_start(out=attsb[:, O:2 * O], in_=BTt[:, :]), s_g3, 16)
            gm["misc"] = cnt[id(s_g3)]
        block.gpsimd(p45_gp)

        ttm = []

        def p6t_pe(t):
            t.wait_ge(s_g1, gm["ln2f"])
            t.wait_ge(s_g2, gm["atv"])
            t.wait_ge(s_ve, p4_ve_end)
            for tc in range(NFT):
                psd = ps_sc0.ap() if tc < 4 else ps_sc1.ap()
                col = (tc % 4) * 128
                for dc in range(NDC):
                    mm = t.matmul(
                        psd[:, col:col + ER],
                        big[:, dc * QS + tc * 128:dc * QS + (tc + 1) * 128],
                        atv[:, dc * ER:(dc + 1) * ER],
                        start=(dc == 0), stop=(dc == NDC - 1))
                inc(mm, s_pe)
                ttm.append(cnt[id(s_pe)])
        block.tensor(p6t_pe)

        if PH < 5:
            meta = dict(cfg=cfg, c=c)
            _finish(block, None)
            return nc, meta
        p5 = dict(hw0=cnt[id(s_hw)], hwB0=cnt[id(s_hwB)], sw0=cnt[id(s_sw)],
                  pe0=cnt[id(s_pe)], ve0=cnt[id(s_ve)], ac0=cnt[id(s_ac)],
                  gp0=cnt[id(s_gp)])

        def p5_dsem(dc):
            if dc % 2 == 0:
                return s_hw, p5["hw0"] + (dc // 2 + 1) * 16
            return s_hwB, p5["hwB0"] + (dc // 2 + 1) * 16

        def p5_sync(sync):
            p45_st_all(sync)
            for dc in range(NDC):
                if dc >= 2:
                    sync.wait_ge(s_ve, p5["ve0"] + dc - 1)
                dst = pt2 if dc % 2 == 0 else evb
                inc(sync.dma_start(
                    out=dst[:, 0:c.MYTOK],
                    in_=ln2T_h[dc * 128:(dc + 1) * 128, 0:c.MYTOK]),
                    p5_dsem(dc)[0], 16)
        block.sync(p5_sync)

        def p5_ve(v):
            v.wait_ge(s_sw, p5["sw0"] + 16)
            for dc in range(NDC):
                v.wait_ge(*p5_dsem(dc))
                src = pt2 if dc % 2 == 0 else evb
                v.tensor_tensor(arf[:, 0:c.MYTOK], src[:, 0:c.MYTOK],
                                wmv[:, 0:c.MYTOK], ALU.mult)
                v.drain()
                v.reduce_sum(qpart[:, dc:dc + 1], arf[:, 0:c.MYTOK], AX.X)
                inc(v.drain(), s_ve)
        block.vector(p5_ve)

        def p5_gp(g):
            p45_st_all(g)
            inc(g.dma_start(out=wmv[:, 0:c.MYTOK], in_=wmaskt[:, :]), s_sw, 16)
            g.wait_ge(s_ve, p5["ve0"] + NDC)
            with g.register("pr5") as preg:
                g.reg_load(preg, prmv[0:1, 8:9])
                with g.If_eq(preg, 0):
                    g.dma_start(out=sh_qpart[0:128, 0:NDC],
                                in_=qpart[:, 0:NDC]).then_inc(s_sw, 16)
                with g.Else():
                    g.dma_start(out=sh_qpart[128:256, 0:NDC],
                                in_=qpart[:, 0:NDC]).then_inc(s_sw, 16)
                cnt[id(s_sw)] += 16
                g.wait_ge(s_sw, cnt[id(s_sw)])
                with g.If_eq(preg, 0):
                    g.dma_start(out=sh_flags[0:1, 2:4],
                                in_=prmv[0:1, 6:8]).then_inc(s_sw, 16)
                with g.Else():
                    g.dma_start(out=sh_flags[1:2, 2:4],
                                in_=prmv[0:1, 6:8]).then_inc(s_sw, 16)
                cnt[id(s_sw)] += 16
                g.wait_ge(s_sw, cnt[id(s_sw)])
                with g.register("cond2") as cond, g.register("tgt2") as tgt, \
                     g.register("fa2") as fa, g.register("fb2") as fb, \
                     g.register("ea2") as ea, g.register("eb2") as eb:
                    g.reg_load(ea, prmv[0:1, 6:7])
                    g.reg_load(eb, prmv[0:1, 7:8])
                    g.reg_mov(cond, 1)
                    g.reg_mov(tgt, 0)
                    pr0, pr1 = (0, 1) if RL else (1, 0)
                    with g.While(cond):
                        g.reg_add(tgt, tgt, 16)
                        with g.If_eq(preg, 0):
                            g.dma_start(out=flgrd[0:1, 2:4],
                                        in_=sh_flags[pr0:pr0 + 1, 2:4]
                                        ).then_inc(s_po2, 16)
                        with g.Else():
                            g.dma_start(out=flgrd[0:1, 2:4],
                                        in_=sh_flags[pr1:pr1 + 1, 2:4]
                                        ).then_inc(s_po2, 16)
                        g.wait_ge(s_po2, tgt)
                        g.reg_load(fa, flgrd[0:1, 2:3])
                        g.reg_load(fb, flgrd[0:1, 3:4])
                        g.reg_alu(fa, fa, ea, ALU.is_equal)
                        g.reg_alu(fb, fb, eb, ALU.is_equal)
                        g.reg_alu(cond, fa, fb, ALU.bitwise_and)
                        g.reg_alu(cond, cond, 1, ALU.bitwise_xor)
                qr0, qr1 = (0, 128) if RL else (128, 0)
                with g.If_eq(preg, 0):
                    g.dma_start(out=qpeer[:, 0:NDC],
                                in_=sh_qpart[qr0:qr0 + 128, 0:NDC]
                                ).then_inc(s_sw, 16)
                with g.Else():
                    g.dma_start(out=qpeer[:, 0:NDC],
                                in_=sh_qpart[qr1:qr1 + 128, 0:NDC]
                                ).then_inc(s_sw, 16)
                cnt[id(s_sw)] += 16
                g.wait_ge(s_sw, cnt[id(s_sw)])
            g.sem_inc(s_gp, 1)
            cnt[id(s_gp)] += 1
        block.gpsimd(p5_gp)
        mark("pair2_ready", s_gp)

        def p5_ve2(v):
            wait(v, "pair2_ready")
            v.tensor_tensor(qpart[:, 0:NDC], qpart[:, 0:NDC], qpeer[:, 0:NDC],
                            ALU.add)
            v.drain()
            for e in range(E):
                v.tensor_tensor(wk1[:, 0:NDC], qpart[:, 0:NDC],
                                rwv[:, e * NDC:(e + 1) * NDC], ALU.mult)
                v.drain()
                v.reduce_sum(wk1[:, NDC + e:NDC + e + 1], wk1[:, 0:NDC], AX.X)
                v.drain()
            inc(v.engine_nop(), s_ve)
        block.vector(p5_ve2)
        mark("p5_dots", s_ve)

        def p5_pe(t):
            wait(t, "p5_dots")
            t.wait_ge(s_ac, p45_ac_end)
            mm = t.matmul(ps_sumA[0:1, 0:E], onefc[:, 0:1],
                          wk1[:, NDC:NDC + E], start=True, stop=True)
            inc(mm, s_pe)
        block.tensor(p5_pe)
        mark("p5_logits_mm", s_pe)

        def p5_ve3(v):
            wait(v, "p5_logits_mm")
            v.tensor_scalar_mul(r8[0:1, 0:E], ps_sumA[0:1, 0:E], 1.0 / c.WIN_N)
            v.drain()
            v.tensor_tensor(r8[0:1, 0:E], r8[0:1, 0:E], rbv[0:1, 0:E], ALU.add)
            v.drain()
            v.reduce_max(r8[0:1, E:E + 1], r8[0:1, 0:E], AX.X)
            v.drain()
            v.tensor_tensor(r8[0:1, 8:8 + E], r8[0:1, 0:E],
                            r8[0:1, E:E + 1].broadcast_to([1, E]), ALU.subtract)
            inc(v.drain(), s_ve)
        block.vector(p5_ve3)
        mark("p5_lg", s_ve)

        def p5_ac(sc):
            wait(sc, "p5_lg")
            sc.activation(r8[0:1, 16:16 + E], r8[0:1, 8:8 + E], AF.Exp,
                          bias=0.0, scale=1.0)
            inc(sc.drain(), s_ac)
        block.scalar(p5_ac)
        mark("p5_exp", s_ac)

        def p5_ve4(v):
            wait(v, "p5_exp")
            v.reduce_sum(r8[0:1, 24:25], r8[0:1, 16:16 + E], AX.X)
            v.drain()
            v.reciprocal(r8[0:1, 25:26], r8[0:1, 24:25])
            v.drain()
            v.tensor_tensor(r8[0:1, 32:32 + E], r8[0:1, 16:16 + E],
                            r8[0:1, 25:26].broadcast_to([1, E]), ALU.mult)
            v.drain()
            v.tensor_copy(r8b[0:1, 0:E], r8[0:1, 32:32 + E])
            inc(v.drain(), s_ve)
        block.vector(p5_ve4)
        mark("p5_routing", s_ve)

        def p5_pe2(t):
            wait(t, "p5_routing")
            t.wait_ge(s_ac, p45_ac_end)
            tr = t.transpose(ps_tr[0:E, 0:1], r8b[0:1, 0:E], idn[0:1, 0:1])
            inc(tr, s_pe)
        block.tensor(p5_pe2)
        mark("p5_rT", s_pe)

        def p5_ac2(sc):
            wait(sc, "p5_rT")
            sc.copy(r8b[0:E, 1:2], ps_tr[0:E, 0:1])
            inc(sc.drain(), s_ac)
        block.scalar(p5_ac2)
        mark("p5_rTs", s_ac)

        def p5_pe3(t):
            wait(t, "p5_rTs")
            mm = t.matmul(ps[4][:, 0:1], repv[0:E, 0:ER], r8b[0:E, 1:2],
                          start=True, stop=True)
            inc(mm, s_pe)
        block.tensor(p5_pe3)
        mark("p5_rexp_mm", s_pe)

        def p5_ac3(sc):
            wait(sc, "p5_rexp_mm")
            sc.copy(rexp[:, 0:1], ps[4][:, 0:1])
            inc(sc.drain(), s_ac)
        block.scalar(p5_ac3)
        mark("p5_rexp", s_ac)

        if PH < 6:
            meta = dict(cfg=cfg, c=c)
            _finish(block, None)
            return nc, meta
        assert OQ6 <= 512
        p6 = dict(hw0=cnt[id(s_hw)], hwB0=cnt[id(s_hwB)], sw0=cnt[id(s_sw)],
                  pe0=cnt[id(s_pe)], ve0=cnt[id(s_ve)], ac0=cnt[id(s_ac)],
                  st0=cnt[id(s_st)])

        def p6_usem(ui):
            if ui % 2 == 0:
                return s_hw, p6["hw0"] + (ui // 2 + 1) * 16
            return s_hwB, p6["hwB0"] + (ui // 2 + 1) * 16

        def p6_sync(sync):
            sync.wait_ge(s_ve, p4_ve_end)
            sync.wait_ge(s_ve, p5["ve0"] + NDC)
            for oq in range(c.OQN):
                for tc in range(NFT):
                    ui = oq * NFT + tc
                    if ui >= 2:
                        sync.wait_ge(s_ve, p6["ve0"] + ui - 1)
                    inc(sync.dma_start(
                        out=xres[:, (ui % 2) * OQ6:(ui % 2) * OQ6 + OQ6],
                        in_=xaft_h[tc * 128:(tc + 1) * 128,
                                   oq * OQ6:(oq + 1) * OQ6]),
                        s_hw if ui % 2 == 0 else s_hwB, 16)
                    if ui >= 1:
                        po, ptc = divmod(ui - 1, NFT)
                        sync.wait_ge(s_ve, p6["ve0"] + ui)
                        inc(sync.dma_start(
                            out=out_t[ptc * 128:(ptc + 1) * 128,
                                      po * OQ6:(po + 1) * OQ6],
                            in_=evf[:, ((ui - 1) % 2) * OQ6:((ui - 1) % 2) * OQ6 + OQ6]),
                            s_st, 16)
            ui = c.OQN * NFT
            po, ptc = divmod(ui - 1, NFT)
            sync.wait_ge(s_ve, p6["ve0"] + ui)
            inc(sync.dma_start(
                out=out_t[ptc * 128:(ptc + 1) * 128, po * OQ6:(po + 1) * OQ6],
                in_=evf[:, ((ui - 1) % 2) * OQ6:((ui - 1) % 2) * OQ6 + OQ6]), s_st, 16)
            sync.wait_ge(s_st, cnt[id(s_st)])
        block.sync(p6_sync)

        def p6_gp(g):
            for oq in range(1, c.OQN):
                if oq == 1:
                    g.wait_ge(s_ve, p5["ve0"] + NDC)
                else:
                    g.wait_ge(s_pe, p6["pe0"] + NFT + (oq - 1) * NFT)
                emit_w(g, oq)
        block.gpsimd(p6_gp)

        def p6_pe(t):
            for tc in range(NFT):
                t.wait_ge(s_ac, p6["ac0"] + 2 * tc + 1)
                tr = t.transpose(ps_tr[:, 0:128],
                                 wkb[:, tc * 128:(tc + 1) * 128], idn[:, :])
                inc(tr, s_pe)
            for oq in range(c.OQN):
                t.wait_ge(*m_w[oq])
                if oq == 0:
                    t.wait_ge(s_g3, gm["misc"])
                for tc in range(NFT):
                    ui = oq * NFT + tc
                    if ui < 4:
                        t.wait_ge(s_ac, p6["ac0"] + 2 * NFT - 1)
                    if ui >= 4:
                        t.wait_ge(s_ve, p6["ve0"] + ui - 3)
                    pb = ps[ui % 4]
                    t.wait_ge(s_ac, p6["ac0"] + 2 * tc + 2)
                    t.matmul(pb[:, 0:OQ6], onebr[0:1, :],
                             attsb[0:1, oq * OQ6:(oq + 1) * OQ6],
                             start=True, stop=False)
                    for dc in range(NDC):
                        t.matmul(pb[:, 0:OQ6],
                                 big[:, dc * QS + tc * 128:dc * QS + (tc + 1) * 128],
                                 w_rhs(oq, dc, 0, OQ6),
                                 start=False, stop=False)
                    mm = t.matmul(pb[:, 0:OQ6],
                                  tts[:, tc * 128:(tc + 1) * 128],
                                  attsb[:, O + oq * OQ6:O + (oq + 1) * OQ6],
                                  start=False, stop=True)
                    inc(mm, s_pe)
        block.tensor(p6_pe)

        def p6_ac(sc):
            sc.wait_ge(s_pe, p45_pe_end)
            for tc in range(NFT):
                psd = ps_sc0.ap() if tc < 4 else ps_sc1.ap()
                col = (tc % 4) * 128
                sc.wait_ge(s_pe, ttm[tc])
                sc.copy(wkb[:, tc * 128:(tc + 1) * 128], psd[:, col:col + ER])
                inc(sc.drain(), s_ac)
                sc.wait_ge(s_pe, p6["pe0"] + tc + 1)
                sc.activation(tts[:, tc * 128:(tc + 1) * 128], ps_tr[:, 0:128],
                              AF.Identity, bias=0.0, scale=rexp[:, 0:1])
                inc(sc.drain(), s_ac)
        block.scalar(p6_ac)

        def p6_ve(v):
            for oq in range(c.OQN):
                for tc in range(NFT):
                    ui = oq * NFT + tc
                    v.wait_ge(s_pe, p6["pe0"] + NFT + ui + 1)
                    v.wait_ge(*p6_usem(ui))
                    if ui >= 2:
                        v.wait_ge(s_st, p6["st0"] + ui * 16)
                    pp = ui % 2
                    v.tensor_tensor(evf[:, pp * OQ6:pp * OQ6 + OQ6],
                                    ps[ui % 4][:, 0:OQ6],
                                    xres[:, pp * OQ6:pp * OQ6 + OQ6],
                                    ALU.add)
                    inc(v.drain(), s_ve)
        block.vector(p6_ve)

        meta = dict(cfg=cfg, c=c)
    return nc, meta



def host_slices(cfg, inputs, core):
    c = C(cfg)
    b, p, q = core // 4, core % 2, core % 4
    s_loc = q // 2
    x = np.ascontiguousarray(inputs["L0_attn_hidden_states"][b])
    wq = inputs["L0_attn_wq"]
    wkv = inputs["L0_attn_wkv"]
    wo = inputs["L0_attn_wo"]
    W = inputs["L0_ffn_W"]
    bias = inputs["L0_ffn_b"]
    A = inputs["L0_ffn_A"]
    B = inputs["L0_ffn_B"]
    rW = inputs["L0_ffn_router_W"]
    rb = inputs["L0_ffn_router_b"]
    S, D, HL, DN, DV, E, R, O = c.S, c.D, c.HL, c.DN, c.DV, c.E, c.R, c.O
    NH = 2 * HL
    DQG = wq.shape[1] // NH
    myheads = list(range(p * HL, (p + 1) * HL))
    own = [p * c.QS, 2 * c.QS + p * c.QS]
    ffn0 = q * c.QS
    strips = [ffn0, own[0] if own[1] == ffn0 else own[1]]

    xT = np.ascontiguousarray(x.T)
    x_strips = np.ascontiguousarray(
        np.concatenate([x[s0:s0 + c.QS] for s0 in strips], axis=0))
    wq_l = np.ascontiguousarray(
        np.concatenate([wq[:, h * DQG:h * DQG + DN] for h in myheads], axis=1))
    wk_l = np.ascontiguousarray(
        np.concatenate([wkv[:, h * DN:(h + 1) * DN] for h in myheads], axis=1))
    vbase = NH * DN
    wv_l = np.ascontiguousarray(
        np.concatenate([wkv[:, vbase + h * DV:vbase + (h + 1) * DV] for h in myheads], axis=1))
    wo_full_a = np.ascontiguousarray(wo)
    WT = np.ascontiguousarray(W.T)
    bT = np.ascontiguousarray(bias[None, :])
    AT = np.ascontiguousarray(A.transpose(2, 0, 1).reshape(D, E * R))
    BT = np.ascontiguousarray(B.transpose(0, 2, 1).reshape(E * R, O))
    rwH = np.ascontiguousarray(
        rW.reshape(E, c.NDC, 128).transpose(2, 0, 1).reshape(128, E * c.NDC))
    rbT = np.ascontiguousarray(rb[None, :])
    gtok = np.concatenate([np.arange(s0, s0 + c.QS) for s0 in strips])
    wm = ((gtok >= c.W0) & (gtok < c.W1)).astype(np.float32)[None, :]
    wmask = np.ascontiguousarray(np.tile(wm, (128, 1)))
    salt = np.random.randint(1, 2**31 - 1, size=4, dtype=np.int32)
    prm = np.array([[p,
                     strips[0] // c.QS, strips[1] // c.QS,
                     0,
                     salt[0], salt[1], salt[2], salt[3],
                     p, 1 - p, 0, 0]], dtype=np.int32)
    return dict(xT=xT, x_strips=x_strips, wq_l=wq_l, wk_l=wk_l, wv_l=wv_l,
                wo_full=wo_full_a, WTt=WT, bTt=bT, ATt=AT, BTt=BT,
                rwHt=rwH, rbTt=rbT, wmaskt=wmask, prm=prm)


_SALT_SHARED = None


def kernel(**inputs):
    cfg = full_cfg()
    nc, meta = build_program(cfg)
    in_maps = []
    for corex in range(8):
        in_maps.append(host_slices(cfg, inputs, corex))
    for k in range(4):
        in_maps[2 * k + 1]["prm"][0, 4:8] = in_maps[2 * k]["prm"][0, 4:8]
    res = run_bass_kernel_spmd(nc, in_maps, core_ids=list(range(8)))
    c = C(cfg)
    out = np.zeros((2, cfg["S"], cfg["O"]), dtype=np.float32)
    for corex in range(8):
        b, q = corex // 4, corex % 4
        out[b, q * c.QS:(q + 1) * c.QS, :] = np.asarray(res.results[corex]["out"])
    return out
